# version 03
# baseline (speedup 1.0000x reference)
import math
from contextlib import ExitStack

import numpy as np
import ml_dtypes
import concourse.bass as bass
import concourse.mybir as mybir
from concourse.bass_utils import run_bass_kernel_spmd

F32 = mybir.dt.float32
BF16 = mybir.dt.bfloat16
AF = mybir.ActivationFunctionType
ALU = mybir.AluOpType
AX = mybir.AxisListType
NPBF = ml_dtypes.bfloat16

NSEQ, S, HID = 4, 4096, 2048
NH, D = 16, 128
DFF = 8192
NCF = HID // 128
HALF = S // 2
NT_OWN = HALF // 128
NT_ALL = S // 128
NHL = NH // 2
SCALE_A = 1.0 / math.sqrt(128.0)
SCALE_B = 128.0 ** -0.5
RMS_EPS_B = 1e-6
FFN_EPS = float(np.finfo(np.float32).eps)
LN_EPS = 1e-5

ENGS = ("pe", "dve", "act", "pool", "sp")


class Sem:
    __slots__ = ("h", "name", "count")

    def __init__(self, handle, name):
        self.h = handle
        self.name = name
        self.count = 0


class Emitter:

    def __init__(self, chain_sems):
        self.streams = {e: [] for e in ENGS}
        self.chain_sems = chain_sems
        self._last = {e: None for e in ENGS}

    def emit(self, eng, fn=None, waits=(), inc=None, inc_amt=1):
        waits = tuple(waits)
        chain = fn is not None and eng in ("dve", "act") and inc_amt == 1
        if chain:
            if self._last[eng] is not None:
                waits = waits + (self._last[eng],)
            if inc is None:
                inc = self.chain_sems[eng]
        self.streams[eng].append((fn, waits, inc, inc_amt))
        if inc is not None:
            inc.count += inc_amt
            if chain:
                self._last[eng] = (inc, inc.count)

    def last(self, eng):
        return self._last[eng]

    def replay(self, eng_name, engobj):
        for fn, waits, inc, inc_amt in self.streams[eng_name]:
            for s, v in waits:
                engobj.wait_ge(s.h, v)
            ins = fn(engobj) if fn is not None else None
            if inc is not None:
                if ins is None:
                    ins = engobj.nop()
                ins.then_inc(inc.h, inc_amt)


class Ring:

    def __init__(self, n):
        self.n = n
        self.fw = []
        self.rw = []

    @property
    def slot(self):
        return len(self.fw) % self.n

    def prod_waits(self):
        i = len(self.fw)
        if i >= self.n:
            return tuple(self.rw[i - self.n])
        return ()

    def filled(self, waits):
        self.fw.append(tuple(waits))

    def cons_waits(self):
        return tuple(self.fw[len(self.rw)])

    def released(self, waits):
        self.rw.append(tuple(waits))


class Bank:
    def __init__(self, tensor, sem):
        self.t = tensor
        self.sem = sem

    def waits(self):
        return ((self.sem, self.sem.count),)


def attn_tiles(lay, g):
    out = []
    if lay == "a":
        for kt in range(4 * g):
            out.append((kt, 0, 512, None))
    else:
        for op_ in range(4):
            kt = 4 * g - 8 + op_
            if kt >= 0:
                out.append((kt, 0, (op_ + 1) * 128, ("w", op_ * 128)))
        for kt in range(max(0, 4 * g - 4), 4 * g):
            out.append((kt, 0, 512, None))
    for o in range(4):
        out.append((4 * g + o, o * 128, 512, ("c", o * 128)))
    return out


def host_constants():
    j = np.arange(0, D, 2, dtype=np.float32) / np.float32(D)
    inv = 1.0 / (10000.0 ** j.astype(np.float64))
    t32 = np.arange(S, dtype=np.float32)
    ang = (t32[:, None] * inv.astype(np.float32)[None, :]).astype(np.float32)
    cos = np.cos(ang.astype(np.float64)).astype(np.float32)
    sin = np.sin(ang.astype(np.float64)).astype(np.float32)
    ropec = np.ascontiguousarray(
        cos.reshape(NT_ALL, 128, 64).transpose(1, 0, 2).reshape(128, NT_ALL * 64))
    ropes = np.ascontiguousarray(
        sin.reshape(NT_ALL, 128, 64).transpose(1, 0, 2).reshape(128, NT_ALL * 64))
    kk = np.arange(128)[:, None]
    qq = np.arange(128)[None, :]
    maskc = np.where(qq < kk, np.float32(-1e30), np.float32(0.0))
    maskw = np.where(kk <= qq, np.float32(-1e30), np.float32(0.0))
    ident = np.eye(128, dtype=np.float32)
    return {
        "ropec": ropec, "ropes": ropes,
        "maskc": np.ascontiguousarray(maskc), "maskw": np.ascontiguousarray(maskw),
        "identb": ident.astype(NPBF), "identf": ident,
        "onesrow": np.ones((1, 128), np.float32),
    }


class KB:
    def __init__(self, debug=False):
        self.debug = debug
        self.nc = bass.Bass("TRN2", target_bir_lowering=False)
        self.ctx = ExitStack()
        self.sems_by_name = {}
        self.rings = {}
        self.em = Emitter({"dve": self.sem("chain_dve"), "act": self.sem("chain_act")})
        self.db_free = ()
        self.arena_free = None
        dt = self.nc.dram_tensor
        dbgkind = "Internal"

        self.x_in = dt("x", [HALF, HID], F32, kind="ExternalInput").ap()
        self.y_out = dt("y", [HALF, HID], F32, kind="ExternalOutput").ap()
        self.wT = {}
        for lay in ("a", "b"):
            for w in ("q", "k", "v"):
                self.wT[(lay, w)] = dt(f"w{w}{lay}T", [HID, NHL * 128], F32,
                                       kind="ExternalInput").ap()
            self.wT[(lay, "o")] = dt(f"wo{lay}T", [HID, HID], F32,
                                     kind="ExternalInput").ap()
        self.wfc = dt("wfc", [DFF, HID], F32, kind="ExternalInput").ap()
        self.wproj = dt("wproj", [HID, DFF], F32, kind="ExternalInput").ap()
        self.qnw_in = dt("qnw", [1, 128], F32, kind="ExternalInput").ap()
        self.knw_in = dt("knw", [1, 128], F32, kind="ExternalInput").ap()
        self.ropec_in = dt("ropec", [128, NT_ALL * 64], F32, kind="ExternalInput").ap()
        self.ropes_in = dt("ropes", [128, NT_ALL * 64], F32, kind="ExternalInput").ap()
        self.maskc_in = dt("maskc", [128, 128], F32, kind="ExternalInput").ap()
        self.maskw_in = dt("maskw", [128, 128], F32, kind="ExternalInput").ap()
        self.identb_in = dt("identb", [128, 128], BF16, kind="ExternalInput").ap()
        self.identf_in = dt("identf", [128, 128], F32, kind="ExternalInput").ap()
        self.onesrow_in = dt("onesrow", [1, 128], F32, kind="ExternalInput").ap()
        self.selA_in = dt("selA", [128, 128], BF16, kind="ExternalInput").ap()
        self.selB_in = dt("selB", [128, 128], BF16, kind="ExternalInput").ap()

        self.xr = [dt(f"xr{i}", [HALF, HID], F32, kind=dbgkind) for i in range(3)]
        self.ag_xn_in = dt("ag_xn_in", [HID, HALF], BF16)
        self.ag_xn_out = dt("ag_xn_out", [2 * HID, HALF], BF16)
        self.qT_d = dt("qT_d", [NHL * 128, S], BF16, kind=dbgkind)
        self.kT_d = dt("kT_d", [NHL * 128, S], BF16, kind=dbgkind)
        self.v_d = dt("v_d", [S, NHL * 128], BF16, kind=dbgkind)
        self.ctx_stack = dt("ctx_stack", [HID, HALF], BF16)
        self.ag_ctx_out = dt("ag_ctx_out", [2 * HID, HALF], BF16)
        self.ctx_sel_d = dt("ctx_sel_d", [HID, HALF], BF16, kind=dbgkind)
        self.qfc_d = dt("qfc_d", [DFF, HID], BF16, kind=dbgkind)
        self.qproj_d = dt("qproj_d", [HID, DFF], BF16, kind=dbgkind)
        self.xnf_d = dt("xnf_d", [HID, HALF], BF16)
        self.qfcT_d = dt("qfcT_d", [HID, DFF], BF16)
        self.qprojT_d = dt("qprojT_d", [DFF, HID], BF16)
        self.w_bf = {}
        for lay in ("a", "b"):
            for w in ("q", "k", "v"):
                self.w_bf[(lay, w)] = dt(f"wbf_{w}{lay}", [HID, NHL * 128], BF16)
            self.w_bf[(lay, "o")] = dt(f"wbf_o{lay}", [HID, HID], BF16)

        self._alloc_onchip()

    def sem(self, name):
        if name not in self.sems_by_name:
            h = self.ctx.enter_context(self.nc.semaphore(name))
            self.sems_by_name[name] = Sem(h, name)
        return self.sems_by_name[name]

    def ring(self, name, n):
        if name not in self.rings:
            self.rings[name] = Ring(n)
        r = self.rings[name]
        assert r.n == n
        return r

    def sbuf(self, name, shape, dtype):
        return self.ctx.enter_context(self.nc.sbuf_tensor(name, shape, dtype))

    def _alloc_onchip(self):
        sb = self.sbuf
        self.ropec = sb("ropec_s", [128, NT_ALL * 64], F32)
        self.ropes = sb("ropes_s", [128, NT_ALL * 64], F32)
        self.maskc = sb("maskc_s", [128, 128], F32)
        self.maskw = sb("maskw_s", [128, 128], F32)
        self.identb = sb("identb_s", [128, 128], BF16)
        self.identf = sb("identf_s", [128, 128], F32)
        self.onesrow = sb("onesrow_s", [1, 128], F32)
        self.ones_bf = sb("ones_bf", [128, 1], BF16)
        self.selA = sb("selA_s", [128, 128], BF16)
        self.selB = sb("selB_s", [128, 128], BF16)
        self.wrep_q = sb("wrep_q", [128, 128], F32)
        self.wrep_k = sb("wrep_k", [128, 128], F32)
        self.nrm_row = sb("nrm_row", [1, 128], F32)
        self.negB = {lay: sb(f"negB_{lay}", [128, NHL], F32) for lay in ("a", "b")}
        self.nq_acc = sb("nq_acc", [128, NHL], F32)
        self.nk_acc = sb("nk_acc", [128, NHL], F32)
        self.bst = sb("bst", [128, 8], F32)

        self.xt = sb("xt", [128, 2 * HID], F32)
        self.lnst = sb("lnst", [128, 8], F32)
        self.xn_tm = sb("xn_tm", [128, 2 * HID], BF16)
        self.tpo = sb("tpo", [128, 2 * 512], BF16)

        self.xnslab = sb("xnslab", [128, 2 * NCF * 512], BF16)
        self.wslab = sb("wslab", [128, 2 * NCF * 512], BF16)
        self.rtmp = sb("rtmp", [128, 1024], F32)
        self.qkv_tm = sb("qkv_tm", [128, 2 * 512], BF16)
        self.sqt = sb("sqt", [128, 2 * 512], BF16)
        self.r4t = self.sqt.ap()
        self.nrm4 = sb("nrm4", [128, 2 * 4], F32)
        self.rDs = sb("rDs", [1, 2 * 512], F32)
        self.bcsb = sb("bcsb", [128, 512], F32)
        self.rlt = sb("rlt", [128, 2 * 512], BF16)
        self.y2t = self.rlt.ap().bitcast(F32)
        self.xnew = sb("xnew", [128, 2048], F32)
        self.xsq = self.xnew.ap()[:, 0:HID]
        self.rsrow = sb("rsrow", [1, 512], F32)
        self.rstmp = sb("rstmp", [1, 512], F32)
        self.qsum = sb("qsum", [128, HID // 64], F32)
        self.qscl = sb("qscl", [128, HID // 64], BF16)
        self.qrec = sb("qrec", [128, HID // 64], F32)

        self.arena = sb("arena", [128, 32768], BF16)
        ar = self.arena.ap()
        arf = ar.bitcast(F32)
        self.qw_in = arf[:, 0:4096]
        self.qwb = ar[:, 8192:10240]
        self.qabs = ar[:, 10240:12288]
        self.qhat = ar[:, 12288:14336]
        self.qg1 = ar[:, 14336:16384]
        self.qout = ar[:, 16384:20480]
        self.qwf = arf[:, 10240:12288]
        self.ternf = arf[:, 12288:14336]
        self.qsclf = sb("qsclf", [128, HID // 64], F32)
        self.kTs = ar[:, 0:8192]
        self.vs = ar[:, 8192:16384]
        self.ptr = ar[:, 16384:18432]
        self.qTs = ar[:, 18432:19456]
        self.ctile = ar[:, 19456:20480]
        self.cslab = ar[:, 20480:28672]
        self.r2buf = ar

        self.banks = []
        for i in range(6):
            t = self.ctx.enter_context(self.nc.psum_tensor(f"pb{i}", [128, 512], F32))
            self.banks.append(Bank(t, self.sem(f"pbs{i}")))
        tt_ = self.ctx.enter_context(self.nc.psum_tensor("ptb", [128, 512], F32))
        self.tbank = Bank(tt_, self.sem("ptbs"))
        dd_ = self.ctx.enter_context(self.nc.psum_tensor("pdb", [1, 512], F32))
        self.dbank = Bank(dd_, self.sem("pdbs"))

    def dma(self, eng, out, in_, sem, waits=(), echo=False, **kw):
        def fn(e, out=out, in_=in_, kw=kw):
            return e.dma_start(out=out, in_=in_, **kw)
        self.em.emit(eng, fn, waits=waits, inc=sem, inc_amt=16)
        for _ in range(int(echo)):
            self.em.emit(eng, fn, waits=(), inc=sem, inc_amt=16)

    def pre_wait(self, eng, waits):
        self.em.emit(eng, None, waits=waits)

    def barrier(self):
        s = self.sem("s_phase")
        base = s.count
        for e in ENGS:
            self.em.emit(e, None, waits=(), inc=s, inc_amt=1)
        for e in ENGS:
            self.em.emit(e, None, waits=((s, base + 5),))

    def p0_consts(self):
        em, s = self.em, self.sem("s_const")
        for dst, src in ((self.ropec, self.ropec_in), (self.ropes, self.ropes_in),
                         (self.maskc, self.maskc_in), (self.maskw, self.maskw_in),
                         (self.identb, self.identb_in), (self.identf, self.identf_in),
                         (self.onesrow, self.onesrow_in), (self.selA, self.selA_in),
                         (self.selB, self.selB_in)):
            self.dma("pool", dst[:, :], src, s)
        self.dma("pool", self.nrm_row[:, :], self.qnw_in, s)
        w = ((s, s.count),)
        sm = self.sem("s_misc")
        em.emit("dve", lambda e: e.memset(self.ones_bf[:, :], 1.0), waits=w, inc=sm)
        em.emit("dve", lambda e: e.tensor_scalar_mul(
            self.nrm_row[:, :], self.nrm_row[:, :], SCALE_B), inc=sm)
        bk = self.banks[0]
        em.emit("pe", lambda e, bk=bk: e.matmul(bk.t[:, 0:128], self.onesrow[:, :],
                                                self.nrm_row[:, :], start=True, stop=True),
                waits=((sm, sm.count),) + bk.waits(), inc=bk.sem)
        em.emit("dve", lambda e, bk=bk: e.tensor_copy(self.wrep_q[:, :], bk.t[:, 0:128]),
                waits=bk.waits(), inc=bk.sem)
        self.dma("pool", self.nrm_row[:, :], self.knw_in, s, waits=bk.waits())
        bk2 = self.banks[1]
        em.emit("pe", lambda e, bk2=bk2: e.matmul(bk2.t[:, 0:128], self.onesrow[:, :],
                                                  self.nrm_row[:, :], start=True, stop=True),
                waits=((s, s.count),) + bk2.waits(), inc=bk2.sem)
        em.emit("dve", lambda e, bk2=bk2: e.tensor_copy(self.wrep_k[:, :], bk2.t[:, 0:128]),
                waits=bk2.waits(), inc=bk2.sem)

    def p1_quant(self):
        em = self.em
        sst = self.sem("s_qw_st")
        NG = HID // 64
        in_ring = self.ring("qw_in", 2)
        out_ring = self.ring("qout", 2)
        sq = self.sem("s_qwdma")

        def section(src_ap, dst_ap):
            sl = in_ring.slot
            self.dma("sp", self.qw_in[:, sl * HID:(sl + 1) * HID], src_ap, sq,
                     waits=in_ring.prod_waits())
            in_ring.filled(((sq, sq.count),))
            w = in_ring.cons_waits()
            inq = self.qw_in[:, sl * HID:(sl + 1) * HID]
            em.emit("dve", lambda e, inq=inq: e.tensor_copy(self.qwb, inq), waits=w)
            in_ring.released((em.last("dve"),))
            em.emit("act", lambda e: e.activation(self.qabs, self.qwb, AF.Abs),
                    waits=(em.last("dve"),))
            abs_done = em.last("act")
            em.emit("dve", lambda e: e.reduce_sum(
                self.qsum.ap(), self.qabs.rearrange("p (g e) -> p g e", e=64),
                axis=AX.X), waits=(abs_done,))
            em.emit("dve", lambda e: e.tensor_scalar(
                self.qscl.ap(), self.qsum.ap(), 1.0 / 64.0, 1e-8, ALU.mult, ALU.max))
            em.emit("dve", lambda e: e.tensor_copy(self.qwf, self.qwb))
            em.emit("dve", lambda e: e.tensor_copy(self.qsclf.ap(), self.qscl.ap()))
            em.emit("dve", lambda e: e.reciprocal(self.qrec.ap(), self.qsclf.ap()))
            em.emit("dve", lambda e: e.tensor_tensor(
                self.qhat.rearrange("p (g e) -> p g e", e=64),
                self.qwf.rearrange("p (g e) -> p g e", e=64),
                self.qrec.ap().unsqueeze(2).broadcast_to([128, NG, 64]), ALU.mult))
            em.emit("dve", lambda e: e.tensor_single_scalar(
                self.qg1, self.qhat, 0.5, ALU.is_gt))
            em.emit("dve", lambda e: e.tensor_single_scalar(
                self.qabs, self.qhat, -0.5, ALU.is_lt))
            em.emit("dve", lambda e: e.tensor_sub(self.ternf, self.qg1, self.qabs))
            osl = out_ring.slot
            oq = self.qout[:, osl * HID:(osl + 1) * HID]
            em.emit("dve", lambda e, oq=oq: e.tensor_tensor(
                oq.rearrange("p (g e) -> p g e", e=64),
                self.ternf.rearrange("p (g e) -> p g e", e=64),
                self.qsclf.ap().unsqueeze(2).broadcast_to([128, NG, 64]), ALU.mult),
                waits=out_ring.prod_waits())
            out_ring.filled((em.last("dve"),))
            self.dma("pool", dst_ap, oq, sst, waits=out_ring.cons_waits())
            out_ring.released(((sst, sst.count),))

        def convert(src_ap, dst_ap, cols):
            sl = in_ring.slot
            self.dma("sp", self.qw_in[:, sl * HID:sl * HID + cols], src_ap, sq,
                     waits=in_ring.prod_waits())
            in_ring.filled(((sq, sq.count),))
            w = in_ring.cons_waits()
            osl = out_ring.slot
            oq = self.qout[:, osl * HID:osl * HID + cols]
            em.emit("dve", lambda e, oq=oq, sl=sl, cols=cols: e.tensor_copy(
                oq, self.qw_in[:, sl * HID:sl * HID + cols]),
                waits=w + out_ring.prod_waits())
            cp_done = em.last("dve")
            in_ring.released((cp_done,))
            out_ring.filled((cp_done,))
            self.dma("pool", dst_ap, oq, sst, waits=out_ring.cons_waits())
            out_ring.released(((sst, sst.count),))

        for lay in ("a", "b"):
            for part in ("q", "k", "v", "o"):
                wsrc = self.wT[(lay, part)]
                wdst = self.w_bf[(lay, part)].ap()
                cols = wsrc.shape[1]
                for r in range(HID // 128):
                    for cs in range(cols // HID if cols > HID else 1):
                        cw = min(cols, HID)
                        convert(wsrc[r * 128:(r + 1) * 128, cs * HID:cs * HID + cw],
                                wdst[r * 128:(r + 1) * 128, cs * HID:cs * HID + cw],
                                cw)
        self.wconv_done = (sst, sst.count)

        qfc = self.qfc_d.ap()
        qpj = self.qproj_d.ap()
        for r in range(DFF // 128):
            section(self.wfc[r * 128:(r + 1) * 128, :],
                    qfc[r * 128:(r + 1) * 128, :])
        for r in range(HID // 128):
            for cs in range(DFF // HID):
                section(self.wproj[r * 128:(r + 1) * 128, cs * HID:(cs + 1) * HID],
                        qpj[r * 128:(r + 1) * 128, cs * HID:(cs + 1) * HID])
        w_ring = self.ring("wslab", 2)
        swd = self.sem("s_wslab_dma")
        sstT = self.sem("s_qwT_st")
        pe_relq = self.sem("s_pe_rel")

        def tblock(src_hbm, dst_hbm, R0, C0, dst_stride):
            wsl = w_ring.slot
            self.dma("sp", self.wslab[:, wsl * NCF * 512:wsl * NCF * 512 + 512],
                     src_hbm[R0:R0 + 512, C0:C0 + 128], swd, transpose=True,
                     waits=w_ring.prod_waits() + (((sst, sst.count),) if True else ()),
                     echo=True)
            w_ring.filled(((swd, swd.count),))
            out = bass.AP(dst_hbm, C0 * dst_stride + R0, [[dst_stride, 128], [1, 512]])
            self.dma("pool", out, self.wslab[:, wsl * NCF * 512:wsl * NCF * 512 + 512],
                     sstT, waits=w_ring.cons_waits())
            w_ring.released(((sstT, sstT.count),))

        qfcA = self.qfc_d
        qpjA = self.qproj_d
        for ob in range(DFF // 512):
            for kc in range(NCF):
                tblock(qfcA.ap(), self.qfcT_d, ob * 512, kc * 128, DFF)
        for ob in range(HID // 512):
            for kc in range(DFF // 128):
                tblock(qpjA.ap(), self.qprojT_d, ob * 512, kc * 128, HID)
        self.quant_done = (sstT, sstT.count)

    def ln_sweep(self, x_src, dst_handle, store_sem, guard=()):
        em = self.em
        stp = store_sem
        x_ring = self.ring("xt", 2)
        xn_ring = self.ring("xn_tm", 2)
        tpo_ring = self.ring("tpo", 2)
        sx = self.sem("s_lnx")
        tb = self.tbank
        tbv = tb.t.ap().bitcast(BF16)

        for ts in range(NT_OWN):
            sl = x_ring.slot
            self.dma("sp", self.xt[:, sl * HID:(sl + 1) * HID],
                     x_src[ts * 128:(ts + 1) * 128, :], sx,
                     waits=x_ring.prod_waits() + (guard if ts == 0 else ()))
            x_ring.filled(((sx, sx.count),))
            w = x_ring.cons_waits()
            xt = self.xt[:, sl * HID:(sl + 1) * HID]
            em.emit("act", lambda e, xt=xt: e.activation(self.xsq, xt, AF.Square),
                    waits=w + getattr(self, "_xsq_guard", ()))
            sq_done = em.last("act")
            em.emit("dve", lambda e, xt=xt: e.reduce_sum(self.lnst[:, 0:1], xt,
                                                         axis=AX.X), waits=w)
            em.emit("dve", lambda e: e.reduce_sum(self.lnst[:, 1:2], self.xsq,
                                                  axis=AX.X), waits=(sq_done,))
            self._xsq_guard = (em.last("dve"),)
            em.emit("dve", lambda e: e.tensor_scalar_mul(self.lnst[:, 2:3],
                                                         self.lnst[:, 0:1], 1.0 / HID))
            em.emit("dve", lambda e: e.tensor_scalar_mul(self.lnst[:, 3:4],
                                                         self.lnst[:, 1:2], 1.0 / HID))
            em.emit("dve", lambda e: e.tensor_mul(self.lnst[:, 4:5], self.lnst[:, 2:3],
                                                  self.lnst[:, 2:3]))
            em.emit("dve", lambda e: e.tensor_sub(self.lnst[:, 5:6], self.lnst[:, 3:4],
                                                  self.lnst[:, 4:5]))
            em.emit("dve", lambda e: e.tensor_scalar_add(self.lnst[:, 6:7],
                                                         self.lnst[:, 5:6], LN_EPS))
            em.emit("act", lambda e: e.activation(self.lnst[:, 7:8], self.lnst[:, 6:7],
                                                  AF.Sqrt), waits=(em.last("dve"),))
            em.emit("dve", lambda e: e.reciprocal(self.lnst[:, 6:7], self.lnst[:, 7:8]),
                    waits=(em.last("act"),))
            xsl = xn_ring.slot
            xn = self.xn_tm[:, xsl * HID:(xsl + 1) * HID]
            em.emit("dve", lambda e, xt=xt, xn=xn: e.tensor_scalar(
                xn, xt, self.lnst[:, 2:3], self.lnst[:, 6:7], ALU.subtract, ALU.mult),
                waits=xn_ring.prod_waits())
            xn_done = em.last("dve")
            x_ring.released((xn_done,))
            xn_ring.filled((xn_done,))
            xnw = xn_ring.cons_waits()
            pe_rel = self.sem("s_pe_rel")
            for grp in range(4):
                for i in range(4):
                    kc = grp * 4 + i
                    ww = (xnw + tb.waits()) if i == 0 else ()
                    em.emit("pe", lambda e, i=i, kc=kc, xn=xn, tbv=tbv: e.transpose(
                        tbv[:, i * 128:(i + 1) * 128], xn[:, kc * 128:(kc + 1) * 128],
                        self.identb[:, :]), waits=ww,
                        inc=tb.sem if i == 3 else None)
                tsl = tpo_ring.slot
                em.emit("dve", lambda e, tsl=tsl, tbv=tbv: e.tensor_copy(
                    self.tpo[:, tsl * 512:(tsl + 1) * 512], tbv[:, 0:512]),
                    waits=tb.waits() + tpo_ring.prod_waits(), inc=tb.sem)
                tpo_ring.filled((em.last("dve"),))
                if grp == 3:
                    xn_ring.released((em.last("dve"),))
                dst_ap = bass.AP(dst_handle, (grp * 4 * 128) * HALF + ts * 128,
                                 [[HALF, 128], [128 * HALF, 4], [1, 128]])
                src_ap = self.tpo[:, tsl * 512:(tsl + 1) * 512].rearrange(
                    "p (c e) -> p c e", e=128)
                self.dma("pool", dst_ap, src_ap, stp, waits=tpo_ring.cons_waits())
                tpo_ring.released(((stp, stp.count),))

    def collective(self, kind, ins, outs, waits):
        cc = self.sem("cc")

        def fn(e, ins=ins, outs=outs, kind=kind):
            return e.collective_compute(
                kind, mybir.AluOpType.bypass,
                replica_groups=[[0, 1], [2, 3], [4, 5], [6, 7]],
                ins=ins, outs=outs)
        self.em.emit("pool", fn, waits=waits, inc=cc, inc_amt=1)
        return (cc, cc.count)

    def qkv_phase(self, lay, ag_wait):
        em = self.em
        sst = self.sem("s_qkv_st")
        sxd = self.sem("s_xnslab_dma")
        swd = self.sem("s_wslab_dma")
        xn_ring = self.ring("xnslab", 2)
        w_ring = self.ring("wslab", 2)
        pool = [self.banks[i] for i in (0, 1, 2, 3)]
        pidx = 0
        pe_rel = self.sem("s_pe_rel")

        skvd = self.sem("s_kv_dma")
        sqd = self.sem("s_qT_dma")
        self.pre_wait("pool", ((skvd, skvd.count), (sqd, sqd.count)))

        self._sq_guard = ()
        tbs = self.tbank.sem
        em.emit("dve", lambda e: e.memset(self.nq_acc[:, :], 0.0),
                waits=((tbs, tbs.count),))
        em.emit("dve", lambda e: e.memset(self.nk_acc[:, :], 0.0))

        parts = [("q", 0), ("q", 1), ("k", 0), ("k", 1), ("v", 0), ("v", 1)]
        for tt in range(8):
            blk = tt // 4
            tloc = (tt % 4) * 512
            xsl = xn_ring.slot
            for kc in range(NCF):
                srcap = bass.AP(self.ag_xn_out, (blk * HID + kc * 128) * HALF + tloc,
                                [[HALF, 128], [1, 512]])
                self.dma("sp", self.xnslab[:, xsl * NCF * 512 + kc * 512:
                                           xsl * NCF * 512 + (kc + 1) * 512],
                         srcap, sxd,
                         waits=(xn_ring.prod_waits() + (ag_wait,)) if kc == 0 else (),
                         echo=(kc == NCF - 1))
            xn_ring.filled(((sxd, sxd.count),))
            xnw = xn_ring.cons_waits()
            xbase = xsl * NCF * 512

            for part, o5 in parts:
                wsl = w_ring.slot
                wsrc = self.w_bf[(lay, part)].ap()
                for kc in range(NCF):
                    self.dma("sp",
                             self.wslab[:, wsl * NCF * 512 + kc * 512:
                                        wsl * NCF * 512 + (kc + 1) * 512],
                             wsrc[kc * 128:(kc + 1) * 128, o5 * 512:(o5 + 1) * 512],
                             swd, waits=(w_ring.prod_waits() +
                                         ((self.wconv_done,) if (tt == 0 and o5 == 0
                                                                 and part == "q") else ()))
                             if kc == 0 else (), echo=(kc == NCF - 1))
                w_ring.filled(((swd, swd.count),))
                wbase = wsl * NCF * 512
                www = w_ring.cons_waits()

                for tsu in range(4):
                    bk = pool[pidx % 4]
                    pidx += 1
                    for kc in range(NCF):
                        ww = (bk.waits() + xnw + www) if kc == 0 else ()
                        em.emit("pe", lambda e, bk=bk, kc=kc, xbase=xbase,
                                wbase=wbase, tsu=tsu: e.matmul(
                                    bk.t[:, :],
                                    self.xnslab[:, xbase + kc * 512 + tsu * 128:
                                                xbase + kc * 512 + (tsu + 1) * 128],
                                    self.wslab[:, wbase + kc * 512:wbase + (kc + 1) * 512],
                                    start=(kc == 0), stop=(kc == NCF - 1)),
                                waits=ww, inc=bk.sem if kc == NCF - 1 else None)
                    mm_done = (bk.sem, bk.sem.count)
                    self._qkv_evac(lay, part, o5, tt * 4 + tsu, bk, sst)
                w_ring.released((mm_done,))
            xn_ring.released((mm_done,))

        self._negb_finalize(lay)
        self.qkv_done = (sst, sst.count)

    def _qkv_evac(self, lay, part, o5, gts, bk, sst):
        em = self.em
        qk_ring = self.ring("qkv_tm", 2)
        tpo_ring = self.ring("tpo", 2)
        pe_rel = self.sem("s_pe_rel")

        if part == "v":
            qsl = qk_ring.slot
            dst = self.qkv_tm[:, qsl * 512:(qsl + 1) * 512]
            em.emit("dve", lambda e, bk=bk, dst=dst: e.tensor_copy(dst, bk.t[:, :]),
                    waits=bk.waits() + qk_ring.prod_waits(), inc=bk.sem)
            qk_ring.filled((em.last("dve"),))
            out = bass.AP(self.v_d, (gts * 128) * (NHL * 128) + o5 * 512,
                          [[NHL * 128, 128], [1, 512]])
            self.dma("pool", out, dst, sst, waits=qk_ring.cons_waits())
            qk_ring.released(((sst, sst.count),))
            return

        t1 = self.rtmp[:, 0:256].rearrange("p (h e) -> p h e", e=64)
        t2 = self.rtmp[:, 256:512].rearrange("p (h e) -> p h e", e=64)
        t3 = self.rtmp[:, 512:1024]
        c_ap = self.ropec[:, gts * 64:(gts + 1) * 64].unsqueeze(1).broadcast_to([128, 4, 64])
        s_ap = self.ropes[:, gts * 64:(gts + 1) * 64].unsqueeze(1).broadcast_to([128, 4, 64])
        bv = bk.t.ap().rearrange("p (h d) -> p h d", d=128)
        bw = bk.waits()

        if lay == "b":
            em.emit("act", lambda e, bk=bk: e.activation(self.sqt[:, 0:512],
                                                         bk.t[:, :], AF.Square),
                    waits=bw + self._sq_guard)
            em.emit("dve", lambda e: e.reduce_sum(
                self.nrm4[:, 0:4],
                self.sqt[:, 0:512].rearrange("p (h d) -> p h d", d=128),
                axis=AX.X), waits=(em.last("act"),))
            self._sq_guard = (em.last("dve"),)
            em.emit("dve", lambda e: e.tensor_scalar(
                self.nrm4[:, 0:4], self.nrm4[:, 0:4], 1.0 / 128.0, RMS_EPS_B,
                ALU.mult, ALU.add))
            em.emit("act", lambda e: e.activation(self.nrm4[:, 0:4], self.nrm4[:, 0:4],
                                                  AF.Sqrt), waits=(em.last("dve"),))
            em.emit("dve", lambda e: e.reciprocal(self.nrm4[:, 0:4], self.nrm4[:, 0:4]),
                    waits=(em.last("act"),))
            em.emit("dve", lambda e, t3=t3: e.tensor_tensor(
                t3.rearrange("p (h d) -> p h d", d=128), bv,
                self.nrm4[:, 0:4].unsqueeze(2).broadcast_to([128, 4, 128]),
                ALU.mult), inc=bk.sem)
            wrep = self.wrep_q if part == "q" else self.wrep_k
            em.emit("dve", lambda e, t3=t3, wrep=wrep: e.tensor_tensor(
                t3.rearrange("p (h d) -> p h d", d=128),
                t3.rearrange("p (h d) -> p h d", d=128),
                wrep.ap().unsqueeze(1).broadcast_to([128, 4, 128]), ALU.mult))
            sv = t3.rearrange("p (h d) -> p h d", d=128)
            brel = None
        else:
            sv = bv
            brel = bk

        src1 = sv[:, :, 0:64]
        src2 = sv[:, :, 64:128]
        qsl = qk_ring.slot
        qtm = self.qkv_tm[:, qsl * 512:(qsl + 1) * 512]
        qv = qtm.rearrange("p (h d) -> p h d", d=128)

        em.emit("dve", lambda e, src1=src1, t1=t1, c_ap=c_ap: e.tensor_tensor(
            t1, src1, c_ap, ALU.mult),
            waits=qk_ring.prod_waits() + (bw if lay == "a" else ()))
        em.emit("dve", lambda e, src2=src2, t2=t2, s_ap=s_ap: e.tensor_tensor(
            t2, src2, s_ap, ALU.mult))
        if lay == "a" and part == "q":
            em.emit("dve", lambda e, t1=t1, t2=t2: e.tensor_sub(t1, t1, t2))
            em.emit("dve", lambda e, qv=qv, t1=t1: e.tensor_scalar_mul(
                qv[:, :, 0:64], t1, SCALE_A))
        else:
            em.emit("dve", lambda e, qv=qv, t1=t1, t2=t2: e.tensor_tensor(
                qv[:, :, 0:64], t1, t2, ALU.subtract))
        em.emit("dve", lambda e, src2=src2, t1=t1, c_ap=c_ap: e.tensor_tensor(
            t1, src2, c_ap, ALU.mult))
        em.emit("dve", lambda e, src1=src1, t2=t2, s_ap=s_ap: e.tensor_tensor(
            t2, src1, s_ap, ALU.mult),
            inc=(brel.sem if brel is not None else None))
        if lay == "a" and part == "q":
            em.emit("dve", lambda e, t1=t1, t2=t2: e.tensor_add(t1, t1, t2))
            em.emit("dve", lambda e, qv=qv, t1=t1: e.tensor_scalar_mul(
                qv[:, :, 64:128], t1, SCALE_A))
        else:
            em.emit("dve", lambda e, qv=qv, t1=t1, t2=t2: e.tensor_tensor(
                qv[:, :, 64:128], t1, t2, ALU.add))
        rope_done = em.last("dve")
        qk_ring.filled((rope_done,))
        qw = qk_ring.cons_waits()

        acc = self.nq_acc if part == "q" else self.nk_acc
        em.emit("act", lambda e, qtm=qtm: e.activation(self.sqt[:, 0:512], qtm,
                                                       AF.Square),
                waits=qw + self._sq_guard)
        sq_done = em.last("act")
        em.emit("dve", lambda e: e.reduce_sum(
            self.nrm4[:, 0:4],
            self.sqt[:, 0:512].rearrange("p (h d) -> p h d", d=128), axis=AX.X),
            waits=(sq_done,))
        self._sq_guard = (em.last("dve"),)
        em.emit("dve", lambda e, acc=acc, o5=o5: e.tensor_max(
            acc[:, o5 * 4:(o5 + 1) * 4], acc[:, o5 * 4:(o5 + 1) * 4],
            self.nrm4[:, 0:4]))

        tb = self.tbank
        tbv = tb.t.ap().bitcast(BF16)
        for i in range(4):
            ww = (qw + tb.waits()) if i == 0 else ()
            em.emit("pe", lambda e, i=i, qtm=qtm, tbv=tbv: e.transpose(
                tbv[:, i * 128:(i + 1) * 128], qtm[:, i * 128:(i + 1) * 128],
                self.identb[:, :]), waits=ww, inc=tb.sem if i == 3 else None)
        qk_ring.released(((tb.sem, tb.sem.count), sq_done))
        tsl = tpo_ring.slot
        em.emit("dve", lambda e, tsl=tsl, tbv=tbv: e.tensor_copy(
            self.tpo[:, tsl * 512:(tsl + 1) * 512], tbv[:, 0:512]),
            waits=tb.waits() + tpo_ring.prod_waits(), inc=tb.sem)
        tpo_ring.filled((em.last("dve"),))
        dstt = self.qT_d if part == "q" else self.kT_d
        out = bass.AP(dstt, (o5 * 4 * 128) * S + gts * 128,
                      [[S, 128], [128 * S, 4], [1, 128]])
        self.dma("pool", out,
                 self.tpo[:, tsl * 512:(tsl + 1) * 512].rearrange("p (c e) -> p c e", e=128),
                 sst, waits=tpo_ring.cons_waits())
        tpo_ring.released(((sst, sst.count),))

    def _negb_finalize(self, lay):
        em = self.em
        tb = self.tbank
        for idx, acc in enumerate((self.nq_acc, self.nk_acc)):
            em.emit("pe", lambda e, acc=acc: e.transpose(
                tb.t[0:NHL, 0:128], acc[:, :], self.identf[:, :]),
                waits=tb.waits() + ((em.last("dve")[0], em.last("dve")[1]),),
                inc=tb.sem)
            em.emit("dve", lambda e, idx=idx: e.reduce_max(
                self.bst[0:NHL, idx:idx + 1], tb.t[0:NHL, 0:128], axis=AX.X),
                waits=tb.waits(), inc=tb.sem)
        em.emit("dve", lambda e: e.tensor_mul(self.bst[0:NHL, 2:3],
                                              self.bst[0:NHL, 0:1],
                                              self.bst[0:NHL, 1:2]))
        em.emit("act", lambda e: e.activation(self.bst[0:NHL, 3:4],
                                              self.bst[0:NHL, 2:3], AF.Sqrt),
                waits=(em.last("dve"),))
        em.emit("dve", lambda e: e.tensor_scalar_mul(
            self.bst[0:NHL, 4:5], self.bst[0:NHL, 3:4], -1.0),
            waits=(em.last("act"),))
        em.emit("pe", lambda e: e.transpose(tb.t[0:1, 0:128], self.bst[:, 4:5],
                                            self.identf[:, :]),
                waits=(em.last("dve"),) + tb.waits(), inc=tb.sem)
        em.emit("dve", lambda e: e.tensor_copy(self.bst[0:1, 0:NHL], tb.t[0:1, 0:NHL]),
                waits=tb.waits(), inc=tb.sem)
        bb = self.banks[4]
        em.emit("pe", lambda e, bb=bb: e.matmul(
            bb.t[:, 0:NHL], self.onesrow[:, :], self.bst[0:1, 0:NHL],
            start=True, stop=True), waits=(em.last("dve"),) + bb.waits(), inc=bb.sem)
        nb = self.negB[lay]
        em.emit("dve", lambda e, bb=bb, nb=nb: e.tensor_copy(nb[:, :], bb.t[:, 0:NHL]),
                waits=bb.waits(), inc=bb.sem)
        self.negb_wait = em.last("dve")

    def attn_phase(self, lay, ctx_prev_ag_wait):
        em = self.em
        scst = self.sem("s_ctx_st")
        skvd = self.sem("s_kv_dma")
        sqd = self.sem("s_qT_dma")
        spt = self.sem("s_pt")
        kv_ring = self.ring("kTs", 2)
        v_ring = self.ring("vs", 2)
        q_ring = self.ring("qTs", 2)
        pt_ring = self.ring("ptr", 4)
        ct_ring = self.ring("ctile", 2)
        rd_ring = self.ring("rDs", 2)
        pool = [self.banks[i] for i in (0, 1, 2)]
        ctxbanks = [self.banks[4], self.banks[5]]
        bcb = self.banks[3]
        db = self.dbank
        qkv_done = self.qkv_done
        negb_w = self.negb_wait

        if ctx_prev_ag_wait is not None:
            self.pre_wait("pool", (ctx_prev_ag_wait,))

        for hl in range(NHL):
            ksl = kv_ring.slot
            extra = ((qkv_done, self.quant_done) +
                     ((self.arena_free,) if self.arena_free else ())
                     if hl == 0 else ())
            self.dma("sp", self.kTs[:, ksl * S:(ksl + 1) * S],
                     self.kT_d.ap()[hl * 128:(hl + 1) * 128, :], skvd,
                     waits=kv_ring.prod_waits() + extra, echo=True)
            kv_ring.filled(((skvd, skvd.count),))
            vsl = v_ring.slot
            vsrc = bass.AP(self.v_d, hl * 128,
                           [[NHL * 128, 128], [128 * NHL * 128, NT_ALL], [1, 128]])
            self.dma("sp", self.vs[:, vsl * S:(vsl + 1) * S].rearrange(
                "p (c e) -> p c e", e=128), vsrc, skvd,
                waits=v_ring.prod_waits(), echo=True)
            v_ring.filled(((skvd, skvd.count),))
            kvw = kv_ring.cons_waits() + v_ring.cons_waits()
            kbase = ksl * S
            vbase = vsl * S

            for g in range(8):
                qsl = q_ring.slot
                self.dma("sp", self.qTs[:, qsl * 512:(qsl + 1) * 512],
                         self.qT_d.ap()[hl * 128:(hl + 1) * 128, g * 512:(g + 1) * 512],
                         sqd, waits=q_ring.prod_waits(), echo=True)
                q_ring.filled(((sqd, sqd.count),))
                qw = q_ring.cons_waits()
                qb = qsl * 512
                tiles = attn_tiles(lay, g)
                n = len(tiles)
                cb = ctxbanks[g % 2]
                exp_t = [None] * n
                stage = []

                pv_last = [None]

                def emit_pv(j, n=n, cb=cb, tiles=tiles, vbase=vbase, exp_t=exp_t,
                            kvw=kvw):
                    kt, qlo, qhi, _m = tiles[j]
                    psl = j % 4
                    em.emit("pe", lambda e, kt=kt, qlo=qlo, qhi=qhi, psl=psl, cb=cb,
                            j=j, n=n, vbase=vbase: e.matmul(
                                cb.t[:, qlo:qhi],
                                self.vs[:, vbase + kt * 128:vbase + (kt + 1) * 128],
                                self.ptr[:, psl * 512 + qlo:psl * 512 + qhi],
                                start=(j == 0), stop=(j == n - 1)),
                            waits=(exp_t[j],) + (cb.waits() if j == 0 else ()),
                            inc=(cb.sem if j == n - 1 else None))
                    if j == n - 1:
                        pv_last[0] = (cb.sem, cb.sem.count)
                    em.emit("pe", lambda e, qlo=qlo, qhi=qhi, psl=psl, j=j, n=n:
                            e.matmul(db.t[0:1, qlo:qhi], self.ones_bf[:, :],
                                     self.ptr[:, psl * 512 + qlo:psl * 512 + qhi],
                                     start=(j == 0), stop=(j == n - 1)),
                            waits=(self.db_free if j == 0 else ()), inc=spt)
                    pt_ring.released(((spt, spt.count),))

                for j, (kt, qlo, qhi, m) in enumerate(tiles):
                    bk = pool[j % 3]
                    ww = bk.waits()
                    if j == 0:
                        ww = ww + kvw + qw + (negb_w,)
                    em.emit("pe", lambda e, bk=bk, kt=kt, qlo=qlo, qhi=qhi,
                            kbase=kbase, qb=qb: e.matmul(
                                bk.t[:, qlo:qhi],
                                self.kTs[:, kbase + kt * 128:kbase + (kt + 1) * 128],
                                self.qTs[:, qb + qlo:qb + qhi],
                                start=True, stop=True), waits=ww, inc=bk.sem)
                    if j == n - 1:
                        q_last = (bk.sem, bk.sem.count)
                    if m is not None:
                        mt, mc = m
                        msk = self.maskc if mt == "c" else self.maskw
                        em.emit("dve", lambda e, bk=bk, mc=mc, msk=msk: e.tensor_tensor(
                            bk.t[:, mc:mc + 128], bk.t[:, mc:mc + 128], msk[:, :],
                            ALU.add), waits=bk.waits(), inc=bk.sem)
                    psl = pt_ring.slot
                    em.emit("act", lambda e, bk=bk, qlo=qlo, qhi=qhi, psl=psl,
                            hl=hl, lay=lay: e.activation(
                                self.ptr[:, psl * 512 + qlo:psl * 512 + qhi],
                                bk.t[:, qlo:qhi], AF.Exp,
                                bias=self.negB[lay][:, hl:hl + 1]),
                            waits=bk.waits() + pt_ring.prod_waits(), inc=bk.sem)
                    exp_t[j] = em.last("act")
                    pt_ring.filled((exp_t[j],))
                    stage.append(j)
                    if len(stage) > 2:
                        emit_pv(stage.pop(0))
                while stage:
                    emit_pv(stage.pop(0))

                rsl = rd_ring.slot
                em.emit("dve", lambda e, rsl=rsl: e.reciprocal(
                    self.rDs[0:1, rsl * 512:(rsl + 1) * 512], db.t[0:1, :]),
                    waits=((spt, spt.count),) + rd_ring.prod_waits())
                recip_done = em.last("dve")
                self.db_free = (recip_done,)
                rd_ring.filled((recip_done,))
                em.emit("pe", lambda e, rsl=rsl, bcb=bcb: e.matmul(
                    bcb.t[:, :], self.onesrow[:, :],
                    self.rDs[0:1, rsl * 512:(rsl + 1) * 512], start=True, stop=True),
                    waits=rd_ring.cons_waits() + bcb.waits(), inc=bcb.sem)
                rd_ring.released(((bcb.sem, bcb.sem.count),))
                em.emit("dve", lambda e, bcb=bcb: e.tensor_copy(self.bcsb[:, :],
                                                                bcb.t[:, :]),
                        waits=bcb.waits(), inc=bcb.sem)
                bc_done = em.last("dve")
                csl = ct_ring.slot
                em.emit("dve", lambda e, csl=csl, cb=cb: e.tensor_tensor(
                    self.ctile[:, csl * 512:(csl + 1) * 512], cb.t[:, :],
                    self.bcsb[:, :], ALU.mult),
                    waits=cb.waits() + (bc_done,) + ct_ring.prod_waits(), inc=cb.sem)
                ct_ring.filled((em.last("dve"),))
                out = bass.AP(self.ctx_stack,
                              ((g // 4) * NHL * 128 + hl * 128) * HALF + (g % 4) * 512,
                              [[HALF, 128], [1, 512]])
                self.dma("pool", out, self.ctile[:, csl * 512:(csl + 1) * 512],
                         scst, waits=ct_ring.cons_waits())
                ct_ring.released(((scst, scst.count),))
                q_ring.released((q_last,))
            kv_ring.released((q_last,))
            v_ring.released((pv_last[0],))
        self.ctx_st_done = (scst, scst.count)

    def sel_build(self, ag_wait):
        em = self.em
        ssel = self.sem("s_sel_st")
        scd = self.sem("s_cs_dma")
        cs_ring = self.ring("cslab", 2)
        tpo_ring = self.ring("tpo", 2)
        pe_rel = self.sem("s_pe_rel")
        pool = [self.banks[i] for i in (0, 1)]
        CSW = 4096
        for ts in range(NT_OWN):
            sl = cs_ring.slot
            first = True
            for cand in range(2):
                for h8 in range(2):
                    srcap = bass.AP(self.ag_ctx_out,
                                    (h8 * HID + cand * NHL * 128) * HALF + ts * 128,
                                    [[HALF, 128], [128 * HALF, 8], [1, 128]])
                    self.dma("sp",
                             self.cslab[:, sl * CSW + cand * 2048 + h8 * 1024:
                                        sl * CSW + cand * 2048 + (h8 + 1) * 1024]
                             .rearrange("p (c e) -> p c e", e=128),
                             srcap, scd,
                             waits=(cs_ring.prod_waits() + (ag_wait,)) if first else (),
                             echo=(cand == 1 and h8 == 1))
                    first = False
            cs_ring.filled(((scd, scd.count),))
            cw = cs_ring.cons_waits()
            sbase = sl * CSW
            for kcg in range(4):
                bk = pool[kcg % 2]
                em.emit("pe", lambda e, bk=bk, kcg=kcg, sbase=sbase: e.matmul(
                    bk.t[:, :], self.selA[:, :],
                    self.cslab[:, sbase + kcg * 512:sbase + (kcg + 1) * 512],
                    start=True, stop=False),
                    waits=bk.waits() + (cw if kcg == 0 else ()))
                em.emit("pe", lambda e, bk=bk, kcg=kcg, sbase=sbase: e.matmul(
                    bk.t[:, :], self.selB[:, :],
                    self.cslab[:, sbase + 2048 + kcg * 512:sbase + 2048 + (kcg + 1) * 512],
                    start=False, stop=True), inc=bk.sem)
                if kcg == 3:
                    cs_ring.released(((bk.sem, bk.sem.count),))
                tsl = tpo_ring.slot
                em.emit("dve", lambda e, bk=bk, tsl=tsl: e.tensor_copy(
                    self.tpo[:, tsl * 512:(tsl + 1) * 512], bk.t[:, :]),
                    waits=bk.waits() + tpo_ring.prod_waits(), inc=bk.sem)
                tpo_ring.filled((em.last("dve"),))
                out = bass.AP(self.ctx_sel_d, (kcg * 4 * 128) * HALF + ts * 128,
                              [[HALF, 128], [128 * HALF, 4], [1, 128]])
                self.dma("pool", out,
                         self.tpo[:, tsl * 512:(tsl + 1) * 512].rearrange(
                             "p (c e) -> p c e", e=128),
                         ssel, waits=tpo_ring.cons_waits())
                tpo_ring.released(((ssel, ssel.count),))
        self.sel_done = (ssel, ssel.count)

    def oproj_phase(self, lay, x_src, x_dst_handle, x_dst_sem_name):
        em = self.em
        sst = self.sem(x_dst_sem_name)
        swd = self.sem("s_wslab_dma")
        scd = self.sem("s_cs_dma")
        sxd = self.sem("s_x_dma")
        wo_ring = self.ring("wslab", 2)
        xi_ring = self.ring("xt", 2)
        xo_ring = self.ring("xnew", 1)
        pool = [self.banks[i] for i in (0, 1, 2, 3)]
        pe_rel = self.sem("s_pe_rel")
        pidx = 0
        wsrc = self.w_bf[(lay, "o")].ap()
        xdst = x_dst_handle

        ctxsb = self.arena.ap()
        sel_done = self.sel_done
        for kc in range(NCF):
            self.dma("sp", ctxsb[:, kc * 2048:(kc + 1) * 2048],
                     self.ctx_sel_d.ap()[kc * 128:(kc + 1) * 128, :], scd,
                     waits=((sel_done, self.ctx_st_done,
                             self.quant_done) if kc == 0 else ()),
                     echo=(kc == NCF - 1))
        ctx_w = (scd, scd.count)

        for og in range(4):
            wsl = wo_ring.slot
            for kc in range(NCF):
                self.dma("sp",
                         self.wslab[:, wsl * NCF * 512 + kc * 512:
                                    wsl * NCF * 512 + (kc + 1) * 512],
                         wsrc[kc * 128:(kc + 1) * 128, og * 512:(og + 1) * 512],
                         swd, waits=wo_ring.prod_waits() if kc == 0 else (),
                         echo=(kc == NCF - 1))
            wo_ring.filled(((swd, swd.count),))
            wbase = wsl * NCF * 512
            www = wo_ring.cons_waits()
            for ts in range(NT_OWN):
                xsl = xi_ring.slot
                self.dma("act", self.xt[:, xsl * HID:xsl * HID + 512],
                         x_src[ts * 128:(ts + 1) * 128, og * 512:(og + 1) * 512],
                         sxd, waits=xi_ring.prod_waits())
                xi_ring.filled(((sxd, sxd.count),))
                xw = xi_ring.cons_waits()
                bk = pool[pidx % 4]
                pidx += 1
                for kc in range(NCF):
                    ww = (bk.waits() + (ctx_w,) + www) if kc == 0 else ()
                    em.emit("pe", lambda e, bk=bk, kc=kc, ts=ts, wbase=wbase,
                            ctxsb=ctxsb: e.matmul(
                                bk.t[:, :],
                                ctxsb[:, kc * 2048 + ts * 128:kc * 2048 + (ts + 1) * 128],
                                self.wslab[:, wbase + kc * 512:wbase + (kc + 1) * 512],
                                start=(kc == 0), stop=(kc == NCF - 1)),
                            waits=ww, inc=(bk.sem if kc == NCF - 1 else None))
                mm_done = (bk.sem, bk.sem.count)
                nsl = xo_ring.slot
                em.emit("dve", lambda e, bk=bk, nsl=nsl, xsl=xsl: e.tensor_tensor(
                    self.xnew[:, nsl * 2048:nsl * 2048 + 512], bk.t[:, :],
                    self.xt[:, xsl * HID:xsl * HID + 512], ALU.add),
                    waits=bk.waits() + xw + xo_ring.prod_waits(), inc=bk.sem)
                add_done = em.last("dve")
                xi_ring.released((add_done,))
                xo_ring.filled((add_done,))
                out = bass.AP(xdst, (ts * 128) * HID + og * 512,
                              [[HID, 128], [1, 512]])
                self.dma("pool", out, self.xnew[:, nsl * 2048:nsl * 2048 + 512],
                         sst, waits=xo_ring.cons_waits())
                xo_ring.released(((sst, sst.count),))
            wo_ring.released((mm_done,))
        self.oproj_all_done = mm_done
        self.arena_free = mm_done
        return (sst, sst.count)

    def ffn_phase(self, x_src, x_dst_handle, x_dst_sem_name, x_guard, xnf_guard):
        em = self.em
        sxnf = self.sem("s_xnf_st")
        self.ln_sweep(x_src, self.xnf_d, sxnf, guard=x_guard + xnf_guard)
        ln_done = (sxnf, sxnf.count)
        sst = self.sem(x_dst_sem_name)
        sxd = self.sem("s_xnslab_dma")
        swd = self.sem("s_wslab_dma")
        sxid = self.sem("s_x_dma")
        xn_ring = self.ring("xnslab", 2)
        w_ring = self.ring("wslab", 2)
        rl_ring = self.ring("rlt", 2)
        r4_ring = self.ring("r4t", 2)
        y2_ring = self.ring("y2t", 1)
        xi_ring = self.ring("xt", 2)
        xo_ring = self.ring("xnew", 1)
        pool = [self.banks[i] for i in (0, 1, 2, 3)]
        bcb = self.banks[5]
        db = self.dbank
        tb = self.tbank
        pe_rel = self.sem("s_pe_rel")
        quant_done = self.quant_done

        for tt in range(4):
            xsl = xn_ring.slot
            srcap = bass.AP(self.xnf_d, tt * 512,
                            [[HALF, 128], [128 * HALF, NCF], [1, 512]])
            self.dma("sp",
                     self.xnslab[:, xsl * NCF * 512:(xsl + 1) * NCF * 512]
                     .rearrange("p (c e) -> p c e", e=512),
                     srcap, sxd, waits=xn_ring.prod_waits() + (ln_done,),
                     echo=not getattr(self, "no_ffn_echo", False))
            xn_ring.filled(((sxd, sxd.count),))
            xnw = xn_ring.cons_waits()
            xbase = xsl * NCF * 512

            for og in range(16):
                wsl = w_ring.slot
                if getattr(self, "ffn_oldload", False):
                    for kc in range(NCF):
                        self.dma("sp",
                                 self.wslab[:, wsl * NCF * 512 + kc * 512:
                                            wsl * NCF * 512 + (kc + 1) * 512],
                                 self.qfc_d.ap()[og * 512:(og + 1) * 512,
                                                 kc * 128:(kc + 1) * 128],
                                 swd, transpose=True,
                                 waits=(w_ring.prod_waits() +
                                        ((quant_done,) if (tt == 0 and og == 0) else ()))
                                 if kc == 0 else (), echo=(kc == NCF - 1))
                else:
                    srcw = bass.AP(self.qfcT_d, og * 512,
                                   [[DFF, 128], [128 * DFF, NCF], [1, 512]])
                    self.dma("sp",
                             self.wslab[:, wsl * NCF * 512:(wsl + 1) * NCF * 512]
                             .rearrange("p (c e) -> p c e", e=512),
                             srcw, swd,
                             waits=w_ring.prod_waits() +
                             ((quant_done,) if (tt == 0 and og == 0) else ()),
                             echo=not getattr(self, "no_ffn_echo", False))
                w_ring.filled(((swd, swd.count),))
                www = w_ring.cons_waits()
                wbase = wsl * NCF * 512
                for oc in range(4):
                    oidx = og * 4 + oc
                    bk = pool[oidx % 4]
                    for kc in range(NCF):
                        ww = (bk.waits() + xnw + www) if kc == 0 else ()
                        em.emit("pe", lambda e, bk=bk, kc=kc, oc=oc, wbase=wbase,
                                xbase=xbase: e.matmul(
                                    bk.t[:, :],
                                    self.wslab[:, wbase + kc * 512 + oc * 128:
                                               wbase + kc * 512 + (oc + 1) * 128],
                                    self.xnslab[:, xbase + kc * 512:xbase + (kc + 1) * 512],
                                    start=(kc == 0), stop=(kc == NCF - 1)),
                                waits=ww, inc=(bk.sem if kc == NCF - 1 else None))
                    if oc == 3:
                        og_mm_done = (bk.sem, bk.sem.count)
                    rsl = rl_ring.slot
                    em.emit("dve", lambda e, bk=bk, rsl=rsl: e.tensor_scalar_max(
                        self.rlt[:, rsl * 512:(rsl + 1) * 512], bk.t[:, :], 0.0),
                        waits=bk.waits() + rl_ring.prod_waits(), inc=bk.sem)
                    rl_ring.filled((em.last("dve"),))
                    em.emit("act", lambda e, rsl=rsl, oidx=oidx: e.activation(
                        self.r2buf[:, oidx * 512:(oidx + 1) * 512],
                        self.rlt[:, rsl * 512:(rsl + 1) * 512], AF.Square),
                        waits=rl_ring.cons_waits() +
                        (((self.arena_free,) if self.arena_free else ())
                         if oidx == 0 else ()))
                    rl_ring.released((em.last("act"),))
                    f4 = r4_ring.slot
                    em.emit("act", lambda e, f4=f4, oidx=oidx: e.activation(
                        self.r4t[:, f4 * 512:(f4 + 1) * 512],
                        self.r2buf[:, oidx * 512:(oidx + 1) * 512], AF.Square),
                        waits=r4_ring.prod_waits())
                    r4_ring.filled((em.last("act"),))
                    em.emit("pe", lambda e, f4=f4, oidx=oidx: e.matmul(
                        db.t[0:1, :], self.ones_bf[:, :],
                        self.r4t[:, f4 * 512:(f4 + 1) * 512],
                        start=(oidx == 0), stop=(oidx == 63)),
                        waits=r4_ring.cons_waits() +
                        (self.db_free if oidx == 0 else ()),
                        inc=(db.sem if oidx == 63 else pe_rel))
                    if oidx == 63:
                        r4_ring.released(((db.sem, db.sem.count),))
                    else:
                        r4_ring.released(((pe_rel, pe_rel.count),))
                w_ring.released((og_mm_done,))
            xn_ring.released((og_mm_done,))

            em.emit("dve", lambda e: e.tensor_scalar(
                self.rstmp[:, :], db.t[0:1, :], 1.0 / DFF, FFN_EPS, ALU.mult, ALU.add),
                waits=db.waits())
            self.db_free = (em.last("dve"),)
            em.emit("act", lambda e: e.activation(self.rstmp[:, :], self.rstmp[:, :],
                                                  AF.Sqrt), waits=(em.last("dve"),))
            em.emit("dve", lambda e: e.reciprocal(self.rsrow[:, :], self.rstmp[:, :]),
                    waits=(em.last("act"),))
            em.emit("pe", lambda e, bcb=bcb: e.matmul(
                bcb.t[:, :], self.onesrow[:, :], self.rsrow[:, :],
                start=True, stop=True), waits=(em.last("dve"),) + bcb.waits(),
                inc=bcb.sem)
            em.emit("dve", lambda e, bcb=bcb: e.tensor_copy(self.bcsb[:, :], bcb.t[:, :]),
                    waits=bcb.waits(), inc=bcb.sem)
            rs_done = em.last("dve")

            for og2 in range(4):
                xsl2 = xi_ring.slot
                xsrc = bass.AP(x_src.tensor, x_src.offset + (tt * 512) * HID + og2 * 512,
                               [[HID, 128], [128 * HID, 4], [1, 512]])
                self.dma("sp",
                         self.xt[:, xsl2 * HID:(xsl2 + 1) * HID].rearrange(
                             "p (c e) -> p c e", e=512),
                         xsrc, sxid, waits=xi_ring.prod_waits())
                xi_ring.filled(((sxid, sxid.count),))
                xw2 = xi_ring.cons_waits()
                bks = [pool[i] for i in range(4)]
                for kg2 in range(4):
                    wsl = w_ring.slot
                    srcw = bass.AP(self.qprojT_d, (kg2 * 16 * 128) * HID + og2 * 512,
                                   [[HID, 128], [128 * HID, 16], [1, 512]])
                    self.dma("sp",
                             self.wslab[:, wsl * NCF * 512:(wsl + 1) * NCF * 512]
                             .rearrange("p (c e) -> p c e", e=512),
                             srcw, swd, waits=w_ring.prod_waits(),
                             echo=getattr(self, "mm2_echoes", 1))
                    w_ring.filled(((swd, swd.count),))
                    www2 = w_ring.cons_waits()
                    wbase = wsl * NCF * 512
                    for oci, oc2 in enumerate((0, 1, 2, 3)):
                        bk = bks[oc2]
                        for i in range(16):
                            kc = kg2 * 16 + i
                            ww = ()
                            if kc == 0:
                                ww = bk.waits()
                            if i == 0 and oci == 0:
                                ww = ww + www2
                            inc = (bk.sem if kc == 63 else
                                   (pe_rel if (oci == 3 and i == 15) else None))
                            em.emit("pe", lambda e, bk=bk, i=i, oc2=oc2, kc=kc,
                                    wbase=wbase: e.matmul(
                                        bk.t[:, :],
                                        self.wslab[:, wbase + i * 512 + oc2 * 128:
                                                   wbase + i * 512 + (oc2 + 1) * 128],
                                        self.r2buf[:, kc * 512:(kc + 1) * 512],
                                        start=(kc == 0), stop=(kc == 63)),
                                    waits=ww, inc=inc)
                        if oci == 3:
                            wrel = ((bk.sem, bk.sem.count) if kg2 == 3
                                    else (pe_rel, pe_rel.count))
                    w_ring.released((wrel,))
                nsl = xo_ring.slot
                add_dones = []
                for oc2 in range(4):
                    bk = bks[oc2]
                    em.emit("dve", lambda e, bk=bk: e.tensor_tensor(
                        self.y2t, bk.t[:, :], self.bcsb[:, :], ALU.mult),
                        waits=bk.waits() + (rs_done,) + y2_ring.prod_waits(),
                        inc=bk.sem)
                    y2_ring.filled((em.last("dve"),))
                    y2w = y2_ring.cons_waits()
                    for tsl in range(4):
                        em.emit("pe", lambda e, tsl=tsl: e.transpose(
                            tb.t[:, tsl * 128:(tsl + 1) * 128],
                            self.y2t[:, tsl * 128:(tsl + 1) * 128], self.identf[:, :]),
                            waits=(y2w + tb.waits()) if tsl == 0 else (),
                            inc=(tb.sem if tsl == 3 else None))
                    y2_ring.released(((tb.sem, tb.sem.count),))
                    wadd = xo_ring.prod_waits() if oc2 == 0 else ()
                    em.emit("dve", lambda e, nsl=nsl, oc2=oc2, xsl2=xsl2: e.tensor_tensor(
                        self.xnew[:, nsl * 2048:(nsl + 1) * 2048].rearrange(
                            "p (t e) -> p t e", e=512)[:, :, oc2 * 128:(oc2 + 1) * 128],
                        tb.t.ap().rearrange("p (t e) -> p t e", e=128),
                        self.xt[:, xsl2 * HID:(xsl2 + 1) * HID].rearrange(
                            "p (t e) -> p t e", e=512)[:, :, oc2 * 128:(oc2 + 1) * 128],
                        ALU.add), waits=tb.waits() + xw2 + wadd, inc=tb.sem)
                    add_dones.append(em.last("dve"))
                xi_ring.released((add_dones[-1],))
                xo_ring.filled((add_dones[-1],))
                out = bass.AP(x_dst_handle, (tt * 512) * HID + og2 * 512,
                              [[HID, 128], [128 * HID, 4], [1, 512]])
                self.dma("pool", out,
                         self.xnew[:, nsl * 2048:(nsl + 1) * 2048].rearrange(
                             "p (t e) -> p t e", e=512),
                         sst, waits=xo_ring.cons_waits())
                xo_ring.released(((sst, sst.count),))
                if og2 == 3:
                    self.arena_free = add_dones[-1]
        return (sst, sst.count)

    def finalize(self, final_waits):
        em = self.em
        if self.debug:
            dmp = self.sem("s_dump")
            dumps = [("d_xr0", self.xr[0], [HALF, HID], F32),
                     ("d_xr1", self.xr[1], [HALF, HID], F32),
                     ("d_xr2", self.xr[2], [HALF, HID], F32),
                     ("d_qT", self.qT_d, [NHL * 128, S], BF16),
                     ("d_kT", self.kT_d, [NHL * 128, S], BF16),
                     ("d_v", self.v_d, [S, NHL * 128], BF16),
                     ("d_ctxsel", self.ctx_sel_d, [HID, HALF], BF16),
                     ("d_qfc", self.qfc_d, [DFF, HID], BF16),
                     ("d_qproj", self.qproj_d, [HID, DFF], BF16),
                     ("d_xnf", self.xnf_d, [HID, HALF], BF16),
                     ("d_qfcT", self.qfcT_d, [HID, DFF], BF16),
                     ("d_qprojT", self.qprojT_d, [DFF, HID], BF16),
                     ("d_negba", self.negB["a"], None, F32),
                     ("d_negbb", self.negB["b"], None, F32)]
            for s, v in final_waits:
                em.emit("sp", None, waits=((s, v),))
            for name, srct, shape, dtyp in dumps:
                if shape is None:
                    o = self.nc.dram_tensor(name, [128, NHL], dtyp,
                                            kind="ExternalOutput").ap()
                    self.dma("sp", o, srct[:, :], dmp)
                    continue
                o = self.nc.dram_tensor(name, shape, dtyp, kind="ExternalOutput").ap()
                self.dma("sp", o, srct.ap()[:, :], dmp)
            em.emit("pool", None, waits=((dmp, dmp.count),))
        for s, v in final_waits:
            em.emit("pool", None, waits=((s, v),))
        with self.nc.Block() as block:
            block.tensor(lambda e: em.replay("pe", e))
            block.vector(lambda e: em.replay("dve", e))
            block.scalar(lambda e: em.replay("act", e))
            block.gpsimd(lambda e: em.replay("pool", e))
            block.sync(lambda e: em.replay("sp", e))
        self.ctx.close()
        return self.nc


def build(debug=False, upto=99):
    kb = KB(debug)
    kb.p0_consts()
    if upto == 0:
        return kb, kb.finalize([])
    kb.p1_quant()
    if upto == 1:
        return kb, kb.finalize([kb.quant_done])

    sagst = kb.sem("s_agxn_st")
    kb.ln_sweep(kb.x_in, kb.ag_xn_in, sagst, guard=())
    if upto == 2:
        return kb, kb.finalize([(sagst, sagst.count)])
    ag_w = kb.collective("AllGather", [kb.ag_xn_in.ap().opt()],
                         [kb.ag_xn_out.ap().opt()],
                         waits=((sagst, sagst.count),))
    if upto == 3:
        return kb, kb.finalize([ag_w])
    kb.qkv_phase("a", ag_w)
    if upto == 4:
        return kb, kb.finalize([kb.qkv_done])
    kb.attn_phase("a", None)
    if upto == 5:
        return kb, kb.finalize([kb.ctx_st_done])
    ag_c = kb.collective("AllGather", [kb.ctx_stack.ap().opt()],
                         [kb.ag_ctx_out.ap().opt()],
                         waits=((kb.sem("s_ctx_st"), kb.sem("s_ctx_st").count),))
    kb.sel_build(ag_c)
    if upto == 6:
        return kb, kb.finalize([kb.sel_done])
    x0_done = kb.oproj_phase("a", kb.x_in, kb.xr[0], "s_xr0_st")
    if upto == 7:
        return kb, kb.finalize([x0_done])

    x1_done = kb.ffn_phase(kb.xr[0].ap(), kb.xr[1], "s_xr1_st",
                           x_guard=(x0_done,), xnf_guard=())
    if upto == 8:
        return kb, kb.finalize([x1_done])

    sagst = kb.sem("s_agxn_st")
    kb.pre_wait("pool", (ag_w,))
    sxn = kb.sem("rs_xnslab")
    kb.ln_sweep(kb.xr[1].ap(), kb.ag_xn_in, sagst, guard=(x1_done,))
    ag_w2 = kb.collective("AllGather", [kb.ag_xn_in.ap().opt()],
                          [kb.ag_xn_out.ap().opt()],
                          waits=((sagst, sagst.count), (sxn, sxn.count)))
    kb.qkv_phase("b", ag_w2)
    kb.attn_phase("b", ag_c)
    scs = kb.sem("rs_cslab")
    ag_c2 = kb.collective("AllGather", [kb.ctx_stack.ap().opt()],
                          [kb.ag_ctx_out.ap().opt()],
                          waits=((kb.sem("s_ctx_st"), kb.sem("s_ctx_st").count),
                                 (scs, scs.count)))
    kb.sel_build(ag_c2)
    x2_done = kb.oproj_phase("b", kb.xr[1].ap(), kb.xr[2], "s_xr2_st")

    sxn2 = kb.sem("rs_xnslab")
    kb.pre_wait("pool", ((sxn2, sxn2.count),))
    y_done = kb.ffn_phase(kb.xr[2].ap(), kb.y_out.tensor, "s_y_st",
                          x_guard=(x2_done,), xnf_guard=())

    nc = kb.finalize([y_done])
    return kb, nc


def kernel(**inputs):
    consts = host_constants()
    kb, nc = build(debug=False)

    x = np.ascontiguousarray(np.asarray(inputs["L0_attn_a_hidden_states"],
                                        dtype=np.float32))
    in_maps = []
    for c in range(8):
        p, r = c // 2, c % 2
        m = {
            "x": np.ascontiguousarray(x[p, r * HALF:(r + 1) * HALF, :]),
            "wfc": np.ascontiguousarray(np.asarray(inputs["L0_ffn_W_fc"], np.float32)),
            "wproj": np.ascontiguousarray(np.asarray(inputs["L0_ffn_W_proj"], np.float32)),
            "qnw": np.ascontiguousarray(
                np.asarray(inputs["L0_attn_b_q_norm_w"], np.float32).reshape(1, 128)),
            "knw": np.ascontiguousarray(
                np.asarray(inputs["L0_attn_b_k_norm_w"], np.float32).reshape(1, 128)),
            "selA": (np.eye(128, dtype=np.float32) * (1.0 if r == 0 else 0.0)).astype(NPBF),
            "selB": (np.eye(128, dtype=np.float32) * (1.0 if r == 1 else 0.0)).astype(NPBF),
        }
        for lay, pref in (("a", "L0_attn_a"), ("b", "L0_attn_b")):
            for w, nm in (("q", "Wq"), ("k", "Wk"), ("v", "Wv")):
                W = np.asarray(inputs[f"{pref}_{nm}"], np.float32)
                m[f"w{w}{lay}T"] = np.ascontiguousarray(
                    W.T[:, r * NHL * 128:(r + 1) * NHL * 128])
            Wo = np.asarray(inputs[f"{pref}_Wo"], np.float32)
            m[f"wo{lay}T"] = np.ascontiguousarray(Wo.T)
        m.update(consts)
        in_maps.append(m)

    res = run_bass_kernel_spmd(nc, in_maps, core_ids=list(range(8)))
    out = np.empty((NSEQ, S, HID), np.float32)
    for c in range(8):
        p, r = c // 2, c % 2
        out[p, r * HALF:(r + 1) * HALF, :] = res.results[c]["y"]
    return out


# version 08 (final)
# speedup vs baseline: 1.0549x; 1.0549x over previous
import math
from contextlib import ExitStack

import numpy as np
import ml_dtypes
import concourse.bass as bass
import concourse.mybir as mybir
from concourse.bass_utils import run_bass_kernel_spmd

F32 = mybir.dt.float32
BF16 = mybir.dt.bfloat16
AF = mybir.ActivationFunctionType
ALU = mybir.AluOpType
AX = mybir.AxisListType
NPBF = ml_dtypes.bfloat16

NSEQ, S, HID = 4, 4096, 2048
NH, D = 16, 128
DFF = 8192
NCF = HID // 128
HALF = S // 2
NT_OWN = HALF // 128
NT_ALL = S // 128
NHL = NH // 2
SCALE_A = 1.0 / math.sqrt(128.0)
SCALE_B = 128.0 ** -0.5
RMS_EPS_B = 1e-6
FFN_EPS = float(np.finfo(np.float32).eps)
LN_EPS = 1e-5

ENGS = ("pe", "dve", "act", "pool", "sp")


class Sem:
    __slots__ = ("h", "name", "count")

    def __init__(self, handle, name):
        self.h = handle
        self.name = name
        self.count = 0


class Emitter:

    def __init__(self, chain_sems):
        self.streams = {e: [] for e in ENGS}
        self.chain_sems = chain_sems
        self._last = {e: None for e in ENGS}

    def emit(self, eng, fn=None, waits=(), inc=None, inc_amt=1):
        waits = tuple(waits)
        chain = fn is not None and eng in ("dve", "act") and inc_amt == 1
        if chain:
            if self._last[eng] is not None:
                waits = waits + (self._last[eng],)
            if inc is None:
                inc = self.chain_sems[eng]
        self.streams[eng].append((fn, waits, inc, inc_amt))
        if inc is not None:
            inc.count += inc_amt
            if chain:
                self._last[eng] = (inc, inc.count)

    def last(self, eng):
        return self._last[eng]

    def replay(self, eng_name, engobj):
        for fn, waits, inc, inc_amt in self.streams[eng_name]:
            for s, v in waits:
                engobj.wait_ge(s.h, v)
            ins = fn(engobj) if fn is not None else None
            if inc is not None:
                if ins is None:
                    ins = engobj.nop()
                ins.then_inc(inc.h, inc_amt)


class Ring:

    def __init__(self, n):
        self.n = n
        self.fw = []
        self.rw = []

    @property
    def slot(self):
        return len(self.fw) % self.n

    def prod_waits(self):
        i = len(self.fw)
        if i >= self.n:
            return tuple(self.rw[i - self.n])
        return ()

    def filled(self, waits):
        self.fw.append(tuple(waits))

    def cons_waits(self):
        return tuple(self.fw[len(self.rw)])

    def released(self, waits):
        self.rw.append(tuple(waits))


class Bank:
    def __init__(self, tensor, sem):
        self.t = tensor
        self.sem = sem

    def waits(self):
        return ((self.sem, self.sem.count),)


def attn_tiles(lay, g):
    out = []
    if lay == "a":
        for kt in range(4 * g):
            out.append((kt, 0, 512, None))
    else:
        for op_ in range(4):
            kt = 4 * g - 8 + op_
            if kt >= 0:
                out.append((kt, 0, (op_ + 1) * 128, ("w", op_ * 128)))
        for kt in range(max(0, 4 * g - 4), 4 * g):
            out.append((kt, 0, 512, None))
    for o in range(4):
        out.append((4 * g + o, o * 128, 512, ("c", o * 128)))
    return out


def host_constants():
    j = np.arange(0, D, 2, dtype=np.float32) / np.float32(D)
    inv = 1.0 / (10000.0 ** j.astype(np.float64))
    t32 = np.arange(S, dtype=np.float32)
    ang = (t32[:, None] * inv.astype(np.float32)[None, :]).astype(np.float32)
    cos = np.cos(ang.astype(np.float64)).astype(np.float32)
    sin = np.sin(ang.astype(np.float64)).astype(np.float32)
    ropec = np.ascontiguousarray(
        cos.reshape(NT_ALL, 128, 64).transpose(1, 0, 2).reshape(128, NT_ALL * 64))
    ropes = np.ascontiguousarray(
        sin.reshape(NT_ALL, 128, 64).transpose(1, 0, 2).reshape(128, NT_ALL * 64))
    kk = np.arange(128)[:, None]
    qq = np.arange(128)[None, :]
    maskc = np.where(qq < kk, np.float32(-1e30), np.float32(0.0))
    maskw = np.where(kk <= qq, np.float32(-1e30), np.float32(0.0))
    ident = np.eye(128, dtype=np.float32)
    return {
        "ropec": ropec, "ropes": ropes,
        "maskc": np.ascontiguousarray(maskc), "maskw": np.ascontiguousarray(maskw),
        "identb": ident.astype(NPBF), "identf": ident,
        "onesrow": np.ones((1, 128), np.float32),
    }


class KB:
    def __init__(self, debug=False):
        self.debug = debug
        self.nc = bass.Bass("TRN2", target_bir_lowering=False)
        self.ctx = ExitStack()
        self.sems_by_name = {}
        self.rings = {}
        self.em = Emitter({"dve": self.sem("chain_dve"), "act": self.sem("chain_act")})
        self.db_free = ()
        self.wconv_done = {}
        self.arena_free = None
        dt = self.nc.dram_tensor
        dbgkind = "Internal"

        self.x_in = dt("x", [HALF, HID], F32, kind="ExternalInput").ap()
        self.y_out = dt("y", [HALF, HID], F32, kind="ExternalOutput").ap()
        self.wT = {}
        for lay in ("a", "b"):
            for w in ("q", "k", "v"):
                self.wT[(lay, w)] = dt(f"w{w}{lay}T", [HID, NHL * 128], F32,
                                       kind="ExternalInput").ap()
            self.wT[(lay, "o")] = dt(f"wo{lay}T", [HID, HID], F32,
                                     kind="ExternalInput").ap()
        self.wfc = dt("wfc", [DFF, HID], F32, kind="ExternalInput").ap()
        self.wproj = dt("wproj", [HID, DFF], F32, kind="ExternalInput").ap()
        self.qnw_in = dt("qnw", [1, 128], F32, kind="ExternalInput").ap()
        self.knw_in = dt("knw", [1, 128], F32, kind="ExternalInput").ap()
        self.ropec_in = dt("ropec", [128, NT_ALL * 64], F32, kind="ExternalInput").ap()
        self.ropes_in = dt("ropes", [128, NT_ALL * 64], F32, kind="ExternalInput").ap()
        self.maskc_in = dt("maskc", [128, 128], F32, kind="ExternalInput").ap()
        self.maskw_in = dt("maskw", [128, 128], F32, kind="ExternalInput").ap()
        self.identb_in = dt("identb", [128, 128], BF16, kind="ExternalInput").ap()
        self.identf_in = dt("identf", [128, 128], F32, kind="ExternalInput").ap()
        self.onesrow_in = dt("onesrow", [1, 128], F32, kind="ExternalInput").ap()
        self.selA_in = dt("selA", [128, 128], BF16, kind="ExternalInput").ap()
        self.selB_in = dt("selB", [128, 128], BF16, kind="ExternalInput").ap()

        self.xr = [dt(f"xr{i}", [HALF, HID], F32, kind=dbgkind) for i in range(3)]
        self.ag_xn_in = dt("ag_xn_in", [HID, HALF], BF16)
        self.ag_xn_out = dt("ag_xn_out", [2 * HID, HALF], BF16)
        self.qT_d = dt("qT_d", [NHL * 128, S], BF16, kind=dbgkind)
        self.kT_d = dt("kT_d", [NHL * 128, S], BF16, kind=dbgkind)
        self.v_d = dt("v_d", [S, NHL * 128], BF16, kind=dbgkind)
        self.ctx_stack = dt("ctx_stack", [HID, HALF], BF16)
        self.ag_ctx_out = dt("ag_ctx_out", [2 * HID, HALF], BF16)
        self.ctx_sel_d = dt("ctx_sel_d", [HID, HALF], BF16, kind=dbgkind)
        self.qfc_d = dt("qfc_d", [DFF, HID], BF16, kind=dbgkind)
        self.qproj_d = dt("qproj_d", [HID, DFF], BF16, kind=dbgkind)
        self.xnf_d = dt("xnf_d", [HID, HALF], BF16)
        self.qfcT_d = dt("qfcT_d", [HID, DFF], BF16)
        self.qprojT_d = dt("qprojT_d", [DFF, HID], BF16)
        self.w_bf = {}
        for lay in ("a", "b"):
            for w in ("q", "k", "v"):
                self.w_bf[(lay, w)] = dt(f"wbf_{w}{lay}", [HID, NHL * 128], BF16)
            self.w_bf[(lay, "o")] = dt(f"wbf_o{lay}", [HID, HID], BF16)

        self._alloc_onchip()

    def sem(self, name):
        if name not in self.sems_by_name:
            h = self.ctx.enter_context(self.nc.semaphore(name))
            self.sems_by_name[name] = Sem(h, name)
        return self.sems_by_name[name]

    def ring(self, name, n):
        if name not in self.rings:
            self.rings[name] = Ring(n)
        r = self.rings[name]
        assert r.n == n
        return r

    def sbuf(self, name, shape, dtype):
        return self.ctx.enter_context(self.nc.sbuf_tensor(name, shape, dtype))

    def _alloc_onchip(self):
        sb = self.sbuf
        self.ropec = sb("ropec_s", [128, NT_ALL * 64], F32)
        self.ropes = sb("ropes_s", [128, NT_ALL * 64], F32)
        self.maskc = sb("maskc_s", [128, 128], F32)
        self.maskw = sb("maskw_s", [128, 128], F32)
        self.identb = sb("identb_s", [128, 128], BF16)
        self.identf = sb("identf_s", [128, 128], F32)
        self.onesrow = sb("onesrow_s", [1, 128], F32)
        self.ones_bf = sb("ones_bf", [128, 1], BF16)
        self.selA = sb("selA_s", [128, 128], BF16)
        self.selB = sb("selB_s", [128, 128], BF16)
        self.wrep_q = sb("wrep_q", [128, 128], F32)
        self.wrep_k = sb("wrep_k", [128, 128], F32)
        self.nrm_row = sb("nrm_row", [1, 128], F32)
        self.negB = {lay: sb(f"negB_{lay}", [128, NHL], F32) for lay in ("a", "b")}
        self.nq_acc = sb("nq_acc", [128, NHL], F32)
        self.nk_acc = sb("nk_acc", [128, NHL], F32)
        self.bst = sb("bst", [128, 8], F32)

        self.xt = sb("xt", [128, 2 * HID], F32)
        self.lnst = sb("lnst", [128, 8], F32)
        self.xn_tm = sb("xn_tm", [128, 2 * HID], BF16)
        self.tpo = sb("tpo", [128, 2 * 512], BF16)

        self.xnslab = sb("xnslab", [128, 2 * NCF * 512], BF16)
        self.wslab = sb("wslab", [128, 2 * NCF * 512], BF16)
        self.rtmp = sb("rtmp", [128, 1024], F32)
        self.qkv_tm = sb("qkv_tm", [128, 2 * 512], BF16)
        self.sqt = sb("sqt", [128, 2 * 512], BF16)
        self.r4t = self.sqt.ap()
        self.nrm4 = sb("nrm4", [128, 2 * 4], F32)
        self.rDs = sb("rDs", [1, 2 * 512], F32)
        self.bcsb = sb("bcsb", [128, 512], F32)
        self.rlt = sb("rlt", [128, 2 * 512], BF16)
        self.y2t = self.rlt.ap().bitcast(F32)
        self.xnew = sb("xnew", [128, 2048], F32)
        self.xsq = self.xnew.ap()[:, 0:HID]
        self.rsrow = sb("rsrow", [1, 512], F32)
        self.rstmp = sb("rstmp", [1, 512], F32)
        self.qsum = sb("qsum", [128, HID // 64], F32)
        self.qscl = sb("qscl", [128, HID // 64], BF16)
        self.qrec = sb("qrec", [128, HID // 64], F32)

        self.arena = sb("arena", [128, 32768], BF16)
        ar = self.arena.ap()
        arf = ar.bitcast(F32)
        self.qw_in = arf[:, 0:4096]
        self.qwb = ar[:, 8192:10240]
        self.qabs = ar[:, 10240:12288]
        self.qhat = ar[:, 12288:14336]
        self.qg1 = ar[:, 14336:16384]
        self.qout = ar[:, 16384:20480]
        self.qwf = arf[:, 10240:12288]
        self.ternf = arf[:, 12288:14336]
        self.qsclf = sb("qsclf", [128, HID // 64], F32)
        self.kTs = ar[:, 0:8192]
        self.vs = ar[:, 8192:16384]
        self.ptr = ar[:, 16384:18432]
        self.qTs = ar[:, 18432:19456]
        self.ctile = ar[:, 19456:20480]
        self.cslab = ar[:, 20480:28672]
        self.r2buf = ar

        self.banks = []
        for i in range(6):
            t = self.ctx.enter_context(self.nc.psum_tensor(f"pb{i}", [128, 512], F32))
            self.banks.append(Bank(t, self.sem(f"pbs{i}")))
        tt_ = self.ctx.enter_context(self.nc.psum_tensor("ptb", [128, 512], F32))
        self.tbank = Bank(tt_, self.sem("ptbs"))
        dd_ = self.ctx.enter_context(self.nc.psum_tensor("pdb", [1, 512], F32))
        self.dbank = Bank(dd_, self.sem("pdbs"))

    def dma(self, eng, out, in_, sem, waits=(), echo=False, echo_pair=None, **kw):
        def fn(e, out=out, in_=in_, kw=kw):
            return e.dma_start(out=out, in_=in_, **kw)
        self.em.emit(eng, fn, waits=waits, inc=sem, inc_amt=16)
        if echo_pair is not None:
            o2, i2 = echo_pair
            def fn2(e, o2=o2, i2=i2, kw=kw):
                return e.dma_start(out=o2, in_=i2, **kw)
            self.em.emit(eng, fn2, waits=(), inc=sem, inc_amt=16)
        for _ in range(int(echo)):
            self.em.emit(eng, fn, waits=(), inc=sem, inc_amt=16)

    def pre_wait(self, eng, waits):
        self.em.emit(eng, None, waits=waits)

    def barrier(self):
        s = self.sem("s_phase")
        base = s.count
        for e in ENGS:
            self.em.emit(e, None, waits=(), inc=s, inc_amt=1)
        for e in ENGS:
            self.em.emit(e, None, waits=((s, base + 5),))

    def p0_consts(self):
        em, s = self.em, self.sem("s_const")
        for dst, src in ((self.ropec, self.ropec_in), (self.ropes, self.ropes_in),
                         (self.maskc, self.maskc_in), (self.maskw, self.maskw_in),
                         (self.identb, self.identb_in), (self.identf, self.identf_in),
                         (self.onesrow, self.onesrow_in), (self.selA, self.selA_in),
                         (self.selB, self.selB_in)):
            self.dma("pool", dst[:, :], src, s)
        self.dma("pool", self.nrm_row[:, :], self.qnw_in, s)
        w = ((s, s.count),)
        sm = self.sem("s_misc")
        em.emit("dve", lambda e: e.memset(self.ones_bf[:, :], 1.0), waits=w, inc=sm)
        em.emit("dve", lambda e: e.tensor_scalar_mul(
            self.nrm_row[:, :], self.nrm_row[:, :], SCALE_B), inc=sm)
        bk = self.banks[0]
        em.emit("pe", lambda e, bk=bk: e.matmul(bk.t[:, 0:128], self.onesrow[:, :],
                                                self.nrm_row[:, :], start=True, stop=True),
                waits=((sm, sm.count),) + bk.waits(), inc=bk.sem)
        em.emit("dve", lambda e, bk=bk: e.tensor_copy(self.wrep_q[:, :], bk.t[:, 0:128]),
                waits=bk.waits(), inc=bk.sem)
        self.dma("pool", self.nrm_row[:, :], self.knw_in, s, waits=bk.waits())
        bk2 = self.banks[1]
        em.emit("pe", lambda e, bk2=bk2: e.matmul(bk2.t[:, 0:128], self.onesrow[:, :],
                                                  self.nrm_row[:, :], start=True, stop=True),
                waits=((s, s.count),) + bk2.waits(), inc=bk2.sem)
        em.emit("dve", lambda e, bk2=bk2: e.tensor_copy(self.wrep_k[:, :], bk2.t[:, 0:128]),
                waits=bk2.waits(), inc=bk2.sem)

    def _convert(self, lay):
        em = self.em
        sst = self.sem("s_qw_st")
        in_ring = self.ring("qw_in", 2)
        out_ring = self.ring("qout", 2)
        sq = self.sem("s_qwdma")
        for part in ("q", "k", "v", "o"):
            wsrc = self.wT[(lay, part)]
            wdst = self.w_bf[(lay, part)].ap()
            cols = wsrc.shape[1]
            for r in range(HID // 128):
                for cs in range(cols // HID if cols > HID else 1):
                    cw = min(cols, HID)
                    src_ap = wsrc[r * 128:(r + 1) * 128, cs * HID:cs * HID + cw]
                    dst_ap = wdst[r * 128:(r + 1) * 128, cs * HID:cs * HID + cw]
                    sl = in_ring.slot
                    self.dma("sp", self.qw_in[:, sl * HID:sl * HID + cw], src_ap,
                             sq, waits=in_ring.prod_waits())
                    in_ring.filled(((sq, sq.count),))
                    w = in_ring.cons_waits()
                    osl = out_ring.slot
                    oq = self.qout[:, osl * HID:osl * HID + cw]
                    em.emit("dve", lambda e, oq=oq, sl=sl, cw=cw: e.tensor_copy(
                        oq, self.qw_in[:, sl * HID:sl * HID + cw]),
                        waits=w + out_ring.prod_waits())
                    cp_done = em.last("dve")
                    in_ring.released((cp_done,))
                    out_ring.filled((cp_done,))
                    self.dma("pool", dst_ap, oq, sst, waits=out_ring.cons_waits())
                    out_ring.released(((sst, sst.count),))
        self.wconv_done[lay] = (sst, sst.count)

    def p1_quant(self):
        em = self.em
        sst = self.sem("s_qw_st")
        NG = HID // 64
        in_ring = self.ring("qw_in", 2)
        out_ring = self.ring("qout", 2)
        sq = self.sem("s_qwdma")

        def section(src_ap, dst_ap):
            sl = in_ring.slot
            self.dma("sp", self.qw_in[:, sl * HID:(sl + 1) * HID], src_ap, sq,
                     waits=in_ring.prod_waits())
            in_ring.filled(((sq, sq.count),))
            w = in_ring.cons_waits()
            inq = self.qw_in[:, sl * HID:(sl + 1) * HID]
            dve_prev = (em.last("dve"),) if em.last("dve") else ()
            em.emit("act", lambda e, inq=inq: e.activation(self.qwb, inq, AF.Copy),
                    waits=w)
            in_ring.released((em.last("act"),))
            em.emit("act", lambda e: e.activation(self.qabs, self.qwb, AF.Abs),
                    waits=dve_prev)
            abs_done = em.last("act")
            em.emit("act", lambda e: e.activation(self.qwf, self.qwb, AF.Copy),
                    waits=dve_prev)
            qwf_done = em.last("act")
            em.emit("dve", lambda e: e.reduce_sum(
                self.qsum.ap(), self.qabs.rearrange("p (g e) -> p g e", e=64),
                axis=AX.X), waits=(abs_done,))
            em.emit("dve", lambda e: e.tensor_scalar(
                self.qscl.ap(), self.qsum.ap(), 1.0 / 64.0, 1e-8, ALU.mult, ALU.max))
            em.emit("dve", lambda e: e.tensor_copy(self.qsclf.ap(), self.qscl.ap()))
            em.emit("dve", lambda e: e.reciprocal(self.qrec.ap(), self.qsclf.ap()))
            em.emit("dve", lambda e: e.tensor_tensor(
                self.qhat.rearrange("p (g e) -> p g e", e=64),
                self.qwf.rearrange("p (g e) -> p g e", e=64),
                self.qrec.ap().unsqueeze(2).broadcast_to([128, NG, 64]), ALU.mult),
                waits=(qwf_done,))
            em.emit("dve", lambda e: e.tensor_single_scalar(
                self.qg1, self.qhat, 0.5, ALU.is_gt))
            em.emit("dve", lambda e: e.tensor_single_scalar(
                self.qabs, self.qhat, -0.5, ALU.is_lt))
            em.emit("dve", lambda e: e.tensor_sub(self.ternf, self.qg1, self.qabs))
            osl = out_ring.slot
            oq = self.qout[:, osl * HID:(osl + 1) * HID]
            em.emit("dve", lambda e, oq=oq: e.tensor_tensor(
                oq.rearrange("p (g e) -> p g e", e=64),
                self.ternf.rearrange("p (g e) -> p g e", e=64),
                self.qsclf.ap().unsqueeze(2).broadcast_to([128, NG, 64]), ALU.mult),
                waits=out_ring.prod_waits())
            out_ring.filled((em.last("dve"),))
            self.dma("pool", dst_ap, oq, sst, waits=out_ring.cons_waits())
            out_ring.released(((sst, sst.count),))

        self._convert("b")

        qfc = self.qfc_d.ap()
        qpj = self.qproj_d.ap()
        for r in range(DFF // 128):
            section(self.wfc[r * 128:(r + 1) * 128, :],
                    qfc[r * 128:(r + 1) * 128, :])
        for r in range(HID // 128):
            for cs in range(DFF // HID):
                section(self.wproj[r * 128:(r + 1) * 128, cs * HID:(cs + 1) * HID],
                        qpj[r * 128:(r + 1) * 128, cs * HID:(cs + 1) * HID])
        w_ring = self.ring("wslab", 2)
        swd = self.sem("s_wslab_dma")
        sstT = self.sem("s_qwT_st")
        pe_relq = self.sem("s_pe_rel")

        def tblock(src_hbm, dst_hbm, R0, C0, dst_stride):
            wsl = w_ring.slot
            self.dma("sp", self.wslab[:, wsl * NCF * 512:wsl * NCF * 512 + 512],
                     src_hbm[R0:R0 + 512, C0:C0 + 128], swd, transpose=True,
                     waits=w_ring.prod_waits() + (((sst, sst.count),) if True else ()),
                     echo=True)
            w_ring.filled(((swd, swd.count),))
            out = bass.AP(dst_hbm, C0 * dst_stride + R0, [[dst_stride, 128], [1, 512]])
            self.dma("pool", out, self.wslab[:, wsl * NCF * 512:wsl * NCF * 512 + 512],
                     sstT, waits=w_ring.cons_waits())
            w_ring.released(((sstT, sstT.count),))

        qfcA = self.qfc_d
        qpjA = self.qproj_d
        for ob in range(DFF // 512):
            for kc in range(NCF):
                tblock(qfcA.ap(), self.qfcT_d, ob * 512, kc * 128, DFF)
        for ob in range(HID // 512):
            for kc in range(DFF // 128):
                tblock(qpjA.ap(), self.qprojT_d, ob * 512, kc * 128, HID)
        self.quant_done = (sstT, sstT.count)

    def ln_sweep(self, x_src, dst_handle, store_sem, guard=()):
        em = self.em
        stp = store_sem
        x_ring = self.ring("xt", 2)
        xn_ring = self.ring("xn_tm", 2)
        tpo_ring = self.ring("tpo", 2)
        sx = self.sem("s_lnx")
        tb = self.tbank
        tbv = tb.t.ap().bitcast(BF16)

        for ts in range(NT_OWN):
            sl = x_ring.slot
            self.dma("sp", self.xt[:, sl * HID:(sl + 1) * HID],
                     x_src[ts * 128:(ts + 1) * 128, :], sx,
                     waits=x_ring.prod_waits() + (guard if ts == 0 else ()))
            x_ring.filled(((sx, sx.count),))
            w = x_ring.cons_waits()
            xt = self.xt[:, sl * HID:(sl + 1) * HID]
            em.emit("act", lambda e, xt=xt: e.activation(self.xsq, xt, AF.Square),
                    waits=w + getattr(self, "_xsq_guard", ()))
            sq_done = em.last("act")
            em.emit("dve", lambda e, xt=xt: e.reduce_sum(self.lnst[:, 0:1], xt,
                                                         axis=AX.X), waits=w)
            em.emit("dve", lambda e: e.reduce_sum(self.lnst[:, 1:2], self.xsq,
                                                  axis=AX.X), waits=(sq_done,))
            self._xsq_guard = (em.last("dve"),)
            em.emit("dve", lambda e: e.tensor_scalar_mul(self.lnst[:, 2:3],
                                                         self.lnst[:, 0:1], 1.0 / HID))
            em.emit("dve", lambda e: e.tensor_scalar_mul(self.lnst[:, 3:4],
                                                         self.lnst[:, 1:2], 1.0 / HID))
            em.emit("dve", lambda e: e.tensor_mul(self.lnst[:, 4:5], self.lnst[:, 2:3],
                                                  self.lnst[:, 2:3]))
            em.emit("dve", lambda e: e.tensor_sub(self.lnst[:, 5:6], self.lnst[:, 3:4],
                                                  self.lnst[:, 4:5]))
            em.emit("dve", lambda e: e.tensor_scalar_add(self.lnst[:, 6:7],
                                                         self.lnst[:, 5:6], LN_EPS))
            em.emit("act", lambda e: e.activation(self.lnst[:, 7:8], self.lnst[:, 6:7],
                                                  AF.Sqrt), waits=(em.last("dve"),))
            em.emit("dve", lambda e: e.reciprocal(self.lnst[:, 6:7], self.lnst[:, 7:8]),
                    waits=(em.last("act"),))
            xsl = xn_ring.slot
            xn = self.xn_tm[:, xsl * HID:(xsl + 1) * HID]
            em.emit("dve", lambda e, xt=xt, xn=xn: e.tensor_scalar(
                xn, xt, self.lnst[:, 2:3], self.lnst[:, 6:7], ALU.subtract, ALU.mult),
                waits=xn_ring.prod_waits())
            xn_done = em.last("dve")
            x_ring.released((xn_done,))
            xn_ring.filled((xn_done,))
            xnw = xn_ring.cons_waits()
            pe_rel = self.sem("s_pe_rel")
            for grp in range(4):
                for i in range(4):
                    kc = grp * 4 + i
                    ww = (xnw + tb.waits()) if i == 0 else ()
                    em.emit("pe", lambda e, i=i, kc=kc, xn=xn, tbv=tbv: e.transpose(
                        tbv[:, i * 128:(i + 1) * 128], xn[:, kc * 128:(kc + 1) * 128],
                        self.identb[:, :]), waits=ww,
                        inc=tb.sem if i == 3 else None)
                tsl = tpo_ring.slot
                em.emit("dve", lambda e, tsl=tsl, tbv=tbv: e.tensor_copy(
                    self.tpo[:, tsl * 512:(tsl + 1) * 512], tbv[:, 0:512]),
                    waits=tb.waits() + tpo_ring.prod_waits(), inc=tb.sem)
                tpo_ring.filled((em.last("dve"),))
                if grp == 3:
                    xn_ring.released((em.last("dve"),))
                dst_ap = bass.AP(dst_handle, (grp * 4 * 128) * HALF + ts * 128,
                                 [[HALF, 128], [128 * HALF, 4], [1, 128]])
                src_ap = self.tpo[:, tsl * 512:(tsl + 1) * 512].rearrange(
                    "p (c e) -> p c e", e=128)
                self.dma("pool", dst_ap, src_ap, stp, waits=tpo_ring.cons_waits())
                tpo_ring.released(((stp, stp.count),))

    def collective(self, kind, ins, outs, waits):
        cc = self.sem("cc")

        def fn(e, ins=ins, outs=outs, kind=kind):
            return e.collective_compute(
                kind, mybir.AluOpType.bypass,
                replica_groups=[[0, 1], [2, 3], [4, 5], [6, 7]],
                ins=ins, outs=outs)
        self.em.emit("pool", fn, waits=waits, inc=cc, inc_amt=1)
        return (cc, cc.count)

    def qkv_phase(self, lay, ag_wait):
        em = self.em
        sst = self.sem("s_qkv_st")
        sxd = self.sem("s_xnslab_dma")
        swd = self.sem("s_wslab_dma")
        xn_ring = self.ring("xnslab", 2)
        w_ring = self.ring("wslab", 2)
        pool = [self.banks[i] for i in (0, 1, 2, 3)]
        pidx = 0
        pe_rel = self.sem("s_pe_rel")

        skvd = self.sem("s_kv_dma")
        sqd = self.sem("s_qT_dma")
        self.pre_wait("pool", ((skvd, skvd.count), (sqd, sqd.count)))

        self._sq_guard = ()
        tbs = self.tbank.sem
        em.emit("dve", lambda e: e.memset(self.nq_acc[:, :], 0.0),
                waits=((tbs, tbs.count),))
        em.emit("dve", lambda e: e.memset(self.nk_acc[:, :], 0.0))

        parts = [("q", 0), ("q", 1), ("k", 0), ("k", 1), ("v", 0), ("v", 1)]
        pend = []
        for tt in range(8):
            blk = tt // 4
            tloc = (tt % 4) * 512
            xsl = xn_ring.slot
            for kc in range(NCF):
                srcap = bass.AP(self.ag_xn_out, (blk * HID + kc * 128) * HALF + tloc,
                                [[HALF, 128], [1, 512]])
                self.dma("sp", self.xnslab[:, xsl * NCF * 512 + kc * 512:
                                           xsl * NCF * 512 + (kc + 1) * 512],
                         srcap, sxd,
                         waits=(xn_ring.prod_waits() + (ag_wait,)) if kc == 0 else (),
                         echo=(kc == NCF - 1))
            xn_ring.filled(((sxd, sxd.count),))
            xnw = xn_ring.cons_waits()
            xbase = xsl * NCF * 512

            for pi, (part, o5) in enumerate(parts):
                wsl = w_ring.slot
                wsrc = self.w_bf[(lay, part)].ap()
                weng = "sp"
                for kc in range(NCF):
                    self.dma(weng,
                             self.wslab[:, wsl * NCF * 512 + kc * 512:
                                        wsl * NCF * 512 + (kc + 1) * 512],
                             wsrc[kc * 128:(kc + 1) * 128, o5 * 512:(o5 + 1) * 512],
                             swd, waits=(w_ring.prod_waits() +
                                         ((self.wconv_done[lay],)
                                          if (tt == 0 and pi <= 1) else ()))
                             if kc == 0 else (), echo=(kc == NCF - 1))
                w_ring.filled(((swd, swd.count),))
                wbase = wsl * NCF * 512
                www = w_ring.cons_waits()

                for tsu in range(4):
                    bk = pool[pidx % 4]
                    pidx += 1
                    for kc in range(NCF):
                        ww = (bk.waits() + xnw + www) if kc == 0 else ()
                        em.emit("pe", lambda e, bk=bk, kc=kc, xbase=xbase,
                                wbase=wbase, tsu=tsu: e.matmul(
                                    bk.t[:, :],
                                    self.xnslab[:, xbase + kc * 512 + tsu * 128:
                                                xbase + kc * 512 + (tsu + 1) * 128],
                                    self.wslab[:, wbase + kc * 512:wbase + (kc + 1) * 512],
                                    start=(kc == 0), stop=(kc == NCF - 1)),
                                waits=ww, inc=bk.sem if kc == NCF - 1 else None)
                    mm_done = (bk.sem, bk.sem.count)
                    pend.append((part, o5, tt * 4 + tsu, bk))
                    if len(pend) >= 2:
                        p_, o_, g_, b_ = pend.pop(0)
                        self._qkv_evac(lay, p_, o_, g_, b_, sst)
                w_ring.released((mm_done,))
            xn_ring.released((mm_done,))
        while pend:
            p_, o_, g_, b_ = pend.pop(0)
            self._qkv_evac(lay, p_, o_, g_, b_, sst)

        self._negb_finalize(lay)
        self.qkv_done = (sst, sst.count)

    def _qkv_evac(self, lay, part, o5, gts, bk, sst):
        em = self.em
        qk_ring = self.ring("qkv_tm", 2)
        tpo_ring = self.ring("tpo", 2)
        pe_rel = self.sem("s_pe_rel")

        if part == "v":
            qsl = qk_ring.slot
            dst = self.qkv_tm[:, qsl * 512:(qsl + 1) * 512]
            em.emit("dve", lambda e, bk=bk, dst=dst: e.tensor_copy(dst, bk.t[:, :]),
                    waits=bk.waits() + qk_ring.prod_waits(), inc=bk.sem)
            qk_ring.filled((em.last("dve"),))
            out = bass.AP(self.v_d, (gts * 128) * (NHL * 128) + o5 * 512,
                          [[NHL * 128, 128], [1, 512]])
            self.dma("pool", out, dst, sst, waits=qk_ring.cons_waits())
            qk_ring.released(((sst, sst.count),))
            return

        t1 = self.rtmp[:, 0:256].rearrange("p (h e) -> p h e", e=64)
        t2 = self.rtmp[:, 256:512].rearrange("p (h e) -> p h e", e=64)
        t3 = self.rtmp[:, 512:1024]
        c_ap = self.ropec[:, gts * 64:(gts + 1) * 64].unsqueeze(1).broadcast_to([128, 4, 64])
        s_ap = self.ropes[:, gts * 64:(gts + 1) * 64].unsqueeze(1).broadcast_to([128, 4, 64])
        bv = bk.t.ap().rearrange("p (h d) -> p h d", d=128)
        bw = bk.waits()

        if lay == "b":
            em.emit("act", lambda e, bk=bk: e.activation(self.sqt[:, 0:512],
                                                         bk.t[:, :], AF.Square),
                    waits=bw + self._sq_guard)
            em.emit("dve", lambda e: e.reduce_sum(
                self.nrm4[:, 0:4],
                self.sqt[:, 0:512].rearrange("p (h d) -> p h d", d=128),
                axis=AX.X), waits=(em.last("act"),))
            self._sq_guard = (em.last("dve"),)
            em.emit("dve", lambda e: e.tensor_scalar(
                self.nrm4[:, 0:4], self.nrm4[:, 0:4], 1.0 / 128.0, RMS_EPS_B,
                ALU.mult, ALU.add))
            em.emit("act", lambda e: e.activation(self.nrm4[:, 0:4], self.nrm4[:, 0:4],
                                                  AF.Sqrt), waits=(em.last("dve"),))
            em.emit("dve", lambda e: e.reciprocal(self.nrm4[:, 0:4], self.nrm4[:, 0:4]),
                    waits=(em.last("act"),))
            em.emit("dve", lambda e, t3=t3: e.tensor_tensor(
                t3.rearrange("p (h d) -> p h d", d=128), bv,
                self.nrm4[:, 0:4].unsqueeze(2).broadcast_to([128, 4, 128]),
                ALU.mult), inc=bk.sem)
            wrep = self.wrep_q if part == "q" else self.wrep_k
            em.emit("dve", lambda e, t3=t3, wrep=wrep: e.tensor_tensor(
                t3.rearrange("p (h d) -> p h d", d=128),
                t3.rearrange("p (h d) -> p h d", d=128),
                wrep.ap().unsqueeze(1).broadcast_to([128, 4, 128]), ALU.mult))
            sv = t3.rearrange("p (h d) -> p h d", d=128)
            brel = None
        else:
            sv = bv
            brel = bk

        src1 = sv[:, :, 0:64]
        src2 = sv[:, :, 64:128]
        qsl = qk_ring.slot
        qtm = self.qkv_tm[:, qsl * 512:(qsl + 1) * 512]
        qv = qtm.rearrange("p (h d) -> p h d", d=128)

        em.emit("dve", lambda e, src1=src1, t1=t1, c_ap=c_ap: e.tensor_tensor(
            t1, src1, c_ap, ALU.mult),
            waits=qk_ring.prod_waits() + (bw if lay == "a" else ()))
        em.emit("dve", lambda e, src2=src2, t2=t2, s_ap=s_ap: e.tensor_tensor(
            t2, src2, s_ap, ALU.mult))
        if lay == "a" and part == "q":
            em.emit("dve", lambda e, t1=t1, t2=t2: e.tensor_sub(t1, t1, t2))
            em.emit("dve", lambda e, qv=qv, t1=t1: e.tensor_scalar_mul(
                qv[:, :, 0:64], t1, SCALE_A))
        else:
            em.emit("dve", lambda e, qv=qv, t1=t1, t2=t2: e.tensor_tensor(
                qv[:, :, 0:64], t1, t2, ALU.subtract))
        em.emit("dve", lambda e, src2=src2, t1=t1, c_ap=c_ap: e.tensor_tensor(
            t1, src2, c_ap, ALU.mult))
        em.emit("dve", lambda e, src1=src1, t2=t2, s_ap=s_ap: e.tensor_tensor(
            t2, src1, s_ap, ALU.mult),
            inc=(brel.sem if brel is not None else None))
        if lay == "a" and part == "q":
            em.emit("dve", lambda e, t1=t1, t2=t2: e.tensor_add(t1, t1, t2))
            em.emit("dve", lambda e, qv=qv, t1=t1: e.tensor_scalar_mul(
                qv[:, :, 64:128], t1, SCALE_A))
        else:
            em.emit("dve", lambda e, qv=qv, t1=t1, t2=t2: e.tensor_tensor(
                qv[:, :, 64:128], t1, t2, ALU.add))
        rope_done = em.last("dve")
        qk_ring.filled((rope_done,))
        qw = qk_ring.cons_waits()

        acc = self.nq_acc if part == "q" else self.nk_acc
        em.emit("act", lambda e, qtm=qtm: e.activation(self.sqt[:, 0:512], qtm,
                                                       AF.Square),
                waits=qw + self._sq_guard)
        sq_done = em.last("act")
        em.emit("dve", lambda e: e.reduce_sum(
            self.nrm4[:, 0:4],
            self.sqt[:, 0:512].rearrange("p (h d) -> p h d", d=128), axis=AX.X),
            waits=(sq_done,))
        self._sq_guard = (em.last("dve"),)
        em.emit("dve", lambda e, acc=acc, o5=o5: e.tensor_max(
            acc[:, o5 * 4:(o5 + 1) * 4], acc[:, o5 * 4:(o5 + 1) * 4],
            self.nrm4[:, 0:4]))

        tb = self.tbank
        tbv = tb.t.ap().bitcast(BF16)
        for i in range(4):
            ww = (qw + tb.waits()) if i == 0 else ()
            em.emit("pe", lambda e, i=i, qtm=qtm, tbv=tbv: e.transpose(
                tbv[:, i * 128:(i + 1) * 128], qtm[:, i * 128:(i + 1) * 128],
                self.identb[:, :]), waits=ww, inc=tb.sem if i == 3 else None)
        qk_ring.released(((tb.sem, tb.sem.count), sq_done))
        tsl = tpo_ring.slot
        em.emit("dve", lambda e, tsl=tsl, tbv=tbv: e.tensor_copy(
            self.tpo[:, tsl * 512:(tsl + 1) * 512], tbv[:, 0:512]),
            waits=tb.waits() + tpo_ring.prod_waits(), inc=tb.sem)
        tpo_ring.filled((em.last("dve"),))
        dstt = self.qT_d if part == "q" else self.kT_d
        out = bass.AP(dstt, (o5 * 4 * 128) * S + gts * 128,
                      [[S, 128], [128 * S, 4], [1, 128]])
        self.dma("pool", out,
                 self.tpo[:, tsl * 512:(tsl + 1) * 512].rearrange("p (c e) -> p c e", e=128),
                 sst, waits=tpo_ring.cons_waits())
        tpo_ring.released(((sst, sst.count),))

    def _negb_finalize(self, lay):
        em = self.em
        tb = self.tbank
        for idx, acc in enumerate((self.nq_acc, self.nk_acc)):
            em.emit("pe", lambda e, acc=acc: e.transpose(
                tb.t[0:NHL, 0:128], acc[:, :], self.identf[:, :]),
                waits=tb.waits() + ((em.last("dve")[0], em.last("dve")[1]),),
                inc=tb.sem)
            em.emit("dve", lambda e, idx=idx: e.reduce_max(
                self.bst[0:NHL, idx:idx + 1], tb.t[0:NHL, 0:128], axis=AX.X),
                waits=tb.waits(), inc=tb.sem)
        em.emit("dve", lambda e: e.tensor_mul(self.bst[0:NHL, 2:3],
                                              self.bst[0:NHL, 0:1],
                                              self.bst[0:NHL, 1:2]))
        em.emit("act", lambda e: e.activation(self.bst[0:NHL, 3:4],
                                              self.bst[0:NHL, 2:3], AF.Sqrt),
                waits=(em.last("dve"),))
        em.emit("dve", lambda e: e.tensor_scalar_mul(
            self.bst[0:NHL, 4:5], self.bst[0:NHL, 3:4], -1.0),
            waits=(em.last("act"),))
        em.emit("pe", lambda e: e.transpose(tb.t[0:1, 0:128], self.bst[:, 4:5],
                                            self.identf[:, :]),
                waits=(em.last("dve"),) + tb.waits(), inc=tb.sem)
        em.emit("dve", lambda e: e.tensor_copy(self.bst[0:1, 0:NHL], tb.t[0:1, 0:NHL]),
                waits=tb.waits(), inc=tb.sem)
        bb = self.banks[4]
        em.emit("pe", lambda e, bb=bb: e.matmul(
            bb.t[:, 0:NHL], self.onesrow[:, :], self.bst[0:1, 0:NHL],
            start=True, stop=True), waits=(em.last("dve"),) + bb.waits(), inc=bb.sem)
        nb = self.negB[lay]
        em.emit("dve", lambda e, bb=bb, nb=nb: e.tensor_copy(nb[:, :], bb.t[:, 0:NHL]),
                waits=bb.waits(), inc=bb.sem)
        self.negb_wait = em.last("dve")

    def attn_phase(self, lay, ctx_prev_ag_wait):
        em = self.em
        scst = self.sem("s_ctx_st")
        skvd = self.sem("s_kv_dma")
        sqd = self.sem("s_qT_dma")
        spt = self.sem("s_pt")
        kv_ring = self.ring("kTs", 2)
        v_ring = self.ring("vs", 2)
        q_ring = self.ring("qTs", 2)
        pt_ring = self.ring("ptr", 4)
        ct_ring = self.ring("ctile", 2)
        rd_ring = self.ring("rDs", 2)
        pool = [self.banks[i] for i in (0, 1, 2)]
        ctxbanks = [self.banks[4], self.banks[5]]
        bcb = self.banks[3]
        db = self.dbank
        qkv_done = self.qkv_done
        negb_w = self.negb_wait

        if ctx_prev_ag_wait is not None:
            self.pre_wait("pool", (ctx_prev_ag_wait,))

        for hl in range(NHL):
            ksl = kv_ring.slot
            extra = ((qkv_done, self.quant_done) +
                     ((self.arena_free,) if self.arena_free else ())
                     if hl == 0 else ())
            self.dma("sp", self.kTs[:, ksl * S:(ksl + 1) * S],
                     self.kT_d.ap()[hl * 128:(hl + 1) * 128, :], skvd,
                     waits=kv_ring.prod_waits() + extra, echo=True)
            kv_ring.filled(((skvd, skvd.count),))
            vsl = v_ring.slot
            vsrc = bass.AP(self.v_d, hl * 128,
                           [[NHL * 128, 128], [128 * NHL * 128, NT_ALL], [1, 128]])
            self.dma("sp", self.vs[:, vsl * S:(vsl + 1) * S].rearrange(
                "p (c e) -> p c e", e=128), vsrc, skvd,
                waits=v_ring.prod_waits(), echo=True)
            v_ring.filled(((skvd, skvd.count),))
            kvw = kv_ring.cons_waits() + v_ring.cons_waits()
            kbase = ksl * S
            vbase = vsl * S

            for g in range(8):
                qsl = q_ring.slot
                self.dma("sp", self.qTs[:, qsl * 512:(qsl + 1) * 512],
                         self.qT_d.ap()[hl * 128:(hl + 1) * 128, g * 512:(g + 1) * 512],
                         sqd, waits=q_ring.prod_waits(), echo=True)
                q_ring.filled(((sqd, sqd.count),))
                qw = q_ring.cons_waits()
                qb = qsl * 512
                tiles = attn_tiles(lay, g)
                n = len(tiles)
                cb = ctxbanks[g % 2]
                exp_t = [None] * n
                stage = []

                pv_last = [None]

                def emit_pv(j, n=n, cb=cb, tiles=tiles, vbase=vbase, exp_t=exp_t,
                            kvw=kvw):
                    kt, qlo, qhi, _m = tiles[j]
                    psl = j % 4
                    em.emit("pe", lambda e, kt=kt, qlo=qlo, qhi=qhi, psl=psl, cb=cb,
                            j=j, n=n, vbase=vbase: e.matmul(
                                cb.t[:, qlo:qhi],
                                self.vs[:, vbase + kt * 128:vbase + (kt + 1) * 128],
                                self.ptr[:, psl * 512 + qlo:psl * 512 + qhi],
                                start=(j == 0), stop=(j == n - 1)),
                            waits=(exp_t[j],) + (cb.waits() if j == 0 else ()),
                            inc=(cb.sem if j == n - 1 else None))
                    if j == n - 1:
                        pv_last[0] = (cb.sem, cb.sem.count)
                    em.emit("pe", lambda e, qlo=qlo, qhi=qhi, psl=psl, j=j, n=n:
                            e.matmul(db.t[0:1, qlo:qhi], self.ones_bf[:, :],
                                     self.ptr[:, psl * 512 + qlo:psl * 512 + qhi],
                                     start=(j == 0), stop=(j == n - 1)),
                            waits=(self.db_free if j == 0 else ()), inc=spt)
                    pt_ring.released(((spt, spt.count),))

                for j, (kt, qlo, qhi, m) in enumerate(tiles):
                    bk = pool[j % 3]
                    ww = bk.waits()
                    if j == 0:
                        ww = ww + kvw + qw + (negb_w,)
                    em.emit("pe", lambda e, bk=bk, kt=kt, qlo=qlo, qhi=qhi,
                            kbase=kbase, qb=qb: e.matmul(
                                bk.t[:, qlo:qhi],
                                self.kTs[:, kbase + kt * 128:kbase + (kt + 1) * 128],
                                self.qTs[:, qb + qlo:qb + qhi],
                                start=True, stop=True), waits=ww, inc=bk.sem)
                    if j == n - 1:
                        q_last = (bk.sem, bk.sem.count)
                    if m is not None:
                        mt, mc = m
                        msk = self.maskc if mt == "c" else self.maskw
                        em.emit("dve", lambda e, bk=bk, mc=mc, msk=msk: e.tensor_tensor(
                            bk.t[:, mc:mc + 128], bk.t[:, mc:mc + 128], msk[:, :],
                            ALU.add), waits=bk.waits(), inc=bk.sem)
                    psl = pt_ring.slot
                    em.emit("act", lambda e, bk=bk, qlo=qlo, qhi=qhi, psl=psl,
                            hl=hl, lay=lay: e.activation(
                                self.ptr[:, psl * 512 + qlo:psl * 512 + qhi],
                                bk.t[:, qlo:qhi], AF.Exp,
                                bias=self.negB[lay][:, hl:hl + 1]),
                            waits=bk.waits() + pt_ring.prod_waits(), inc=bk.sem)
                    exp_t[j] = em.last("act")
                    pt_ring.filled((exp_t[j],))
                    stage.append(j)
                    if len(stage) > 2:
                        emit_pv(stage.pop(0))
                while stage:
                    emit_pv(stage.pop(0))

                rsl = rd_ring.slot
                em.emit("dve", lambda e, rsl=rsl: e.reciprocal(
                    self.rDs[0:1, rsl * 512:(rsl + 1) * 512], db.t[0:1, :]),
                    waits=((spt, spt.count),) + rd_ring.prod_waits())
                recip_done = em.last("dve")
                self.db_free = (recip_done,)
                rd_ring.filled((recip_done,))
                em.emit("pe", lambda e, rsl=rsl, bcb=bcb: e.matmul(
                    bcb.t[:, :], self.onesrow[:, :],
                    self.rDs[0:1, rsl * 512:(rsl + 1) * 512], start=True, stop=True),
                    waits=rd_ring.cons_waits() + bcb.waits(), inc=bcb.sem)
                rd_ring.released(((bcb.sem, bcb.sem.count),))
                em.emit("dve", lambda e, bcb=bcb: e.tensor_copy(self.bcsb[:, :],
                                                                bcb.t[:, :]),
                        waits=bcb.waits(), inc=bcb.sem)
                bc_done = em.last("dve")
                csl = ct_ring.slot
                em.emit("dve", lambda e, csl=csl, cb=cb: e.tensor_tensor(
                    self.ctile[:, csl * 512:(csl + 1) * 512], cb.t[:, :],
                    self.bcsb[:, :], ALU.mult),
                    waits=cb.waits() + (bc_done,) + ct_ring.prod_waits(), inc=cb.sem)
                ct_ring.filled((em.last("dve"),))
                out = bass.AP(self.ctx_stack,
                              ((g // 4) * NHL * 128 + hl * 128) * HALF + (g % 4) * 512,
                              [[HALF, 128], [1, 512]])
                self.dma("pool", out, self.ctile[:, csl * 512:(csl + 1) * 512],
                         scst, waits=ct_ring.cons_waits())
                ct_ring.released(((scst, scst.count),))
                q_ring.released((q_last,))
            kv_ring.released((q_last,))
            v_ring.released((pv_last[0],))
        self.ctx_st_done = (scst, scst.count)

    def sel_build(self, ag_wait):
        em = self.em
        ssel = self.sem("s_sel_st")
        scd = self.sem("s_cs_dma")
        cs_ring = self.ring("cslab", 2)
        tpo_ring = self.ring("tpo", 2)
        pe_rel = self.sem("s_pe_rel")
        pool = [self.banks[i] for i in (0, 1)]
        CSW = 4096
        for ts in range(NT_OWN):
            sl = cs_ring.slot
            first = True
            for cand in range(2):
                for h8 in range(2):
                    srcap = bass.AP(self.ag_ctx_out,
                                    (h8 * HID + cand * NHL * 128) * HALF + ts * 128,
                                    [[HALF, 128], [128 * HALF, 8], [1, 128]])
                    self.dma("sp",
                             self.cslab[:, sl * CSW + cand * 2048 + h8 * 1024:
                                        sl * CSW + cand * 2048 + (h8 + 1) * 1024]
                             .rearrange("p (c e) -> p c e", e=128),
                             srcap, scd,
                             waits=(cs_ring.prod_waits() + (ag_wait,)) if first else (),
                             echo=(cand == 1 and h8 == 1))
                    first = False
            cs_ring.filled(((scd, scd.count),))
            cw = cs_ring.cons_waits()
            sbase = sl * CSW
            for kcg in range(4):
                bk = pool[kcg % 2]
                em.emit("pe", lambda e, bk=bk, kcg=kcg, sbase=sbase: e.matmul(
                    bk.t[:, :], self.selA[:, :],
                    self.cslab[:, sbase + kcg * 512:sbase + (kcg + 1) * 512],
                    start=True, stop=False),
                    waits=bk.waits() + (cw if kcg == 0 else ()))
                em.emit("pe", lambda e, bk=bk, kcg=kcg, sbase=sbase: e.matmul(
                    bk.t[:, :], self.selB[:, :],
                    self.cslab[:, sbase + 2048 + kcg * 512:sbase + 2048 + (kcg + 1) * 512],
                    start=False, stop=True), inc=bk.sem)
                if kcg == 3:
                    cs_ring.released(((bk.sem, bk.sem.count),))
                tsl = tpo_ring.slot
                em.emit("dve", lambda e, bk=bk, tsl=tsl: e.tensor_copy(
                    self.tpo[:, tsl * 512:(tsl + 1) * 512], bk.t[:, :]),
                    waits=bk.waits() + tpo_ring.prod_waits(), inc=bk.sem)
                tpo_ring.filled((em.last("dve"),))
                out = bass.AP(self.ctx_sel_d, (kcg * 4 * 128) * HALF + ts * 128,
                              [[HALF, 128], [128 * HALF, 4], [1, 128]])
                self.dma("pool", out,
                         self.tpo[:, tsl * 512:(tsl + 1) * 512].rearrange(
                             "p (c e) -> p c e", e=128),
                         ssel, waits=tpo_ring.cons_waits())
                tpo_ring.released(((ssel, ssel.count),))
        self.sel_done = (ssel, ssel.count)

    def oproj_phase(self, lay, x_src, x_dst_handle, x_dst_sem_name):
        em = self.em
        sst = self.sem(x_dst_sem_name)
        swd = self.sem("s_wslab_dma")
        scd = self.sem("s_cs_dma")
        sxd = self.sem("s_x_dma")
        wo_ring = self.ring("wslab", 2)
        xi_ring = self.ring("xt", 2)
        xo_ring = self.ring("xnew", 1)
        pool = [self.banks[i] for i in (0, 1, 2, 3)]
        pe_rel = self.sem("s_pe_rel")
        pidx = 0
        wsrc = self.w_bf[(lay, "o")].ap()
        xdst = x_dst_handle

        ctxsb = self.arena.ap()
        sel_done = self.sel_done
        for kc in range(NCF):
            self.dma("sp", ctxsb[:, kc * 2048:(kc + 1) * 2048],
                     self.ctx_sel_d.ap()[kc * 128:(kc + 1) * 128, :], scd,
                     waits=((sel_done, self.ctx_st_done,
                             self.quant_done) if kc == 0 else ()),
                     echo=(kc == NCF - 1))
        ctx_w = (scd, scd.count)

        for og in range(4):
            wsl = wo_ring.slot
            for kc in range(NCF):
                self.dma("sp",
                         self.wslab[:, wsl * NCF * 512 + kc * 512:
                                    wsl * NCF * 512 + (kc + 1) * 512],
                         wsrc[kc * 128:(kc + 1) * 128, og * 512:(og + 1) * 512],
                         swd, waits=wo_ring.prod_waits() if kc == 0 else (),
                         echo=(kc == NCF - 1))
            wo_ring.filled(((swd, swd.count),))
            wbase = wsl * NCF * 512
            www = wo_ring.cons_waits()
            for ts in range(NT_OWN):
                xsl = xi_ring.slot
                self.dma("act", self.xt[:, xsl * HID:xsl * HID + 512],
                         x_src[ts * 128:(ts + 1) * 128, og * 512:(og + 1) * 512],
                         sxd, waits=xi_ring.prod_waits())
                xi_ring.filled(((sxd, sxd.count),))
                xw = xi_ring.cons_waits()
                bk = pool[pidx % 4]
                pidx += 1
                for kc in range(NCF):
                    ww = (bk.waits() + (ctx_w,) + www) if kc == 0 else ()
                    em.emit("pe", lambda e, bk=bk, kc=kc, ts=ts, wbase=wbase,
                            ctxsb=ctxsb: e.matmul(
                                bk.t[:, :],
                                ctxsb[:, kc * 2048 + ts * 128:kc * 2048 + (ts + 1) * 128],
                                self.wslab[:, wbase + kc * 512:wbase + (kc + 1) * 512],
                                start=(kc == 0), stop=(kc == NCF - 1)),
                            waits=ww, inc=(bk.sem if kc == NCF - 1 else None))
                mm_done = (bk.sem, bk.sem.count)
                nsl = xo_ring.slot
                em.emit("dve", lambda e, bk=bk, nsl=nsl, xsl=xsl: e.tensor_tensor(
                    self.xnew[:, nsl * 2048:nsl * 2048 + 512], bk.t[:, :],
                    self.xt[:, xsl * HID:xsl * HID + 512], ALU.add),
                    waits=bk.waits() + xw + xo_ring.prod_waits(), inc=bk.sem)
                add_done = em.last("dve")
                xi_ring.released((add_done,))
                xo_ring.filled((add_done,))
                out = bass.AP(xdst, (ts * 128) * HID + og * 512,
                              [[HID, 128], [1, 512]])
                self.dma("pool", out, self.xnew[:, nsl * 2048:nsl * 2048 + 512],
                         sst, waits=xo_ring.cons_waits())
                xo_ring.released(((sst, sst.count),))
            wo_ring.released((mm_done,))
        self.oproj_all_done = mm_done
        self.arena_free = mm_done
        return (sst, sst.count)

    def ffn_phase(self, x_src, x_dst_handle, x_dst_sem_name, x_guard, xnf_guard):
        em = self.em
        sxnf = self.sem("s_xnf_st")
        self.ln_sweep(x_src, self.xnf_d, sxnf, guard=x_guard + xnf_guard)
        ln_done = (sxnf, sxnf.count)
        sst = self.sem(x_dst_sem_name)
        sxd = self.sem("s_xnslab_dma")
        swd = self.sem("s_wslab_dma")
        sxid = self.sem("s_x_dma")
        xn_ring = self.ring("xnslab", 2)
        w_ring = self.ring("wslab", 2)
        rl_ring = self.ring("rlt", 2)
        r4_ring = self.ring("r4t", 2)
        y2_ring = self.ring("y2t", 1)
        xi_ring = self.ring("xt", 2)
        xo_ring = self.ring("xnew", 1)
        pool = [self.banks[i] for i in (0, 1, 2, 3)]
        bcb = self.banks[5]
        db = self.dbank
        tb = self.tbank
        pe_rel = self.sem("s_pe_rel")
        quant_done = self.quant_done

        for tt in range(4):
            xsl = xn_ring.slot
            srcap = bass.AP(self.xnf_d, tt * 512,
                            [[HALF, 128], [128 * HALF, NCF], [1, 512]])
            self.dma("sp",
                     self.xnslab[:, xsl * NCF * 512:(xsl + 1) * NCF * 512]
                     .rearrange("p (c e) -> p c e", e=512),
                     srcap, sxd, waits=xn_ring.prod_waits() + (ln_done,),
                     echo=True)
            xn_ring.filled(((sxd, sxd.count),))
            xnw = xn_ring.cons_waits()
            xbase = xsl * NCF * 512

            for og in range(16):
                wsl = w_ring.slot
                if getattr(self, "ffn_oldload", False):
                    for kc in range(NCF):
                        self.dma("sp",
                                 self.wslab[:, wsl * NCF * 512 + kc * 512:
                                            wsl * NCF * 512 + (kc + 1) * 512],
                                 self.qfc_d.ap()[og * 512:(og + 1) * 512,
                                                 kc * 128:(kc + 1) * 128],
                                 swd, transpose=True,
                                 waits=(w_ring.prod_waits() +
                                        ((quant_done,) if (tt == 0 and og == 0) else ()))
                                 if kc == 0 else (), echo=(kc == NCF - 1))
                else:
                    srcw = bass.AP(self.qfcT_d, og * 512,
                                   [[DFF, 128], [128 * DFF, NCF], [1, 512]])
                    self.dma("sp",
                             self.wslab[:, wsl * NCF * 512:(wsl + 1) * NCF * 512]
                             .rearrange("p (c e) -> p c e", e=512),
                             srcw, swd,
                             waits=w_ring.prod_waits() +
                             ((quant_done,) if (tt == 0 and og <= 1) else ()),
                             echo=True)
                w_ring.filled(((swd, swd.count),))
                www = w_ring.cons_waits()
                wbase = wsl * NCF * 512
                for oc in range(4):
                    oidx = og * 4 + oc
                    bk = pool[oidx % 4]
                    for kc in range(NCF):
                        ww = (bk.waits() + xnw + www) if kc == 0 else ()
                        em.emit("pe", lambda e, bk=bk, kc=kc, oc=oc, wbase=wbase,
                                xbase=xbase: e.matmul(
                                    bk.t[:, :],
                                    self.wslab[:, wbase + kc * 512 + oc * 128:
                                               wbase + kc * 512 + (oc + 1) * 128],
                                    self.xnslab[:, xbase + kc * 512:xbase + (kc + 1) * 512],
                                    start=(kc == 0), stop=(kc == NCF - 1)),
                                waits=ww, inc=(bk.sem if kc == NCF - 1 else None))
                    if oc == 3:
                        og_mm_done = (bk.sem, bk.sem.count)
                    rsl = rl_ring.slot
                    em.emit("dve", lambda e, bk=bk, rsl=rsl: e.tensor_scalar_max(
                        self.rlt[:, rsl * 512:(rsl + 1) * 512], bk.t[:, :], 0.0),
                        waits=bk.waits() + rl_ring.prod_waits(), inc=bk.sem)
                    rl_ring.filled((em.last("dve"),))
                    em.emit("act", lambda e, rsl=rsl, oidx=oidx: e.activation(
                        self.r2buf[:, oidx * 512:(oidx + 1) * 512],
                        self.rlt[:, rsl * 512:(rsl + 1) * 512], AF.Square),
                        waits=rl_ring.cons_waits() +
                        (((self.arena_free,) if self.arena_free else ())
                         if oidx == 0 else ()))
                    rl_ring.released((em.last("act"),))
                    f4 = r4_ring.slot
                    em.emit("act", lambda e, f4=f4, oidx=oidx: e.activation(
                        self.r4t[:, f4 * 512:(f4 + 1) * 512],
                        self.r2buf[:, oidx * 512:(oidx + 1) * 512], AF.Square),
                        waits=r4_ring.prod_waits())
                    r4_ring.filled((em.last("act"),))
                    em.emit("pe", lambda e, f4=f4, oidx=oidx: e.matmul(
                        db.t[0:1, :], self.ones_bf[:, :],
                        self.r4t[:, f4 * 512:(f4 + 1) * 512],
                        start=(oidx == 0), stop=(oidx == 63)),
                        waits=r4_ring.cons_waits() +
                        (self.db_free if oidx == 0 else ()),
                        inc=(db.sem if oidx == 63 else pe_rel))
                    if oidx == 63:
                        r4_ring.released(((db.sem, db.sem.count),))
                    else:
                        r4_ring.released(((pe_rel, pe_rel.count),))
                w_ring.released((og_mm_done,))
            xn_ring.released((og_mm_done,))

            em.emit("dve", lambda e: e.tensor_scalar(
                self.rstmp[:, :], db.t[0:1, :], 1.0 / DFF, FFN_EPS, ALU.mult, ALU.add),
                waits=db.waits())
            self.db_free = (em.last("dve"),)
            em.emit("act", lambda e: e.activation(self.rstmp[:, :], self.rstmp[:, :],
                                                  AF.Sqrt), waits=(em.last("dve"),))
            em.emit("dve", lambda e: e.reciprocal(self.rsrow[:, :], self.rstmp[:, :]),
                    waits=(em.last("act"),))
            em.emit("pe", lambda e, bcb=bcb: e.matmul(
                bcb.t[:, :], self.onesrow[:, :], self.rsrow[:, :],
                start=True, stop=True), waits=(em.last("dve"),) + bcb.waits(),
                inc=bcb.sem)
            em.emit("dve", lambda e, bcb=bcb: e.tensor_copy(self.bcsb[:, :], bcb.t[:, :]),
                    waits=bcb.waits(), inc=bcb.sem)
            rs_done = em.last("dve")

            for og2 in range(4):
                xsl2 = xi_ring.slot
                xsrc = bass.AP(x_src.tensor, x_src.offset + (tt * 512) * HID + og2 * 512,
                               [[HID, 128], [128 * HID, 4], [1, 512]])
                self.dma("sp",
                         self.xt[:, xsl2 * HID:(xsl2 + 1) * HID].rearrange(
                             "p (c e) -> p c e", e=512),
                         xsrc, sxid, waits=xi_ring.prod_waits())
                xi_ring.filled(((sxid, sxid.count),))
                xw2 = xi_ring.cons_waits()
                bks = [pool[i] for i in range(4)]
                for kg2 in range(4):
                    wsl = w_ring.slot
                    srcw = bass.AP(self.qprojT_d, (kg2 * 16 * 128) * HID + og2 * 512,
                                   [[HID, 128], [128 * HID, 16], [1, 512]])
                    self.dma("sp",
                             self.wslab[:, wsl * NCF * 512:(wsl + 1) * NCF * 512]
                             .rearrange("p (c e) -> p c e", e=512),
                             srcw, swd, waits=w_ring.prod_waits(),
                             echo=True)
                    w_ring.filled(((swd, swd.count),))
                    www2 = w_ring.cons_waits()
                    wbase = wsl * NCF * 512
                    for oci, oc2 in enumerate((0, 1, 2, 3)):
                        bk = bks[oc2]
                        for i in range(16):
                            kc = kg2 * 16 + i
                            ww = ()
                            if kc == 0:
                                ww = bk.waits()
                            if i == 0 and oci == 0:
                                ww = ww + www2
                            inc = (bk.sem if kc == 63 else
                                   (pe_rel if (oci == 3 and i == 15) else None))
                            em.emit("pe", lambda e, bk=bk, i=i, oc2=oc2, kc=kc,
                                    wbase=wbase: e.matmul(
                                        bk.t[:, :],
                                        self.wslab[:, wbase + i * 512 + oc2 * 128:
                                                   wbase + i * 512 + (oc2 + 1) * 128],
                                        self.r2buf[:, kc * 512:(kc + 1) * 512],
                                        start=(kc == 0), stop=(kc == 63)),
                                    waits=ww, inc=inc)
                        if oci == 3:
                            wrel = ((bk.sem, bk.sem.count) if kg2 == 3
                                    else (pe_rel, pe_rel.count))
                    w_ring.released((wrel,))
                nsl = xo_ring.slot
                add_dones = []
                for oc2 in range(4):
                    bk = bks[oc2]
                    em.emit("dve", lambda e, bk=bk: e.tensor_tensor(
                        self.y2t, bk.t[:, :], self.bcsb[:, :], ALU.mult),
                        waits=bk.waits() + (rs_done,) + y2_ring.prod_waits(),
                        inc=bk.sem)
                    y2_ring.filled((em.last("dve"),))
                    y2w = y2_ring.cons_waits()
                    for tsl in range(4):
                        em.emit("pe", lambda e, tsl=tsl: e.transpose(
                            tb.t[:, tsl * 128:(tsl + 1) * 128],
                            self.y2t[:, tsl * 128:(tsl + 1) * 128], self.identf[:, :]),
                            waits=(y2w + tb.waits()) if tsl == 0 else (),
                            inc=(tb.sem if tsl == 3 else None))
                    y2_ring.released(((tb.sem, tb.sem.count),))
                    wadd = xo_ring.prod_waits() if oc2 == 0 else ()
                    em.emit("dve", lambda e, nsl=nsl, oc2=oc2, xsl2=xsl2: e.tensor_tensor(
                        self.xnew[:, nsl * 2048:(nsl + 1) * 2048].rearrange(
                            "p (t e) -> p t e", e=512)[:, :, oc2 * 128:(oc2 + 1) * 128],
                        tb.t.ap().rearrange("p (t e) -> p t e", e=128),
                        self.xt[:, xsl2 * HID:(xsl2 + 1) * HID].rearrange(
                            "p (t e) -> p t e", e=512)[:, :, oc2 * 128:(oc2 + 1) * 128],
                        ALU.add), waits=tb.waits() + xw2 + wadd, inc=tb.sem)
                    add_dones.append(em.last("dve"))
                xi_ring.released((add_dones[-1],))
                xo_ring.filled((add_dones[-1],))
                out = bass.AP(x_dst_handle, (tt * 512) * HID + og2 * 512,
                              [[HID, 128], [128 * HID, 4], [1, 512]])
                self.dma("pool", out,
                         self.xnew[:, nsl * 2048:(nsl + 1) * 2048].rearrange(
                             "p (t e) -> p t e", e=512),
                         sst, waits=xo_ring.cons_waits())
                xo_ring.released(((sst, sst.count),))
                if og2 == 3:
                    self.arena_free = add_dones[-1]
        return (sst, sst.count)

    def finalize(self, final_waits):
        em = self.em
        if self.debug:
            dmp = self.sem("s_dump")
            dumps = [("d_xr0", self.xr[0], [HALF, HID], F32),
                     ("d_xr1", self.xr[1], [HALF, HID], F32),
                     ("d_xr2", self.xr[2], [HALF, HID], F32),
                     ("d_qT", self.qT_d, [NHL * 128, S], BF16),
                     ("d_kT", self.kT_d, [NHL * 128, S], BF16),
                     ("d_v", self.v_d, [S, NHL * 128], BF16),
                     ("d_ctxsel", self.ctx_sel_d, [HID, HALF], BF16),
                     ("d_qfc", self.qfc_d, [DFF, HID], BF16),
                     ("d_qproj", self.qproj_d, [HID, DFF], BF16),
                     ("d_xnf", self.xnf_d, [HID, HALF], BF16),
                     ("d_qfcT", self.qfcT_d, [HID, DFF], BF16),
                     ("d_qprojT", self.qprojT_d, [DFF, HID], BF16),
                     ("d_negba", self.negB["a"], None, F32),
                     ("d_negbb", self.negB["b"], None, F32)]
            for s, v in final_waits:
                em.emit("sp", None, waits=((s, v),))
            for name, srct, shape, dtyp in dumps:
                if shape is None:
                    o = self.nc.dram_tensor(name, [128, NHL], dtyp,
                                            kind="ExternalOutput").ap()
                    self.dma("sp", o, srct[:, :], dmp)
                    continue
                o = self.nc.dram_tensor(name, shape, dtyp, kind="ExternalOutput").ap()
                self.dma("sp", o, srct.ap()[:, :], dmp)
            em.emit("pool", None, waits=((dmp, dmp.count),))
        for s, v in final_waits:
            em.emit("pool", None, waits=((s, v),))
        with self.nc.Block() as block:
            block.tensor(lambda e: em.replay("pe", e))
            block.vector(lambda e: em.replay("dve", e))
            block.scalar(lambda e: em.replay("act", e))
            block.gpsimd(lambda e: em.replay("pool", e))
            block.sync(lambda e: em.replay("sp", e))
        self.ctx.close()
        return self.nc


def build(debug=False, upto=99):
    kb = KB(debug)
    kb.p0_consts()
    if upto == 0:
        return kb, kb.finalize([])
    kb._convert("a")

    sagst = kb.sem("s_agxn_st")
    kb.ln_sweep(kb.x_in, kb.ag_xn_in, sagst, guard=())
    if upto == 2:
        return kb, kb.finalize([(sagst, sagst.count)])
    ag_w = kb.collective("AllGather", [kb.ag_xn_in.ap().opt()],
                         [kb.ag_xn_out.ap().opt()],
                         waits=((sagst, sagst.count),))
    if upto == 3:
        return kb, kb.finalize([ag_w])
    kb.qkv_phase("a", ag_w)
    if upto == 4:
        return kb, kb.finalize([kb.qkv_done])
    kb.p1_quant()
    if upto == 1:
        return kb, kb.finalize([kb.quant_done])
    kb.attn_phase("a", None)
    if upto == 5:
        return kb, kb.finalize([kb.ctx_st_done])
    ag_c = kb.collective("AllGather", [kb.ctx_stack.ap().opt()],
                         [kb.ag_ctx_out.ap().opt()],
                         waits=((kb.sem("s_ctx_st"), kb.sem("s_ctx_st").count),))
    kb.sel_build(ag_c)
    if upto == 6:
        return kb, kb.finalize([kb.sel_done])
    x0_done = kb.oproj_phase("a", kb.x_in, kb.xr[0], "s_xr0_st")
    if upto == 7:
        return kb, kb.finalize([x0_done])

    x1_done = kb.ffn_phase(kb.xr[0].ap(), kb.xr[1], "s_xr1_st",
                           x_guard=(x0_done,), xnf_guard=())
    if upto == 8:
        return kb, kb.finalize([x1_done])

    sagst = kb.sem("s_agxn_st")
    kb.pre_wait("pool", (ag_w,))
    sxn = kb.sem("rs_xnslab")
    kb.ln_sweep(kb.xr[1].ap(), kb.ag_xn_in, sagst, guard=(x1_done,))
    ag_w2 = kb.collective("AllGather", [kb.ag_xn_in.ap().opt()],
                          [kb.ag_xn_out.ap().opt()],
                          waits=((sagst, sagst.count), (sxn, sxn.count)))
    kb.qkv_phase("b", ag_w2)
    kb.attn_phase("b", ag_c)
    scs = kb.sem("rs_cslab")
    ag_c2 = kb.collective("AllGather", [kb.ctx_stack.ap().opt()],
                          [kb.ag_ctx_out.ap().opt()],
                          waits=((kb.sem("s_ctx_st"), kb.sem("s_ctx_st").count),
                                 (scs, scs.count)))
    kb.sel_build(ag_c2)
    x2_done = kb.oproj_phase("b", kb.xr[1].ap(), kb.xr[2], "s_xr2_st")

    sxn2 = kb.sem("rs_xnslab")
    kb.pre_wait("pool", ((sxn2, sxn2.count),))
    y_done = kb.ffn_phase(kb.xr[2].ap(), kb.y_out.tensor, "s_y_st",
                          x_guard=(x2_done,), xnf_guard=())

    nc = kb.finalize([y_done])
    return kb, nc


def kernel(**inputs):
    consts = host_constants()
    kb, nc = build(debug=False)

    x = np.ascontiguousarray(np.asarray(inputs["L0_attn_a_hidden_states"],
                                        dtype=np.float32))
    in_maps = []
    for c in range(8):
        p, r = c // 2, c % 2
        m = {
            "x": np.ascontiguousarray(x[p, r * HALF:(r + 1) * HALF, :]),
            "wfc": np.ascontiguousarray(np.asarray(inputs["L0_ffn_W_fc"], np.float32)),
            "wproj": np.ascontiguousarray(np.asarray(inputs["L0_ffn_W_proj"], np.float32)),
            "qnw": np.ascontiguousarray(
                np.asarray(inputs["L0_attn_b_q_norm_w"], np.float32).reshape(1, 128)),
            "knw": np.ascontiguousarray(
                np.asarray(inputs["L0_attn_b_k_norm_w"], np.float32).reshape(1, 128)),
            "selA": (np.eye(128, dtype=np.float32) * (1.0 if r == 0 else 0.0)).astype(NPBF),
            "selB": (np.eye(128, dtype=np.float32) * (1.0 if r == 1 else 0.0)).astype(NPBF),
        }
        for lay, pref in (("a", "L0_attn_a"), ("b", "L0_attn_b")):
            for w, nm in (("q", "Wq"), ("k", "Wk"), ("v", "Wv")):
                W = np.asarray(inputs[f"{pref}_{nm}"], np.float32)
                m[f"w{w}{lay}T"] = np.ascontiguousarray(
                    W.T[:, r * NHL * 128:(r + 1) * NHL * 128])
            Wo = np.asarray(inputs[f"{pref}_Wo"], np.float32)
            m[f"wo{lay}T"] = np.ascontiguousarray(Wo.T)
        m.update(consts)
        in_maps.append(m)

    res = run_bass_kernel_spmd(nc, in_maps, core_ids=list(range(8)))
    out = np.empty((NSEQ, S, HID), np.float32)
    for c in range(8):
        p, r = c // 2, c % 2
        out[p, r * HALF:(r + 1) * HALF, :] = res.results[c]["y"]
    return out


# version 10
# speedup vs baseline: 1.0579x; 1.0029x over previous
import math
from contextlib import ExitStack

import numpy as np
import ml_dtypes
import concourse.bass as bass
import concourse.mybir as mybir
from concourse.bass_utils import run_bass_kernel_spmd

F32 = mybir.dt.float32
BF16 = mybir.dt.bfloat16
AF = mybir.ActivationFunctionType
ALU = mybir.AluOpType
AX = mybir.AxisListType
NPBF = ml_dtypes.bfloat16

NSEQ, S, HID = 4, 4096, 2048
NH, D = 16, 128
DFF = 8192
NCF = HID // 128
HALF = S // 2
NT_OWN = HALF // 128
NT_ALL = S // 128
NHL = NH // 2
SCALE_A = 1.0 / math.sqrt(128.0)
SCALE_B = 128.0 ** -0.5
RMS_EPS_B = 1e-6
FFN_EPS = float(np.finfo(np.float32).eps)
LN_EPS = 1e-5

ENGS = ("pe", "dve", "act", "pool", "sp")


class Sem:
    __slots__ = ("h", "name", "count")

    def __init__(self, handle, name):
        self.h = handle
        self.name = name
        self.count = 0


class Emitter:

    def __init__(self, chain_sems):
        self.streams = {e: [] for e in ENGS}
        self.chain_sems = chain_sems
        self._last = {e: None for e in ENGS}

    def emit(self, eng, fn=None, waits=(), inc=None, inc_amt=1):
        waits = tuple(waits)
        chain = fn is not None and eng in ("dve", "act") and inc_amt == 1
        if chain:
            if self._last[eng] is not None:
                waits = waits + (self._last[eng],)
            if inc is None:
                inc = self.chain_sems[eng]
        self.streams[eng].append((fn, waits, inc, inc_amt))
        if inc is not None:
            inc.count += inc_amt
            if chain:
                self._last[eng] = (inc, inc.count)

    def last(self, eng):
        return self._last[eng]

    def replay(self, eng_name, engobj):
        for fn, waits, inc, inc_amt in self.streams[eng_name]:
            for s, v in waits:
                engobj.wait_ge(s.h, v)
            ins = fn(engobj) if fn is not None else None
            if inc is not None:
                if ins is None:
                    ins = engobj.nop()
                ins.then_inc(inc.h, inc_amt)


class Ring:

    def __init__(self, n):
        self.n = n
        self.fw = []
        self.rw = []

    @property
    def slot(self):
        return len(self.fw) % self.n

    def prod_waits(self):
        i = len(self.fw)
        if i >= self.n:
            return tuple(self.rw[i - self.n])
        return ()

    def filled(self, waits):
        self.fw.append(tuple(waits))

    def cons_waits(self):
        return tuple(self.fw[len(self.rw)])

    def released(self, waits):
        self.rw.append(tuple(waits))


class Bank:
    def __init__(self, tensor, sem):
        self.t = tensor
        self.sem = sem

    def waits(self):
        return ((self.sem, self.sem.count),)


def attn_tiles(lay, g):
    out = []
    if lay == "a":
        for kt in range(4 * g):
            out.append((kt, 0, 512, None))
    else:
        for op_ in range(4):
            kt = 4 * g - 8 + op_
            if kt >= 0:
                out.append((kt, 0, (op_ + 1) * 128, ("w", op_ * 128)))
        for kt in range(max(0, 4 * g - 4), 4 * g):
            out.append((kt, 0, 512, None))
    for o in range(4):
        out.append((4 * g + o, o * 128, 512, ("c", o * 128)))
    return out


def host_constants():
    j = np.arange(0, D, 2, dtype=np.float32) / np.float32(D)
    inv = 1.0 / (10000.0 ** j.astype(np.float64))
    t32 = np.arange(S, dtype=np.float32)
    ang = (t32[:, None] * inv.astype(np.float32)[None, :]).astype(np.float32)
    cos = np.cos(ang.astype(np.float64)).astype(np.float32)
    sin = np.sin(ang.astype(np.float64)).astype(np.float32)
    ropec = np.ascontiguousarray(
        cos.reshape(NT_ALL, 128, 64).transpose(1, 0, 2).reshape(128, NT_ALL * 64))
    ropes = np.ascontiguousarray(
        sin.reshape(NT_ALL, 128, 64).transpose(1, 0, 2).reshape(128, NT_ALL * 64))
    kk = np.arange(128)[:, None]
    qq = np.arange(128)[None, :]
    maskc = np.where(qq < kk, np.float32(-1e30), np.float32(0.0))
    maskw = np.where(kk <= qq, np.float32(-1e30), np.float32(0.0))
    ident = np.eye(128, dtype=np.float32)
    return {
        "ropec": ropec, "ropes": ropes,
        "maskc": np.ascontiguousarray(maskc), "maskw": np.ascontiguousarray(maskw),
        "identb": ident.astype(NPBF), "identf": ident,
        "onesrow": np.ones((1, 128), np.float32),
    }


class KB:
    def __init__(self, debug=False):
        self.debug = debug
        self.nc = bass.Bass("TRN2", target_bir_lowering=False)
        self.ctx = ExitStack()
        self.sems_by_name = {}
        self.rings = {}
        self.em = Emitter({"dve": self.sem("chain_dve"), "act": self.sem("chain_act")})
        self.db_free = ()
        self.wconv_done = {}
        self.arena_free = None
        dt = self.nc.dram_tensor
        dbgkind = "Internal"

        self.x_in = dt("x", [HALF, HID], F32, kind="ExternalInput").ap()
        self.y_out = dt("y", [HALF, HID], F32, kind="ExternalOutput").ap()
        self.wT = {}
        for lay in ("a", "b"):
            for w in ("q", "k", "v"):
                self.wT[(lay, w)] = dt(f"w{w}{lay}T", [HID, NHL * 128], F32,
                                       kind="ExternalInput").ap()
            self.wT[(lay, "o")] = dt(f"wo{lay}T", [HID, HID], F32,
                                     kind="ExternalInput").ap()
        self.wfc = dt("wfc", [DFF, HID], F32, kind="ExternalInput").ap()
        self.wproj = dt("wproj", [HID, DFF], F32, kind="ExternalInput").ap()
        self.qnw_in = dt("qnw", [1, 128], F32, kind="ExternalInput").ap()
        self.knw_in = dt("knw", [1, 128], F32, kind="ExternalInput").ap()
        self.ropec_in = dt("ropec", [128, NT_ALL * 64], F32, kind="ExternalInput").ap()
        self.ropes_in = dt("ropes", [128, NT_ALL * 64], F32, kind="ExternalInput").ap()
        self.maskc_in = dt("maskc", [128, 128], F32, kind="ExternalInput").ap()
        self.maskw_in = dt("maskw", [128, 128], F32, kind="ExternalInput").ap()
        self.identb_in = dt("identb", [128, 128], BF16, kind="ExternalInput").ap()
        self.identf_in = dt("identf", [128, 128], F32, kind="ExternalInput").ap()
        self.onesrow_in = dt("onesrow", [1, 128], F32, kind="ExternalInput").ap()
        self.selA_in = dt("selA", [128, 128], BF16, kind="ExternalInput").ap()
        self.selB_in = dt("selB", [128, 128], BF16, kind="ExternalInput").ap()

        self.xr = [dt(f"xr{i}", [HALF, HID], F32, kind=dbgkind) for i in range(3)]
        self.ag_xn_in = dt("ag_xn_in", [HID, HALF], BF16)
        self.ag_xn_out = dt("ag_xn_out", [2 * HID, HALF], BF16)
        self.qT_d = dt("qT_d", [NHL * 128, S], BF16, kind=dbgkind)
        self.kT_d = dt("kT_d", [NHL * 128, S], BF16, kind=dbgkind)
        self.v_d = dt("v_d", [S, NHL * 128], BF16, kind=dbgkind)
        self.ctx_stack = dt("ctx_stack", [HID, HALF], BF16)
        self.ag_ctx_out = dt("ag_ctx_out", [2 * HID, HALF], BF16)
        self.ctx_sel_d = dt("ctx_sel_d", [HID, HALF], BF16, kind=dbgkind)
        self.qfc_d = dt("qfc_d", [DFF, HID], BF16, kind=dbgkind)
        self.qproj_d = dt("qproj_d", [HID, DFF], BF16, kind=dbgkind)
        self.xnf_d = dt("xnf_d", [HID, HALF], BF16)
        self.qfcT_d = dt("qfcT_d", [HID, DFF], BF16)
        self.qprojT_d = dt("qprojT_d", [DFF, HID], BF16)
        self.w_bf = {}
        for lay in ("a", "b"):
            for w in ("q", "k", "v"):
                self.w_bf[(lay, w)] = dt(f"wbf_{w}{lay}", [HID, NHL * 128], BF16)
            self.w_bf[(lay, "o")] = dt(f"wbf_o{lay}", [HID, HID], BF16)

        self._alloc_onchip()

    def sem(self, name):
        if name not in self.sems_by_name:
            h = self.ctx.enter_context(self.nc.semaphore(name))
            self.sems_by_name[name] = Sem(h, name)
        return self.sems_by_name[name]

    def ring(self, name, n):
        if name not in self.rings:
            self.rings[name] = Ring(n)
        r = self.rings[name]
        assert r.n == n
        return r

    def sbuf(self, name, shape, dtype):
        return self.ctx.enter_context(self.nc.sbuf_tensor(name, shape, dtype))

    def _alloc_onchip(self):
        sb = self.sbuf
        self.ropec = sb("ropec_s", [128, NT_ALL * 64], F32)
        self.ropes = sb("ropes_s", [128, NT_ALL * 64], F32)
        self.maskc = sb("maskc_s", [128, 128], F32)
        self.maskw = sb("maskw_s", [128, 128], F32)
        self.identb = sb("identb_s", [128, 128], BF16)
        self.identf = sb("identf_s", [128, 128], F32)
        self.onesrow = sb("onesrow_s", [1, 128], F32)
        self.ones_bf = sb("ones_bf", [128, 1], BF16)
        self.selA = sb("selA_s", [128, 128], BF16)
        self.selB = sb("selB_s", [128, 128], BF16)
        self.wrep_q = sb("wrep_q", [128, 128], F32)
        self.wrep_k = sb("wrep_k", [128, 128], F32)
        self.nrm_row = sb("nrm_row", [1, 128], F32)
        self.negB = {lay: sb(f"negB_{lay}", [128, NHL], F32) for lay in ("a", "b")}
        self.nq_acc = sb("nq_acc", [128, NHL], F32)
        self.nk_acc = sb("nk_acc", [128, NHL], F32)
        self.bst = sb("bst", [128, 8], F32)

        self.xt = sb("xt", [128, 2 * HID], F32)
        self.lnst = sb("lnst", [128, 8], F32)
        self.xn_tm = sb("xn_tm", [128, 2 * HID], BF16)
        self.tpo = sb("tpo", [128, 2 * 512], BF16)

        self.xnslab = sb("xnslab", [128, 2 * NCF * 512], BF16)
        self.wslab = sb("wslab", [128, 2 * NCF * 512], BF16)
        self.rtmp = sb("rtmp", [128, 1024], F32)
        self.qkv_tm = sb("qkv_tm", [128, 2 * 512], BF16)
        self.sqt = sb("sqt", [128, 2 * 512], BF16)
        self.r4t = self.sqt.ap()
        self.nrm4 = sb("nrm4", [128, 2 * 4], F32)
        self.rDs = sb("rDs", [1, 2 * 512], F32)
        self.bcsb = sb("bcsb", [128, 512], F32)
        self.rlt = sb("rlt", [128, 2 * 512], BF16)
        self.y2t = self.rlt.ap().bitcast(F32)
        self.xnew = sb("xnew", [128, 2048], F32)
        self.xsq = self.xnew.ap()[:, 0:HID]
        self.rsrow = sb("rsrow", [1, 512], F32)
        self.rstmp = sb("rstmp", [1, 512], F32)
        self.qsum = sb("qsum", [128, HID // 64], F32)
        self.qscl = sb("qscl", [128, HID // 64], BF16)
        self.qrec = sb("qrec", [128, HID // 64], F32)

        self.arena = sb("arena", [128, 32768], BF16)
        ar = self.arena.ap()
        arf = ar.bitcast(F32)
        self.qw_in = arf[:, 0:4096]
        self.qwb = ar[:, 8192:10240]
        self.qabs = ar[:, 10240:12288]
        self.qhat = ar[:, 12288:14336]
        self.qg1 = ar[:, 14336:16384]
        self.qout = ar[:, 16384:20480]
        self.qwf = arf[:, 10240:12288]
        self.ternf = arf[:, 12288:14336]
        self.qsclf = sb("qsclf", [128, HID // 64], F32)
        self.kTs = ar[:, 0:8192]
        self.vs = ar[:, 8192:16384]
        self.ptr = ar[:, 16384:18432]
        self.qTs = ar[:, 18432:19456]
        self.ctile = ar[:, 19456:20480]
        self.cslab = ar[:, 20480:28672]
        self.r2buf = ar

        self.banks = []
        for i in range(6):
            t = self.ctx.enter_context(self.nc.psum_tensor(f"pb{i}", [128, 512], F32))
            self.banks.append(Bank(t, self.sem(f"pbs{i}")))
        tt_ = self.ctx.enter_context(self.nc.psum_tensor("ptb", [128, 512], F32))
        self.tbank = Bank(tt_, self.sem("ptbs"))
        dd_ = self.ctx.enter_context(self.nc.psum_tensor("pdb", [1, 512], F32))
        self.dbank = Bank(dd_, self.sem("pdbs"))

    def dma(self, eng, out, in_, sem, waits=(), echo=False, echo_pair=None, **kw):
        def fn(e, out=out, in_=in_, kw=kw):
            return e.dma_start(out=out, in_=in_, **kw)
        self.em.emit(eng, fn, waits=waits, inc=sem, inc_amt=16)
        if echo_pair is not None:
            o2, i2 = echo_pair
            def fn2(e, o2=o2, i2=i2, kw=kw):
                return e.dma_start(out=o2, in_=i2, **kw)
            self.em.emit(eng, fn2, waits=(), inc=sem, inc_amt=16)
        for _ in range(int(echo)):
            self.em.emit(eng, fn, waits=(), inc=sem, inc_amt=16)

    def pre_wait(self, eng, waits):
        self.em.emit(eng, None, waits=waits)

    def barrier(self):
        s = self.sem("s_phase")
        base = s.count
        for e in ENGS:
            self.em.emit(e, None, waits=(), inc=s, inc_amt=1)
        for e in ENGS:
            self.em.emit(e, None, waits=((s, base + 5),))

    def p0_consts(self):
        em, s = self.em, self.sem("s_const")
        for dst, src in ((self.ropec, self.ropec_in), (self.ropes, self.ropes_in),
                         (self.maskc, self.maskc_in), (self.maskw, self.maskw_in),
                         (self.identb, self.identb_in), (self.identf, self.identf_in),
                         (self.onesrow, self.onesrow_in), (self.selA, self.selA_in),
                         (self.selB, self.selB_in)):
            self.dma("pool", dst[:, :], src, s)
        self.dma("pool", self.nrm_row[:, :], self.qnw_in, s)
        w = ((s, s.count),)
        sm = self.sem("s_misc")
        em.emit("dve", lambda e: e.memset(self.ones_bf[:, :], 1.0), waits=w, inc=sm)
        em.emit("dve", lambda e: e.tensor_scalar_mul(
            self.nrm_row[:, :], self.nrm_row[:, :], SCALE_B), inc=sm)
        bk = self.banks[0]
        em.emit("pe", lambda e, bk=bk: e.matmul(bk.t[:, 0:128], self.onesrow[:, :],
                                                self.nrm_row[:, :], start=True, stop=True),
                waits=((sm, sm.count),) + bk.waits(), inc=bk.sem)
        em.emit("dve", lambda e, bk=bk: e.tensor_copy(self.wrep_q[:, :], bk.t[:, 0:128]),
                waits=bk.waits(), inc=bk.sem)
        self.dma("pool", self.nrm_row[:, :], self.knw_in, s, waits=bk.waits())
        bk2 = self.banks[1]
        em.emit("pe", lambda e, bk2=bk2: e.matmul(bk2.t[:, 0:128], self.onesrow[:, :],
                                                  self.nrm_row[:, :], start=True, stop=True),
                waits=((s, s.count),) + bk2.waits(), inc=bk2.sem)
        em.emit("dve", lambda e, bk2=bk2: e.tensor_copy(self.wrep_k[:, :], bk2.t[:, 0:128]),
                waits=bk2.waits(), inc=bk2.sem)

    def _convert(self, lay):
        em = self.em
        sst = self.sem("s_qw_st")
        in_ring = self.ring("qw_in", 2)
        out_ring = self.ring("qout", 2)
        sq = self.sem("s_qwdma")
        for part in ("q", "k", "v", "o"):
            wsrc = self.wT[(lay, part)]
            wdst = self.w_bf[(lay, part)].ap()
            cols = wsrc.shape[1]
            for r in range(HID // 128):
                for cs in range(cols // HID if cols > HID else 1):
                    cw = min(cols, HID)
                    src_ap = wsrc[r * 128:(r + 1) * 128, cs * HID:cs * HID + cw]
                    dst_ap = wdst[r * 128:(r + 1) * 128, cs * HID:cs * HID + cw]
                    sl = in_ring.slot
                    self.dma("sp", self.qw_in[:, sl * HID:sl * HID + cw], src_ap,
                             sq, waits=in_ring.prod_waits())
                    in_ring.filled(((sq, sq.count),))
                    w = in_ring.cons_waits()
                    osl = out_ring.slot
                    oq = self.qout[:, osl * HID:osl * HID + cw]
                    em.emit("dve", lambda e, oq=oq, sl=sl, cw=cw: e.tensor_copy(
                        oq, self.qw_in[:, sl * HID:sl * HID + cw]),
                        waits=w + out_ring.prod_waits())
                    cp_done = em.last("dve")
                    in_ring.released((cp_done,))
                    out_ring.filled((cp_done,))
                    self.dma("pool", dst_ap, oq, sst, waits=out_ring.cons_waits())
                    out_ring.released(((sst, sst.count),))
        self.wconv_done[lay] = (sst, sst.count)

    def p1_quant(self):
        em = self.em
        sst = self.sem("s_qw_st")
        NG = HID // 64
        in_ring = self.ring("qw_in", 2)
        out_ring = self.ring("qout", 2)
        sq = self.sem("s_qwdma")

        def section(src_ap, dst_ap):
            sl = in_ring.slot
            self.dma("sp", self.qw_in[:, sl * HID:(sl + 1) * HID], src_ap, sq,
                     waits=in_ring.prod_waits())
            in_ring.filled(((sq, sq.count),))
            w = in_ring.cons_waits()
            inq = self.qw_in[:, sl * HID:(sl + 1) * HID]
            dve_prev = (em.last("dve"),) if em.last("dve") else ()
            em.emit("act", lambda e, inq=inq: e.activation(self.qwb, inq, AF.Copy),
                    waits=w)
            in_ring.released((em.last("act"),))
            em.emit("act", lambda e: e.activation(self.qabs, self.qwb, AF.Abs),
                    waits=dve_prev)
            abs_done = em.last("act")
            em.emit("act", lambda e: e.activation(self.qwf, self.qwb, AF.Copy),
                    waits=dve_prev)
            qwf_done = em.last("act")
            em.emit("dve", lambda e: e.reduce_sum(
                self.qsum.ap(), self.qabs.rearrange("p (g e) -> p g e", e=64),
                axis=AX.X), waits=(abs_done,))
            em.emit("dve", lambda e: e.tensor_scalar(
                self.qscl.ap(), self.qsum.ap(), 1.0 / 64.0, 1e-8, ALU.mult, ALU.max))
            em.emit("dve", lambda e: e.tensor_copy(self.qsclf.ap(), self.qscl.ap()))
            em.emit("dve", lambda e: e.reciprocal(self.qrec.ap(), self.qsclf.ap()))
            em.emit("dve", lambda e: e.tensor_tensor(
                self.qhat.rearrange("p (g e) -> p g e", e=64),
                self.qwf.rearrange("p (g e) -> p g e", e=64),
                self.qrec.ap().unsqueeze(2).broadcast_to([128, NG, 64]), ALU.mult),
                waits=(qwf_done,))
            em.emit("dve", lambda e: e.tensor_single_scalar(
                self.qg1, self.qhat, 0.5, ALU.is_gt))
            em.emit("dve", lambda e: e.tensor_single_scalar(
                self.qabs, self.qhat, -0.5, ALU.is_lt))
            em.emit("dve", lambda e: e.tensor_sub(self.ternf, self.qg1, self.qabs))
            osl = out_ring.slot
            oq = self.qout[:, osl * HID:(osl + 1) * HID]
            em.emit("dve", lambda e, oq=oq: e.tensor_tensor(
                oq.rearrange("p (g e) -> p g e", e=64),
                self.ternf.rearrange("p (g e) -> p g e", e=64),
                self.qsclf.ap().unsqueeze(2).broadcast_to([128, NG, 64]), ALU.mult),
                waits=out_ring.prod_waits())
            out_ring.filled((em.last("dve"),))
            self.dma("pool", dst_ap, oq, sst, waits=out_ring.cons_waits())
            out_ring.released(((sst, sst.count),))

        self._convert("b")

        qfc = self.qfc_d.ap()
        qpj = self.qproj_d.ap()
        for r in range(DFF // 128):
            section(self.wfc[r * 128:(r + 1) * 128, :],
                    qfc[r * 128:(r + 1) * 128, :])
        for r in range(HID // 128):
            for cs in range(DFF // HID):
                section(self.wproj[r * 128:(r + 1) * 128, cs * HID:(cs + 1) * HID],
                        qpj[r * 128:(r + 1) * 128, cs * HID:(cs + 1) * HID])
        w_ring = self.ring("wslab", 2)
        swd = self.sem("s_wslab_dma")
        sstT = self.sem("s_qwT_st")
        pe_relq = self.sem("s_pe_rel")

        def tblock(src_hbm, dst_hbm, R0, C0, dst_stride):
            wsl = w_ring.slot
            self.dma("sp", self.wslab[:, wsl * NCF * 512:wsl * NCF * 512 + 512],
                     src_hbm[R0:R0 + 512, C0:C0 + 128], swd, transpose=True,
                     waits=w_ring.prod_waits() + (((sst, sst.count),) if True else ()),
                     echo=True)
            w_ring.filled(((swd, swd.count),))
            out = bass.AP(dst_hbm, C0 * dst_stride + R0, [[dst_stride, 128], [1, 512]])
            self.dma("pool", out, self.wslab[:, wsl * NCF * 512:wsl * NCF * 512 + 512],
                     sstT, waits=w_ring.cons_waits())
            w_ring.released(((sstT, sstT.count),))

        qfcA = self.qfc_d
        qpjA = self.qproj_d
        for ob in range(DFF // 512):
            for kc in range(NCF):
                tblock(qfcA.ap(), self.qfcT_d, ob * 512, kc * 128, DFF)
        for ob in range(HID // 512):
            for kc in range(DFF // 128):
                tblock(qpjA.ap(), self.qprojT_d, ob * 512, kc * 128, HID)
        self.quant_done = (sstT, sstT.count)

    def ln_sweep(self, x_src, dst_handle, store_sem, guard=()):
        em = self.em
        stp = store_sem
        x_ring = self.ring("xt", 2)
        xn_ring = self.ring("xn_tm", 2)
        tpo_ring = self.ring("tpo", 2)
        sx = self.sem("s_lnx")
        tb = self.tbank
        tbv = tb.t.ap().bitcast(BF16)

        for ts in range(NT_OWN):
            sl = x_ring.slot
            self.dma("sp", self.xt[:, sl * HID:(sl + 1) * HID],
                     x_src[ts * 128:(ts + 1) * 128, :], sx,
                     waits=x_ring.prod_waits() + (guard if ts == 0 else ()))
            x_ring.filled(((sx, sx.count),))
            w = x_ring.cons_waits()
            xt = self.xt[:, sl * HID:(sl + 1) * HID]
            em.emit("act", lambda e, xt=xt: e.activation(self.xsq, xt, AF.Square),
                    waits=w + getattr(self, "_xsq_guard", ()))
            sq_done = em.last("act")
            em.emit("dve", lambda e, xt=xt: e.reduce_sum(self.lnst[:, 0:1], xt,
                                                         axis=AX.X), waits=w)
            em.emit("dve", lambda e: e.reduce_sum(self.lnst[:, 1:2], self.xsq,
                                                  axis=AX.X), waits=(sq_done,))
            self._xsq_guard = (em.last("dve"),)
            em.emit("dve", lambda e: e.tensor_scalar_mul(self.lnst[:, 2:3],
                                                         self.lnst[:, 0:1], 1.0 / HID))
            em.emit("dve", lambda e: e.tensor_scalar_mul(self.lnst[:, 3:4],
                                                         self.lnst[:, 1:2], 1.0 / HID))
            em.emit("dve", lambda e: e.tensor_mul(self.lnst[:, 4:5], self.lnst[:, 2:3],
                                                  self.lnst[:, 2:3]))
            em.emit("dve", lambda e: e.tensor_sub(self.lnst[:, 5:6], self.lnst[:, 3:4],
                                                  self.lnst[:, 4:5]))
            em.emit("dve", lambda e: e.tensor_scalar_add(self.lnst[:, 6:7],
                                                         self.lnst[:, 5:6], LN_EPS))
            em.emit("act", lambda e: e.activation(self.lnst[:, 7:8], self.lnst[:, 6:7],
                                                  AF.Sqrt), waits=(em.last("dve"),))
            em.emit("dve", lambda e: e.reciprocal(self.lnst[:, 6:7], self.lnst[:, 7:8]),
                    waits=(em.last("act"),))
            xsl = xn_ring.slot
            xn = self.xn_tm[:, xsl * HID:(xsl + 1) * HID]
            em.emit("dve", lambda e, xt=xt, xn=xn: e.tensor_scalar(
                xn, xt, self.lnst[:, 2:3], self.lnst[:, 6:7], ALU.subtract, ALU.mult),
                waits=xn_ring.prod_waits())
            xn_done = em.last("dve")
            x_ring.released((xn_done,))
            xn_ring.filled((xn_done,))
            xnw = xn_ring.cons_waits()
            pe_rel = self.sem("s_pe_rel")
            for grp in range(4):
                for i in range(4):
                    kc = grp * 4 + i
                    ww = (xnw + tb.waits()) if i == 0 else ()
                    em.emit("pe", lambda e, i=i, kc=kc, xn=xn, tbv=tbv: e.transpose(
                        tbv[:, i * 128:(i + 1) * 128], xn[:, kc * 128:(kc + 1) * 128],
                        self.identb[:, :]), waits=ww,
                        inc=tb.sem if i == 3 else None)
                tsl = tpo_ring.slot
                em.emit("dve", lambda e, tsl=tsl, tbv=tbv: e.tensor_copy(
                    self.tpo[:, tsl * 512:(tsl + 1) * 512], tbv[:, 0:512]),
                    waits=tb.waits() + tpo_ring.prod_waits(), inc=tb.sem)
                tpo_ring.filled((em.last("dve"),))
                if grp == 3:
                    xn_ring.released((em.last("dve"),))
                dst_ap = bass.AP(dst_handle, (grp * 4 * 128) * HALF + ts * 128,
                                 [[HALF, 128], [128 * HALF, 4], [1, 128]])
                src_ap = self.tpo[:, tsl * 512:(tsl + 1) * 512].rearrange(
                    "p (c e) -> p c e", e=128)
                self.dma("pool", dst_ap, src_ap, stp, waits=tpo_ring.cons_waits())
                tpo_ring.released(((stp, stp.count),))

    def collective(self, kind, ins, outs, waits):
        cc = self.sem("cc")

        def fn(e, ins=ins, outs=outs, kind=kind):
            return e.collective_compute(
                kind, mybir.AluOpType.bypass,
                replica_groups=[[0, 1], [2, 3], [4, 5], [6, 7]],
                ins=ins, outs=outs)
        self.em.emit("pool", fn, waits=waits, inc=cc, inc_amt=1)
        return (cc, cc.count)

    def qkv_phase(self, lay, ag_wait):
        em = self.em
        sst = self.sem("s_qkv_st")
        sxd = self.sem("s_xnslab_dma")
        swd = self.sem("s_wslab_dma")
        xn_ring = self.ring("xnslab", 2)
        w_ring = self.ring("wslab", 2)
        pool = [self.banks[i] for i in (0, 1, 2, 3)]
        pidx = 0
        pe_rel = self.sem("s_pe_rel")

        skvd = self.sem("s_kv_dma")
        sqd = self.sem("s_qT_dma")
        self.pre_wait("pool", ((skvd, skvd.count), (sqd, sqd.count)))

        self._sq_guard = ()
        tbs = self.tbank.sem
        em.emit("dve", lambda e: e.memset(self.nq_acc[:, :], 0.0),
                waits=((tbs, tbs.count),))
        em.emit("dve", lambda e: e.memset(self.nk_acc[:, :], 0.0))

        parts = [("q", 0), ("q", 1), ("k", 0), ("k", 1), ("v", 0), ("v", 1)]
        pend = []
        for tt in range(8):
            blk = tt // 4
            tloc = (tt % 4) * 512
            xsl = xn_ring.slot
            for kc in range(NCF):
                srcap = bass.AP(self.ag_xn_out, (blk * HID + kc * 128) * HALF + tloc,
                                [[HALF, 128], [1, 512]])
                self.dma("sp", self.xnslab[:, xsl * NCF * 512 + kc * 512:
                                           xsl * NCF * 512 + (kc + 1) * 512],
                         srcap, sxd,
                         waits=(xn_ring.prod_waits() + (ag_wait,)) if kc == 0 else (),
                         echo=(kc == NCF - 1))
            xn_ring.filled(((sxd, sxd.count),))
            xnw = xn_ring.cons_waits()
            xbase = xsl * NCF * 512

            for pi, (part, o5) in enumerate(parts):
                wsl = w_ring.slot
                wsrc = self.w_bf[(lay, part)].ap()
                weng = "sp"
                for kc in range(NCF):
                    self.dma(weng,
                             self.wslab[:, wsl * NCF * 512 + kc * 512:
                                        wsl * NCF * 512 + (kc + 1) * 512],
                             wsrc[kc * 128:(kc + 1) * 128, o5 * 512:(o5 + 1) * 512],
                             swd, waits=(w_ring.prod_waits() +
                                         ((self.wconv_done[lay],)
                                          if (tt == 0 and pi <= 1) else ()))
                             if kc == 0 else (), echo=(kc == NCF - 1))
                w_ring.filled(((swd, swd.count),))
                wbase = wsl * NCF * 512
                www = w_ring.cons_waits()

                for tsu in range(4):
                    bk = pool[pidx % 4]
                    pidx += 1
                    for kc in range(NCF):
                        ww = (bk.waits() + xnw + www) if kc == 0 else ()
                        em.emit("pe", lambda e, bk=bk, kc=kc, xbase=xbase,
                                wbase=wbase, tsu=tsu: e.matmul(
                                    bk.t[:, :],
                                    self.xnslab[:, xbase + kc * 512 + tsu * 128:
                                                xbase + kc * 512 + (tsu + 1) * 128],
                                    self.wslab[:, wbase + kc * 512:wbase + (kc + 1) * 512],
                                    start=(kc == 0), stop=(kc == NCF - 1)),
                                waits=ww, inc=bk.sem if kc == NCF - 1 else None)
                    mm_done = (bk.sem, bk.sem.count)
                    pend.append((part, o5, tt * 4 + tsu, bk))
                    if len(pend) >= 3:
                        p_, o_, g_, b_ = pend.pop(0)
                        self._qkv_evac(lay, p_, o_, g_, b_, sst)
                w_ring.released((mm_done,))
            xn_ring.released((mm_done,))
        while pend:
            p_, o_, g_, b_ = pend.pop(0)
            self._qkv_evac(lay, p_, o_, g_, b_, sst)

        self._negb_finalize(lay)
        self.qkv_done = (sst, sst.count)

    def _qkv_evac(self, lay, part, o5, gts, bk, sst):
        em = self.em
        qk_ring = self.ring("qkv_tm", 2)
        tpo_ring = self.ring("tpo", 2)
        pe_rel = self.sem("s_pe_rel")

        if part == "v":
            qsl = qk_ring.slot
            dst = self.qkv_tm[:, qsl * 512:(qsl + 1) * 512]
            em.emit("dve", lambda e, bk=bk, dst=dst: e.tensor_copy(dst, bk.t[:, :]),
                    waits=bk.waits() + qk_ring.prod_waits(), inc=bk.sem)
            qk_ring.filled((em.last("dve"),))
            out = bass.AP(self.v_d, (gts * 128) * (NHL * 128) + o5 * 512,
                          [[NHL * 128, 128], [1, 512]])
            self.dma("pool", out, dst, sst, waits=qk_ring.cons_waits())
            qk_ring.released(((sst, sst.count),))
            return

        t1 = self.rtmp[:, 0:256].rearrange("p (h e) -> p h e", e=64)
        t2 = self.rtmp[:, 256:512].rearrange("p (h e) -> p h e", e=64)
        t3 = self.rtmp[:, 512:1024]
        c_ap = self.ropec[:, gts * 64:(gts + 1) * 64].unsqueeze(1).broadcast_to([128, 4, 64])
        s_ap = self.ropes[:, gts * 64:(gts + 1) * 64].unsqueeze(1).broadcast_to([128, 4, 64])
        bv = bk.t.ap().rearrange("p (h d) -> p h d", d=128)
        bw = bk.waits()

        if lay == "b":
            em.emit("act", lambda e, bk=bk: e.activation(self.sqt[:, 0:512],
                                                         bk.t[:, :], AF.Square),
                    waits=bw + self._sq_guard)
            em.emit("dve", lambda e: e.reduce_sum(
                self.nrm4[:, 0:4],
                self.sqt[:, 0:512].rearrange("p (h d) -> p h d", d=128),
                axis=AX.X), waits=(em.last("act"),))
            self._sq_guard = (em.last("dve"),)
            em.emit("dve", lambda e: e.tensor_scalar(
                self.nrm4[:, 0:4], self.nrm4[:, 0:4], 1.0 / 128.0, RMS_EPS_B,
                ALU.mult, ALU.add))
            em.emit("act", lambda e: e.activation(self.nrm4[:, 0:4], self.nrm4[:, 0:4],
                                                  AF.Sqrt), waits=(em.last("dve"),))
            em.emit("dve", lambda e: e.reciprocal(self.nrm4[:, 0:4], self.nrm4[:, 0:4]),
                    waits=(em.last("act"),))
            em.emit("dve", lambda e, t3=t3: e.tensor_tensor(
                t3.rearrange("p (h d) -> p h d", d=128), bv,
                self.nrm4[:, 0:4].unsqueeze(2).broadcast_to([128, 4, 128]),
                ALU.mult), inc=bk.sem)
            wrep = self.wrep_q if part == "q" else self.wrep_k
            em.emit("dve", lambda e, t3=t3, wrep=wrep: e.tensor_tensor(
                t3.rearrange("p (h d) -> p h d", d=128),
                t3.rearrange("p (h d) -> p h d", d=128),
                wrep.ap().unsqueeze(1).broadcast_to([128, 4, 128]), ALU.mult))
            sv = t3.rearrange("p (h d) -> p h d", d=128)
            brel = None
        else:
            sv = bv
            brel = bk

        src1 = sv[:, :, 0:64]
        src2 = sv[:, :, 64:128]
        qsl = qk_ring.slot
        qtm = self.qkv_tm[:, qsl * 512:(qsl + 1) * 512]
        qv = qtm.rearrange("p (h d) -> p h d", d=128)

        em.emit("dve", lambda e, src1=src1, t1=t1, c_ap=c_ap: e.tensor_tensor(
            t1, src1, c_ap, ALU.mult),
            waits=qk_ring.prod_waits() + (bw if lay == "a" else ()))
        em.emit("dve", lambda e, src2=src2, t2=t2, s_ap=s_ap: e.tensor_tensor(
            t2, src2, s_ap, ALU.mult))
        if lay == "a" and part == "q":
            em.emit("dve", lambda e, t1=t1, t2=t2: e.tensor_sub(t1, t1, t2))
            em.emit("dve", lambda e, qv=qv, t1=t1: e.tensor_scalar_mul(
                qv[:, :, 0:64], t1, SCALE_A))
        else:
            em.emit("dve", lambda e, qv=qv, t1=t1, t2=t2: e.tensor_tensor(
                qv[:, :, 0:64], t1, t2, ALU.subtract))
        em.emit("dve", lambda e, src2=src2, t1=t1, c_ap=c_ap: e.tensor_tensor(
            t1, src2, c_ap, ALU.mult))
        em.emit("dve", lambda e, src1=src1, t2=t2, s_ap=s_ap: e.tensor_tensor(
            t2, src1, s_ap, ALU.mult),
            inc=(brel.sem if brel is not None else None))
        if lay == "a" and part == "q":
            em.emit("dve", lambda e, t1=t1, t2=t2: e.tensor_add(t1, t1, t2))
            em.emit("dve", lambda e, qv=qv, t1=t1: e.tensor_scalar_mul(
                qv[:, :, 64:128], t1, SCALE_A))
        else:
            em.emit("dve", lambda e, qv=qv, t1=t1, t2=t2: e.tensor_tensor(
                qv[:, :, 64:128], t1, t2, ALU.add))
        rope_done = em.last("dve")
        qk_ring.filled((rope_done,))
        qw = qk_ring.cons_waits()

        acc = self.nq_acc if part == "q" else self.nk_acc
        em.emit("act", lambda e, qtm=qtm: e.activation(self.sqt[:, 0:512], qtm,
                                                       AF.Square),
                waits=qw + self._sq_guard)
        sq_done = em.last("act")
        em.emit("dve", lambda e: e.reduce_sum(
            self.nrm4[:, 0:4],
            self.sqt[:, 0:512].rearrange("p (h d) -> p h d", d=128), axis=AX.X),
            waits=(sq_done,))
        self._sq_guard = (em.last("dve"),)
        em.emit("dve", lambda e, acc=acc, o5=o5: e.tensor_max(
            acc[:, o5 * 4:(o5 + 1) * 4], acc[:, o5 * 4:(o5 + 1) * 4],
            self.nrm4[:, 0:4]))

        tb = self.tbank
        tbv = tb.t.ap().bitcast(BF16)
        for i in range(4):
            ww = (qw + tb.waits()) if i == 0 else ()
            em.emit("pe", lambda e, i=i, qtm=qtm, tbv=tbv: e.transpose(
                tbv[:, i * 128:(i + 1) * 128], qtm[:, i * 128:(i + 1) * 128],
                self.identb[:, :]), waits=ww, inc=tb.sem if i == 3 else None)
        qk_ring.released(((tb.sem, tb.sem.count), sq_done))
        tsl = tpo_ring.slot
        em.emit("dve", lambda e, tsl=tsl, tbv=tbv: e.tensor_copy(
            self.tpo[:, tsl * 512:(tsl + 1) * 512], tbv[:, 0:512]),
            waits=tb.waits() + tpo_ring.prod_waits(), inc=tb.sem)
        tpo_ring.filled((em.last("dve"),))
        dstt = self.qT_d if part == "q" else self.kT_d
        out = bass.AP(dstt, (o5 * 4 * 128) * S + gts * 128,
                      [[S, 128], [128 * S, 4], [1, 128]])
        self.dma("pool", out,
                 self.tpo[:, tsl * 512:(tsl + 1) * 512].rearrange("p (c e) -> p c e", e=128),
                 sst, waits=tpo_ring.cons_waits())
        tpo_ring.released(((sst, sst.count),))

    def _negb_finalize(self, lay):
        em = self.em
        tb = self.tbank
        for idx, acc in enumerate((self.nq_acc, self.nk_acc)):
            em.emit("pe", lambda e, acc=acc: e.transpose(
                tb.t[0:NHL, 0:128], acc[:, :], self.identf[:, :]),
                waits=tb.waits() + ((em.last("dve")[0], em.last("dve")[1]),),
                inc=tb.sem)
            em.emit("dve", lambda e, idx=idx: e.reduce_max(
                self.bst[0:NHL, idx:idx + 1], tb.t[0:NHL, 0:128], axis=AX.X),
                waits=tb.waits(), inc=tb.sem)
        em.emit("dve", lambda e: e.tensor_mul(self.bst[0:NHL, 2:3],
                                              self.bst[0:NHL, 0:1],
                                              self.bst[0:NHL, 1:2]))
        em.emit("act", lambda e: e.activation(self.bst[0:NHL, 3:4],
                                              self.bst[0:NHL, 2:3], AF.Sqrt),
                waits=(em.last("dve"),))
        em.emit("dve", lambda e: e.tensor_scalar_mul(
            self.bst[0:NHL, 4:5], self.bst[0:NHL, 3:4], -1.0),
            waits=(em.last("act"),))
        em.emit("pe", lambda e: e.transpose(tb.t[0:1, 0:128], self.bst[:, 4:5],
                                            self.identf[:, :]),
                waits=(em.last("dve"),) + tb.waits(), inc=tb.sem)
        em.emit("dve", lambda e: e.tensor_copy(self.bst[0:1, 0:NHL], tb.t[0:1, 0:NHL]),
                waits=tb.waits(), inc=tb.sem)
        bb = self.banks[4]
        em.emit("pe", lambda e, bb=bb: e.matmul(
            bb.t[:, 0:NHL], self.onesrow[:, :], self.bst[0:1, 0:NHL],
            start=True, stop=True), waits=(em.last("dve"),) + bb.waits(), inc=bb.sem)
        nb = self.negB[lay]
        em.emit("dve", lambda e, bb=bb, nb=nb: e.tensor_copy(nb[:, :], bb.t[:, 0:NHL]),
                waits=bb.waits(), inc=bb.sem)
        self.negb_wait = em.last("dve")

    def attn_phase(self, lay, ctx_prev_ag_wait):
        em = self.em
        scst = self.sem("s_ctx_st")
        skvd = self.sem("s_kv_dma")
        sqd = self.sem("s_qT_dma")
        spt = self.sem("s_pt")
        kv_ring = self.ring("kTs", 2)
        v_ring = self.ring("vs", 2)
        q_ring = self.ring("qTs", 2)
        pt_ring = self.ring("ptr", 4)
        ct_ring = self.ring("ctile", 2)
        rd_ring = self.ring("rDs", 2)
        pool = [self.banks[i] for i in (0, 1, 2)]
        ctxbanks = [self.banks[4], self.banks[5]]
        bcb = self.banks[3]
        db = self.dbank
        qkv_done = self.qkv_done
        negb_w = self.negb_wait

        if ctx_prev_ag_wait is not None:
            self.pre_wait("pool", (ctx_prev_ag_wait,))

        for hl in range(NHL):
            ksl = kv_ring.slot
            extra = ((qkv_done, self.quant_done) +
                     ((self.arena_free,) if self.arena_free else ())
                     if hl == 0 else ())
            self.dma("sp", self.kTs[:, ksl * S:(ksl + 1) * S],
                     self.kT_d.ap()[hl * 128:(hl + 1) * 128, :], skvd,
                     waits=kv_ring.prod_waits() + extra, echo=True)
            kv_ring.filled(((skvd, skvd.count),))
            vsl = v_ring.slot
            vsrc = bass.AP(self.v_d, hl * 128,
                           [[NHL * 128, 128], [128 * NHL * 128, NT_ALL], [1, 128]])
            self.dma("sp", self.vs[:, vsl * S:(vsl + 1) * S].rearrange(
                "p (c e) -> p c e", e=128), vsrc, skvd,
                waits=v_ring.prod_waits(), echo=True)
            v_ring.filled(((skvd, skvd.count),))
            kvw = kv_ring.cons_waits() + v_ring.cons_waits()
            kbase = ksl * S
            vbase = vsl * S

            for g in range(8):
                qsl = q_ring.slot
                self.dma("sp", self.qTs[:, qsl * 512:(qsl + 1) * 512],
                         self.qT_d.ap()[hl * 128:(hl + 1) * 128, g * 512:(g + 1) * 512],
                         sqd, waits=q_ring.prod_waits(), echo=True)
                q_ring.filled(((sqd, sqd.count),))
                qw = q_ring.cons_waits()
                qb = qsl * 512
                tiles = attn_tiles(lay, g)
                n = len(tiles)
                cb = ctxbanks[g % 2]
                exp_t = [None] * n
                stage = []

                pv_last = [None]

                def emit_pv(j, n=n, cb=cb, tiles=tiles, vbase=vbase, exp_t=exp_t,
                            kvw=kvw):
                    kt, qlo, qhi, _m = tiles[j]
                    psl = j % 4
                    em.emit("pe", lambda e, kt=kt, qlo=qlo, qhi=qhi, psl=psl, cb=cb,
                            j=j, n=n, vbase=vbase: e.matmul(
                                cb.t[:, qlo:qhi],
                                self.vs[:, vbase + kt * 128:vbase + (kt + 1) * 128],
                                self.ptr[:, psl * 512 + qlo:psl * 512 + qhi],
                                start=(j == 0), stop=(j == n - 1)),
                            waits=(exp_t[j],) + (cb.waits() if j == 0 else ()),
                            inc=(cb.sem if j == n - 1 else None))
                    if j == n - 1:
                        pv_last[0] = (cb.sem, cb.sem.count)
                    em.emit("pe", lambda e, qlo=qlo, qhi=qhi, psl=psl, j=j, n=n:
                            e.matmul(db.t[0:1, qlo:qhi], self.ones_bf[:, :],
                                     self.ptr[:, psl * 512 + qlo:psl * 512 + qhi],
                                     start=(j == 0), stop=(j == n - 1)),
                            waits=(self.db_free if j == 0 else ()), inc=spt)
                    pt_ring.released(((spt, spt.count),))

                for j, (kt, qlo, qhi, m) in enumerate(tiles):
                    bk = pool[j % 3]
                    ww = bk.waits()
                    if j == 0:
                        ww = ww + kvw + qw + (negb_w,)
                    em.emit("pe", lambda e, bk=bk, kt=kt, qlo=qlo, qhi=qhi,
                            kbase=kbase, qb=qb: e.matmul(
                                bk.t[:, qlo:qhi],
                                self.kTs[:, kbase + kt * 128:kbase + (kt + 1) * 128],
                                self.qTs[:, qb + qlo:qb + qhi],
                                start=True, stop=True), waits=ww, inc=bk.sem)
                    if j == n - 1:
                        q_last = (bk.sem, bk.sem.count)
                    if m is not None:
                        mt, mc = m
                        msk = self.maskc if mt == "c" else self.maskw
                        em.emit("dve", lambda e, bk=bk, mc=mc, msk=msk: e.tensor_tensor(
                            bk.t[:, mc:mc + 128], bk.t[:, mc:mc + 128], msk[:, :],
                            ALU.add), waits=bk.waits(), inc=bk.sem)
                    psl = pt_ring.slot
                    em.emit("act", lambda e, bk=bk, qlo=qlo, qhi=qhi, psl=psl,
                            hl=hl, lay=lay: e.activation(
                                self.ptr[:, psl * 512 + qlo:psl * 512 + qhi],
                                bk.t[:, qlo:qhi], AF.Exp,
                                bias=self.negB[lay][:, hl:hl + 1]),
                            waits=bk.waits() + pt_ring.prod_waits(), inc=bk.sem)
                    exp_t[j] = em.last("act")
                    pt_ring.filled((exp_t[j],))
                    stage.append(j)
                    if len(stage) > 2:
                        emit_pv(stage.pop(0))
                while stage:
                    emit_pv(stage.pop(0))

                rsl = rd_ring.slot
                em.emit("dve", lambda e, rsl=rsl: e.reciprocal(
                    self.rDs[0:1, rsl * 512:(rsl + 1) * 512], db.t[0:1, :]),
                    waits=((spt, spt.count),) + rd_ring.prod_waits())
                recip_done = em.last("dve")
                self.db_free = (recip_done,)
                rd_ring.filled((recip_done,))
                em.emit("pe", lambda e, rsl=rsl, bcb=bcb: e.matmul(
                    bcb.t[:, :], self.onesrow[:, :],
                    self.rDs[0:1, rsl * 512:(rsl + 1) * 512], start=True, stop=True),
                    waits=rd_ring.cons_waits() + bcb.waits(), inc=bcb.sem)
                rd_ring.released(((bcb.sem, bcb.sem.count),))
                em.emit("dve", lambda e, bcb=bcb: e.tensor_copy(self.bcsb[:, :],
                                                                bcb.t[:, :]),
                        waits=bcb.waits(), inc=bcb.sem)
                bc_done = em.last("dve")
                csl = ct_ring.slot
                em.emit("dve", lambda e, csl=csl, cb=cb: e.tensor_tensor(
                    self.ctile[:, csl * 512:(csl + 1) * 512], cb.t[:, :],
                    self.bcsb[:, :], ALU.mult),
                    waits=cb.waits() + (bc_done,) + ct_ring.prod_waits(), inc=cb.sem)
                ct_ring.filled((em.last("dve"),))
                out = bass.AP(self.ctx_stack,
                              ((g // 4) * NHL * 128 + hl * 128) * HALF + (g % 4) * 512,
                              [[HALF, 128], [1, 512]])
                self.dma("pool", out, self.ctile[:, csl * 512:(csl + 1) * 512],
                         scst, waits=ct_ring.cons_waits())
                ct_ring.released(((scst, scst.count),))
                q_ring.released((q_last,))
            kv_ring.released((q_last,))
            v_ring.released((pv_last[0],))
        self.ctx_st_done = (scst, scst.count)

    def sel_build(self, ag_wait):
        em = self.em
        ssel = self.sem("s_sel_st")
        scd = self.sem("s_cs_dma")
        cs_ring = self.ring("cslab", 2)
        tpo_ring = self.ring("tpo", 2)
        pe_rel = self.sem("s_pe_rel")
        pool = [self.banks[i] for i in (0, 1)]
        CSW = 4096
        for ts in range(NT_OWN):
            sl = cs_ring.slot
            first = True
            for cand in range(2):
                for h8 in range(2):
                    srcap = bass.AP(self.ag_ctx_out,
                                    (h8 * HID + cand * NHL * 128) * HALF + ts * 128,
                                    [[HALF, 128], [128 * HALF, 8], [1, 128]])
                    self.dma("sp",
                             self.cslab[:, sl * CSW + cand * 2048 + h8 * 1024:
                                        sl * CSW + cand * 2048 + (h8 + 1) * 1024]
                             .rearrange("p (c e) -> p c e", e=128),
                             srcap, scd,
                             waits=(cs_ring.prod_waits() + (ag_wait,)) if first else (),
                             echo=(cand == 1 and h8 == 1))
                    first = False
            cs_ring.filled(((scd, scd.count),))
            cw = cs_ring.cons_waits()
            sbase = sl * CSW
            for kcg in range(4):
                bk = pool[kcg % 2]
                em.emit("pe", lambda e, bk=bk, kcg=kcg, sbase=sbase: e.matmul(
                    bk.t[:, :], self.selA[:, :],
                    self.cslab[:, sbase + kcg * 512:sbase + (kcg + 1) * 512],
                    start=True, stop=False),
                    waits=bk.waits() + (cw if kcg == 0 else ()))
                em.emit("pe", lambda e, bk=bk, kcg=kcg, sbase=sbase: e.matmul(
                    bk.t[:, :], self.selB[:, :],
                    self.cslab[:, sbase + 2048 + kcg * 512:sbase + 2048 + (kcg + 1) * 512],
                    start=False, stop=True), inc=bk.sem)
                if kcg == 3:
                    cs_ring.released(((bk.sem, bk.sem.count),))
                tsl = tpo_ring.slot
                em.emit("dve", lambda e, bk=bk, tsl=tsl: e.tensor_copy(
                    self.tpo[:, tsl * 512:(tsl + 1) * 512], bk.t[:, :]),
                    waits=bk.waits() + tpo_ring.prod_waits(), inc=bk.sem)
                tpo_ring.filled((em.last("dve"),))
                out = bass.AP(self.ctx_sel_d, (kcg * 4 * 128) * HALF + ts * 128,
                              [[HALF, 128], [128 * HALF, 4], [1, 128]])
                self.dma("pool", out,
                         self.tpo[:, tsl * 512:(tsl + 1) * 512].rearrange(
                             "p (c e) -> p c e", e=128),
                         ssel, waits=tpo_ring.cons_waits())
                tpo_ring.released(((ssel, ssel.count),))
        self.sel_done = (ssel, ssel.count)

    def oproj_phase(self, lay, x_src, x_dst_handle, x_dst_sem_name):
        em = self.em
        sst = self.sem(x_dst_sem_name)
        swd = self.sem("s_wslab_dma")
        scd = self.sem("s_cs_dma")
        sxd = self.sem("s_x_dma")
        wo_ring = self.ring("wslab", 2)
        xi_ring = self.ring("xt", 2)
        xo_ring = self.ring("xnew", 1)
        pool = [self.banks[i] for i in (0, 1, 2, 3)]
        pe_rel = self.sem("s_pe_rel")
        pidx = 0
        wsrc = self.w_bf[(lay, "o")].ap()
        xdst = x_dst_handle

        ctxsb = self.arena.ap()
        sel_done = self.sel_done
        for kc in range(NCF):
            self.dma("sp", ctxsb[:, kc * 2048:(kc + 1) * 2048],
                     self.ctx_sel_d.ap()[kc * 128:(kc + 1) * 128, :], scd,
                     waits=((sel_done, self.ctx_st_done,
                             self.quant_done) if kc == 0 else ()),
                     echo=(kc == NCF - 1))
        ctx_w = (scd, scd.count)

        for og in range(4):
            wsl = wo_ring.slot
            for kc in range(NCF):
                self.dma("sp",
                         self.wslab[:, wsl * NCF * 512 + kc * 512:
                                    wsl * NCF * 512 + (kc + 1) * 512],
                         wsrc[kc * 128:(kc + 1) * 128, og * 512:(og + 1) * 512],
                         swd, waits=wo_ring.prod_waits() if kc == 0 else (),
                         echo=(kc == NCF - 1))
            wo_ring.filled(((swd, swd.count),))
            wbase = wsl * NCF * 512
            www = wo_ring.cons_waits()
            for ts in range(NT_OWN):
                xsl = xi_ring.slot
                self.dma("act", self.xt[:, xsl * HID:xsl * HID + 512],
                         x_src[ts * 128:(ts + 1) * 128, og * 512:(og + 1) * 512],
                         sxd, waits=xi_ring.prod_waits())
                xi_ring.filled(((sxd, sxd.count),))
                xw = xi_ring.cons_waits()
                bk = pool[pidx % 4]
                pidx += 1
                for kc in range(NCF):
                    ww = (bk.waits() + (ctx_w,) + www) if kc == 0 else ()
                    em.emit("pe", lambda e, bk=bk, kc=kc, ts=ts, wbase=wbase,
                            ctxsb=ctxsb: e.matmul(
                                bk.t[:, :],
                                ctxsb[:, kc * 2048 + ts * 128:kc * 2048 + (ts + 1) * 128],
                                self.wslab[:, wbase + kc * 512:wbase + (kc + 1) * 512],
                                start=(kc == 0), stop=(kc == NCF - 1)),
                            waits=ww, inc=(bk.sem if kc == NCF - 1 else None))
                mm_done = (bk.sem, bk.sem.count)
                nsl = xo_ring.slot
                em.emit("dve", lambda e, bk=bk, nsl=nsl, xsl=xsl: e.tensor_tensor(
                    self.xnew[:, nsl * 2048:nsl * 2048 + 512], bk.t[:, :],
                    self.xt[:, xsl * HID:xsl * HID + 512], ALU.add),
                    waits=bk.waits() + xw + xo_ring.prod_waits(), inc=bk.sem)
                add_done = em.last("dve")
                xi_ring.released((add_done,))
                xo_ring.filled((add_done,))
                out = bass.AP(xdst, (ts * 128) * HID + og * 512,
                              [[HID, 128], [1, 512]])
                self.dma("pool", out, self.xnew[:, nsl * 2048:nsl * 2048 + 512],
                         sst, waits=xo_ring.cons_waits())
                xo_ring.released(((sst, sst.count),))
            wo_ring.released((mm_done,))
        self.oproj_all_done = mm_done
        self.arena_free = mm_done
        return (sst, sst.count)

    def ffn_phase(self, x_src, x_dst_handle, x_dst_sem_name, x_guard, xnf_guard):
        em = self.em
        sxnf = self.sem("s_xnf_st")
        self.ln_sweep(x_src, self.xnf_d, sxnf, guard=x_guard + xnf_guard)
        ln_done = (sxnf, sxnf.count)
        sst = self.sem(x_dst_sem_name)
        sxd = self.sem("s_xnslab_dma")
        swd = self.sem("s_wslab_dma")
        sxid = self.sem("s_x_dma")
        xn_ring = self.ring("xnslab", 2)
        w_ring = self.ring("wslab", 2)
        rl_ring = self.ring("rlt", 2)
        r4_ring = self.ring("r4t", 2)
        y2_ring = self.ring("y2t", 1)
        xi_ring = self.ring("xt", 2)
        xo_ring = self.ring("xnew", 1)
        pool = [self.banks[i] for i in (0, 1, 2, 3)]
        bcb = self.banks[5]
        db = self.dbank
        tb = self.tbank
        pe_rel = self.sem("s_pe_rel")
        quant_done = self.quant_done

        for tt in range(4):
            xsl = xn_ring.slot
            srcap = bass.AP(self.xnf_d, tt * 512,
                            [[HALF, 128], [128 * HALF, NCF], [1, 512]])
            self.dma("sp",
                     self.xnslab[:, xsl * NCF * 512:(xsl + 1) * NCF * 512]
                     .rearrange("p (c e) -> p c e", e=512),
                     srcap, sxd, waits=xn_ring.prod_waits() + (ln_done,),
                     echo=True)
            xn_ring.filled(((sxd, sxd.count),))
            xnw = xn_ring.cons_waits()
            xbase = xsl * NCF * 512

            for og in range(16):
                wsl = w_ring.slot
                if getattr(self, "ffn_oldload", False):
                    for kc in range(NCF):
                        self.dma("sp",
                                 self.wslab[:, wsl * NCF * 512 + kc * 512:
                                            wsl * NCF * 512 + (kc + 1) * 512],
                                 self.qfc_d.ap()[og * 512:(og + 1) * 512,
                                                 kc * 128:(kc + 1) * 128],
                                 swd, transpose=True,
                                 waits=(w_ring.prod_waits() +
                                        ((quant_done,) if (tt == 0 and og == 0) else ()))
                                 if kc == 0 else (), echo=(kc == NCF - 1))
                else:
                    srcw = bass.AP(self.qfcT_d, og * 512,
                                   [[DFF, 128], [128 * DFF, NCF], [1, 512]])
                    self.dma("sp",
                             self.wslab[:, wsl * NCF * 512:(wsl + 1) * NCF * 512]
                             .rearrange("p (c e) -> p c e", e=512),
                             srcw, swd,
                             waits=w_ring.prod_waits() +
                             ((quant_done,) if (tt == 0 and og <= 1) else ()),
                             echo=True)
                w_ring.filled(((swd, swd.count),))
                www = w_ring.cons_waits()
                wbase = wsl * NCF * 512
                for oc in range(4):
                    oidx = og * 4 + oc
                    bk = pool[oidx % 4]
                    for kc in range(NCF):
                        ww = (bk.waits() + xnw + www) if kc == 0 else ()
                        em.emit("pe", lambda e, bk=bk, kc=kc, oc=oc, wbase=wbase,
                                xbase=xbase: e.matmul(
                                    bk.t[:, :],
                                    self.wslab[:, wbase + kc * 512 + oc * 128:
                                               wbase + kc * 512 + (oc + 1) * 128],
                                    self.xnslab[:, xbase + kc * 512:xbase + (kc + 1) * 512],
                                    start=(kc == 0), stop=(kc == NCF - 1)),
                                waits=ww, inc=(bk.sem if kc == NCF - 1 else None))
                    if oc == 3:
                        og_mm_done = (bk.sem, bk.sem.count)
                    rsl = rl_ring.slot
                    em.emit("dve", lambda e, bk=bk, rsl=rsl: e.tensor_scalar_max(
                        self.rlt[:, rsl * 512:(rsl + 1) * 512], bk.t[:, :], 0.0),
                        waits=bk.waits() + rl_ring.prod_waits(), inc=bk.sem)
                    rl_ring.filled((em.last("dve"),))
                    em.emit("act", lambda e, rsl=rsl, oidx=oidx: e.activation(
                        self.r2buf[:, oidx * 512:(oidx + 1) * 512],
                        self.rlt[:, rsl * 512:(rsl + 1) * 512], AF.Square),
                        waits=rl_ring.cons_waits() +
                        (((self.arena_free,) if self.arena_free else ())
                         if oidx == 0 else ()))
                    rl_ring.released((em.last("act"),))
                    f4 = r4_ring.slot
                    em.emit("act", lambda e, f4=f4, oidx=oidx: e.activation(
                        self.r4t[:, f4 * 512:(f4 + 1) * 512],
                        self.r2buf[:, oidx * 512:(oidx + 1) * 512], AF.Square),
                        waits=r4_ring.prod_waits())
                    r4_ring.filled((em.last("act"),))
                    em.emit("pe", lambda e, f4=f4, oidx=oidx: e.matmul(
                        db.t[0:1, :], self.ones_bf[:, :],
                        self.r4t[:, f4 * 512:(f4 + 1) * 512],
                        start=(oidx == 0), stop=(oidx == 63)),
                        waits=r4_ring.cons_waits() +
                        (self.db_free if oidx == 0 else ()),
                        inc=(db.sem if oidx == 63 else pe_rel))
                    if oidx == 63:
                        r4_ring.released(((db.sem, db.sem.count),))
                    else:
                        r4_ring.released(((pe_rel, pe_rel.count),))
                w_ring.released((og_mm_done,))
            xn_ring.released((og_mm_done,))

            em.emit("dve", lambda e: e.tensor_scalar(
                self.rstmp[:, :], db.t[0:1, :], 1.0 / DFF, FFN_EPS, ALU.mult, ALU.add),
                waits=db.waits())
            self.db_free = (em.last("dve"),)
            em.emit("act", lambda e: e.activation(self.rstmp[:, :], self.rstmp[:, :],
                                                  AF.Sqrt), waits=(em.last("dve"),))
            em.emit("dve", lambda e: e.reciprocal(self.rsrow[:, :], self.rstmp[:, :]),
                    waits=(em.last("act"),))
            em.emit("pe", lambda e, bcb=bcb: e.matmul(
                bcb.t[:, :], self.onesrow[:, :], self.rsrow[:, :],
                start=True, stop=True), waits=(em.last("dve"),) + bcb.waits(),
                inc=bcb.sem)
            em.emit("dve", lambda e, bcb=bcb: e.tensor_copy(self.bcsb[:, :], bcb.t[:, :]),
                    waits=bcb.waits(), inc=bcb.sem)
            rs_done = em.last("dve")

            for og2 in range(4):
                xsl2 = xi_ring.slot
                xsrc = bass.AP(x_src.tensor, x_src.offset + (tt * 512) * HID + og2 * 512,
                               [[HID, 128], [128 * HID, 4], [1, 512]])
                self.dma("sp",
                         self.xt[:, xsl2 * HID:(xsl2 + 1) * HID].rearrange(
                             "p (c e) -> p c e", e=512),
                         xsrc, sxid, waits=xi_ring.prod_waits())
                xi_ring.filled(((sxid, sxid.count),))
                xw2 = xi_ring.cons_waits()
                bks = [pool[i] for i in range(4)]
                for kg2 in range(4):
                    wsl = w_ring.slot
                    srcw = bass.AP(self.qprojT_d, (kg2 * 16 * 128) * HID + og2 * 512,
                                   [[HID, 128], [128 * HID, 16], [1, 512]])
                    self.dma("sp",
                             self.wslab[:, wsl * NCF * 512:(wsl + 1) * NCF * 512]
                             .rearrange("p (c e) -> p c e", e=512),
                             srcw, swd, waits=w_ring.prod_waits(),
                             echo=True)
                    w_ring.filled(((swd, swd.count),))
                    www2 = w_ring.cons_waits()
                    wbase = wsl * NCF * 512
                    for oci, oc2 in enumerate((0, 1, 2, 3)):
                        bk = bks[oc2]
                        for i in range(16):
                            kc = kg2 * 16 + i
                            ww = ()
                            if kc == 0:
                                ww = bk.waits()
                            if i == 0 and oci == 0:
                                ww = ww + www2
                            inc = (bk.sem if kc == 63 else
                                   (pe_rel if (oci == 3 and i == 15) else None))
                            em.emit("pe", lambda e, bk=bk, i=i, oc2=oc2, kc=kc,
                                    wbase=wbase: e.matmul(
                                        bk.t[:, :],
                                        self.wslab[:, wbase + i * 512 + oc2 * 128:
                                                   wbase + i * 512 + (oc2 + 1) * 128],
                                        self.r2buf[:, kc * 512:(kc + 1) * 512],
                                        start=(kc == 0), stop=(kc == 63)),
                                    waits=ww, inc=inc)
                        if oci == 3:
                            wrel = ((bk.sem, bk.sem.count) if kg2 == 3
                                    else (pe_rel, pe_rel.count))
                    w_ring.released((wrel,))
                nsl = xo_ring.slot
                add_dones = []
                for oc2 in range(4):
                    bk = bks[oc2]
                    em.emit("dve", lambda e, bk=bk: e.tensor_tensor(
                        self.y2t, bk.t[:, :], self.bcsb[:, :], ALU.mult),
                        waits=bk.waits() + (rs_done,) + y2_ring.prod_waits(),
                        inc=bk.sem)
                    y2_ring.filled((em.last("dve"),))
                    y2w = y2_ring.cons_waits()
                    for tsl in range(4):
                        em.emit("pe", lambda e, tsl=tsl: e.transpose(
                            tb.t[:, tsl * 128:(tsl + 1) * 128],
                            self.y2t[:, tsl * 128:(tsl + 1) * 128], self.identf[:, :]),
                            waits=(y2w + tb.waits()) if tsl == 0 else (),
                            inc=(tb.sem if tsl == 3 else None))
                    y2_ring.released(((tb.sem, tb.sem.count),))
                    wadd = xo_ring.prod_waits() if oc2 == 0 else ()
                    em.emit("dve", lambda e, nsl=nsl, oc2=oc2, xsl2=xsl2: e.tensor_tensor(
                        self.xnew[:, nsl * 2048:(nsl + 1) * 2048].rearrange(
                            "p (t e) -> p t e", e=512)[:, :, oc2 * 128:(oc2 + 1) * 128],
                        tb.t.ap().rearrange("p (t e) -> p t e", e=128),
                        self.xt[:, xsl2 * HID:(xsl2 + 1) * HID].rearrange(
                            "p (t e) -> p t e", e=512)[:, :, oc2 * 128:(oc2 + 1) * 128],
                        ALU.add), waits=tb.waits() + xw2 + wadd, inc=tb.sem)
                    add_dones.append(em.last("dve"))
                xi_ring.released((add_dones[-1],))
                xo_ring.filled((add_dones[-1],))
                out = bass.AP(x_dst_handle, (tt * 512) * HID + og2 * 512,
                              [[HID, 128], [128 * HID, 4], [1, 512]])
                self.dma("pool", out,
                         self.xnew[:, nsl * 2048:(nsl + 1) * 2048].rearrange(
                             "p (t e) -> p t e", e=512),
                         sst, waits=xo_ring.cons_waits())
                xo_ring.released(((sst, sst.count),))
                if og2 == 3:
                    self.arena_free = add_dones[-1]
        return (sst, sst.count)

    def finalize(self, final_waits):
        em = self.em
        if self.debug:
            dmp = self.sem("s_dump")
            dumps = [("d_xr0", self.xr[0], [HALF, HID], F32),
                     ("d_xr1", self.xr[1], [HALF, HID], F32),
                     ("d_xr2", self.xr[2], [HALF, HID], F32),
                     ("d_qT", self.qT_d, [NHL * 128, S], BF16),
                     ("d_kT", self.kT_d, [NHL * 128, S], BF16),
                     ("d_v", self.v_d, [S, NHL * 128], BF16),
                     ("d_ctxsel", self.ctx_sel_d, [HID, HALF], BF16),
                     ("d_qfc", self.qfc_d, [DFF, HID], BF16),
                     ("d_qproj", self.qproj_d, [HID, DFF], BF16),
                     ("d_xnf", self.xnf_d, [HID, HALF], BF16),
                     ("d_qfcT", self.qfcT_d, [HID, DFF], BF16),
                     ("d_qprojT", self.qprojT_d, [DFF, HID], BF16),
                     ("d_negba", self.negB["a"], None, F32),
                     ("d_negbb", self.negB["b"], None, F32)]
            for s, v in final_waits:
                em.emit("sp", None, waits=((s, v),))
            for name, srct, shape, dtyp in dumps:
                if shape is None:
                    o = self.nc.dram_tensor(name, [128, NHL], dtyp,
                                            kind="ExternalOutput").ap()
                    self.dma("sp", o, srct[:, :], dmp)
                    continue
                o = self.nc.dram_tensor(name, shape, dtyp, kind="ExternalOutput").ap()
                self.dma("sp", o, srct.ap()[:, :], dmp)
            em.emit("pool", None, waits=((dmp, dmp.count),))
        for s, v in final_waits:
            em.emit("pool", None, waits=((s, v),))
        with self.nc.Block() as block:
            block.tensor(lambda e: em.replay("pe", e))
            block.vector(lambda e: em.replay("dve", e))
            block.scalar(lambda e: em.replay("act", e))
            block.gpsimd(lambda e: em.replay("pool", e))
            block.sync(lambda e: em.replay("sp", e))
        self.ctx.close()
        return self.nc


def build(debug=False, upto=99):
    kb = KB(debug)
    kb.p0_consts()
    if upto == 0:
        return kb, kb.finalize([])
    kb._convert("a")

    sagst = kb.sem("s_agxn_st")
    kb.ln_sweep(kb.x_in, kb.ag_xn_in, sagst, guard=())
    if upto == 2:
        return kb, kb.finalize([(sagst, sagst.count)])
    ag_w = kb.collective("AllGather", [kb.ag_xn_in.ap().opt()],
                         [kb.ag_xn_out.ap().opt()],
                         waits=((sagst, sagst.count),))
    if upto == 3:
        return kb, kb.finalize([ag_w])
    kb.qkv_phase("a", ag_w)
    if upto == 4:
        return kb, kb.finalize([kb.qkv_done])
    kb.p1_quant()
    if upto == 1:
        return kb, kb.finalize([kb.quant_done])
    kb.attn_phase("a", None)
    if upto == 5:
        return kb, kb.finalize([kb.ctx_st_done])
    ag_c = kb.collective("AllGather", [kb.ctx_stack.ap().opt()],
                         [kb.ag_ctx_out.ap().opt()],
                         waits=((kb.sem("s_ctx_st"), kb.sem("s_ctx_st").count),))
    kb.sel_build(ag_c)
    if upto == 6:
        return kb, kb.finalize([kb.sel_done])
    x0_done = kb.oproj_phase("a", kb.x_in, kb.xr[0], "s_xr0_st")
    if upto == 7:
        return kb, kb.finalize([x0_done])

    x1_done = kb.ffn_phase(kb.xr[0].ap(), kb.xr[1], "s_xr1_st",
                           x_guard=(x0_done,), xnf_guard=())
    if upto == 8:
        return kb, kb.finalize([x1_done])

    sagst = kb.sem("s_agxn_st")
    kb.pre_wait("pool", (ag_w,))
    sxn = kb.sem("rs_xnslab")
    kb.ln_sweep(kb.xr[1].ap(), kb.ag_xn_in, sagst, guard=(x1_done,))
    ag_w2 = kb.collective("AllGather", [kb.ag_xn_in.ap().opt()],
                          [kb.ag_xn_out.ap().opt()],
                          waits=((sagst, sagst.count), (sxn, sxn.count)))
    kb.qkv_phase("b", ag_w2)
    kb.attn_phase("b", ag_c)
    scs = kb.sem("rs_cslab")
    ag_c2 = kb.collective("AllGather", [kb.ctx_stack.ap().opt()],
                          [kb.ag_ctx_out.ap().opt()],
                          waits=((kb.sem("s_ctx_st"), kb.sem("s_ctx_st").count),
                                 (scs, scs.count)))
    kb.sel_build(ag_c2)
    x2_done = kb.oproj_phase("b", kb.xr[1].ap(), kb.xr[2], "s_xr2_st")

    sxn2 = kb.sem("rs_xnslab")
    kb.pre_wait("pool", ((sxn2, sxn2.count),))
    y_done = kb.ffn_phase(kb.xr[2].ap(), kb.y_out.tensor, "s_y_st",
                          x_guard=(x2_done,), xnf_guard=())

    nc = kb.finalize([y_done])
    return kb, nc


def kernel(**inputs):
    consts = host_constants()
    kb, nc = build(debug=False)

    x = np.ascontiguousarray(np.asarray(inputs["L0_attn_a_hidden_states"],
                                        dtype=np.float32))
    in_maps = []
    for c in range(8):
        p, r = c // 2, c % 2
        m = {
            "x": np.ascontiguousarray(x[p, r * HALF:(r + 1) * HALF, :]),
            "wfc": np.ascontiguousarray(np.asarray(inputs["L0_ffn_W_fc"], np.float32)),
            "wproj": np.ascontiguousarray(np.asarray(inputs["L0_ffn_W_proj"], np.float32)),
            "qnw": np.ascontiguousarray(
                np.asarray(inputs["L0_attn_b_q_norm_w"], np.float32).reshape(1, 128)),
            "knw": np.ascontiguousarray(
                np.asarray(inputs["L0_attn_b_k_norm_w"], np.float32).reshape(1, 128)),
            "selA": (np.eye(128, dtype=np.float32) * (1.0 if r == 0 else 0.0)).astype(NPBF),
            "selB": (np.eye(128, dtype=np.float32) * (1.0 if r == 1 else 0.0)).astype(NPBF),
        }
        for lay, pref in (("a", "L0_attn_a"), ("b", "L0_attn_b")):
            for w, nm in (("q", "Wq"), ("k", "Wk"), ("v", "Wv")):
                W = np.asarray(inputs[f"{pref}_{nm}"], np.float32)
                m[f"w{w}{lay}T"] = np.ascontiguousarray(
                    W.T[:, r * NHL * 128:(r + 1) * NHL * 128])
            Wo = np.asarray(inputs[f"{pref}_Wo"], np.float32)
            m[f"wo{lay}T"] = np.ascontiguousarray(Wo.T)
        m.update(consts)
        in_maps.append(m)

    res = run_bass_kernel_spmd(nc, in_maps, core_ids=list(range(8)))
    out = np.empty((NSEQ, S, HID), np.float32)
    for c in range(8):
        p, r = c // 2, c % 2
        out[p, r * HALF:(r + 1) * HALF, :] = res.results[c]["y"]
    return out
